# version 14
# baseline (speedup 1.0000x reference)
import numpy as np
import ml_dtypes
from contextlib import ExitStack
import concourse.bass as bass
import concourse.mybir as mybir
from concourse.bass_utils import run_bass_kernel_spmd

"""Core framework + slot emitters for the mk_seq_postln_d2 megakernel.

Layout conventions:
  - token-major (tm): [tokens(part) x features(free)]; residual x lives here, f32.
    x_tm sbuf tensor [128, NTC*D]: t-chunk tc at cols tc*D.
  - feature-major (fm): [features(part) x tokens(free)] bf16, matmul operand.
    fm tensor [128, n_fc*Tpad]: f-chunk fc at cols fc*Tpad (+pad lead zero cols).
  - matmul: out[M,N] = lhsT.T @ rhs, lhsT [K<=128, M<=128], rhs [K, N<=512].
    * fm emission: lhsT = W[fi,fo] natural, rhs = act_fm -> out [fo, t]
    * tm emission: lhsT = act_fm[fi, tchunk], rhs = W[fi, fo] -> out [t, fo]
  - biases live as [1, len] f32 rows; injected into PSUM with K=1 matmuls:
    * fm emission: lhsT = b_row[0:1, sl], rhs = ones_row[0:1, 0:N]
    * tm emission: lhsT = ones_row[0:1, 0:128], rhs = b_row[0:1, sl]

Sync model: every instruction increments its engine's progress semaphore
(+1 compute, +16 DMA); Ev(sem,count) events; consumers wait explicitly.
Same-engine RAW REQUIRES an event wait (hardware pipelines do not interlock).
"""

F32 = mybir.dt.float32
BF16 = mybir.dt.bfloat16
AF = mybir.ActivationFunctionType
ALU = mybir.AluOpType
AX = mybir.AxisListType

D = 1024
T = 2048
NTC = T // 128
NFC = D // 128

ENGINES = ("tensor", "vector", "scalar", "gpsimd", "sync")


class Ev:
    __slots__ = ("sem", "count")

    def __init__(self, sem, count):
        self.sem = sem
        self.count = count


class Prog:
    def __init__(self, nc, st: ExitStack):
        self.nc = nc
        self.ops = {e: [] for e in ENGINES}
        self.sems = {}
        self.counts = {}
        for e in ENGINES:
            s = st.enter_context(nc.semaphore(f"sp_{e}"))
            self.sems[e] = s
            self.counts[e] = 0
        self.dsems = {}
        self.dcounts = {}
        for e in ("sync", "scalar", "gpsimd", "vector"):
            self.dsems[e] = [st.enter_context(nc.semaphore(f"sd_{e}{j}")) for j in range(4)]
            self.dcounts[e] = [0, 0, 0, 0]
            setattr(self, "_rr_" + e, 0)

    @staticmethod
    def _flat(waits):
        out = []
        for w in waits:
            if w is None:
                continue
            if isinstance(w, (list, tuple)):
                out.extend(x for x in w if x is not None)
            else:
                out.append(w)
        return out

    def op(self, eng, fn, waits=()):
        for w in self._flat(waits):
            self.ops[eng].append(("w", w.sem, w.count))
        sem = self.sems[eng]
        self.counts[eng] += 1
        self.ops[eng].append(("op", fn, sem, 1))
        return Ev(sem, self.counts[eng])

    def dma(self, eng, out_ap, in_ap, waits=(), **kw):
        for w in self._flat(waits):
            self.ops[eng].append(("w", w.sem, w.count))
        j = getattr(self, "_rr_" + eng) % 4
        setattr(self, "_rr_" + eng, j + 1)
        sem = self.dsems[eng][j]
        if self.dcounts[eng][j] > 0:
            self.ops[eng].append(("w", sem, self.dcounts[eng][j]))
        self.dcounts[eng][j] += 16
        self.ops[eng].append(("op", lambda E: E.dma_start(out_ap, in_ap, **kw), sem, 16))
        return Ev(sem, self.dcounts[eng][j])

    def barrier_ev(self, eng):
        return Ev(self.sems[eng], self.counts[eng])

    def full_barrier(self):
        evs = [Ev(self.sems[e], self.counts[e]) for e in ENGINES if self.counts[e] > 0]
        for e in self.dsems:
            for j in range(4):
                if self.dcounts[e][j] > 0:
                    evs.append(Ev(self.dsems[e][j], self.dcounts[e][j]))
        return evs

    def replay(self):
        nc = self.nc
        with nc.Block() as block:
            for eng in ENGINES:
                seq = self.ops[eng]

                def make(seq):
                    def body(E):
                        waited = {}
                        for item in seq:
                            if item[0] == "w":
                                _, sem, count = item
                                key = id(sem)
                                if waited.get(key, -1) < count:
                                    E.wait_ge(sem, count)
                                    waited[key] = count
                            else:
                                _, fn, sem, inc = item
                                fn(E).then_inc(sem, inc)
                    return body

                getattr(block, eng)(make(seq))


class Ctx:

    def __init__(self, nc):
        self.nc = nc
        self.pb = []
        self.pb_rr = 0
        self.pbt = []
        self.pbt_rr = 0
        self.pb_ev = {}
        self.gates = {}
        self.phase_list = []
        self._cms = {}

    def alloc(self, name, shape, dtype):
        self._uid = getattr(self, "_uid", 0) + 1
        cm = self.nc.sbuf_tensor(f"{name}_{self._uid}", shape, dtype)
        h = cm.__enter__()
        self._cms[id(h)] = cm
        return h

    def free(self, *hs):
        for h in reversed(hs):
            self._cms.pop(id(h)).__exit__(None, None, None)

    def bank(self):
        b = self.pb[self.pb_rr % len(self.pb)]
        self.pb_rr += 1
        return b, self.pb_ev.get(id(b))

    def bankt(self):
        b = self.pbt[self.pbt_rr % len(self.pbt)]
        self.pbt_rr += 1
        return b, self.pb_ev.get(id(b))

    def gate(self, name):
        return self.gates.get(name, self.phase_list)

    def new_phase(self, P):
        self.gates = {}
        self.phase_list = P.full_barrier()

    def set_gate(self, name, ev):
        self.gates[name] = ev


def ceil_div(a, b):
    return (a + b - 1) // b


def _copy(E, out, in_):
    if hasattr(E, "tensor_copy"):
        return E.tensor_copy(out, in_)
    return E.copy(out, in_)



def emit_load_x(P, ctx, seq, waits=(), src=None):
    xin = src if src is not None else ctx.xin
    ap = xin[seq].rearrange("(tc p) d -> p tc d", p=128)
    dst = ctx.x_tm[:].rearrange("p (tc d) -> p tc d", tc=NTC)
    return P.dma("scalar", dst, ap, waits=list(waits))


def emit_store_x(P, ctx, seq, waits, dst=None):
    xout = dst if dst is not None else ctx.xout
    ap = xout[seq].rearrange("(tc p) d -> p tc d", p=128)
    src = ctx.x_tm[:].rearrange("p (tc d) -> p tc d", tc=NTC)
    return P.dma("scalar", ap, src, waits=list(waits))


def emit_tm_to_fm(P, ctx, x_ready, dst_fm, Tpad, pad=0, evac_eng="scalar"):
    x_tm = ctx.x_tm
    ident = ctx.ident_f32
    ev = None
    for fc in range(NFC):
        for tg in range(NTC // 4):
            bank, bfree = ctx.bank()
            mev = None
            for j, tc in enumerate(range(tg * 4, tg * 4 + 4)):
                def mk(fc=fc, tc=tc, j=j, bank=bank, x_tm=x_tm, ident=ident):
                    def f(E):
                        return E.transpose(
                            bank[:, j * 128:(j + 1) * 128],
                            x_tm[:, tc * D + fc * 128: tc * D + fc * 128 + 128],
                            ident[:])
                    return f
                w = [x_ready, bfree, ctx.ident_ready] if j == 0 else []
                mev = P.op("tensor", mk(), waits=w)

            def mkev(fc=fc, tg=tg, bank=bank, dst_fm=dst_fm):
                def f(E):
                    return _copy(E,
                        dst_fm[:, fc * Tpad + pad + tg * 512: fc * Tpad + pad + tg * 512 + 512],
                        bank[:, 0:512])
                return f
            ev = P.op(evac_eng, mkev(), waits=[mev])
            ctx.pb_ev[id(bank)] = ev
    return ev


def emit_ln_inplace(P, ctx, tc, dep_ev, eps=1e-5):
    x = ctx.x_tm
    sl = x[:, tc * D:(tc + 1) * D]
    st = ctx.stat[:, tc * 8:(tc + 1) * 8]
    s1, s2, m, w, u, r, ri, s2b = (st[:, i:i + 1] for i in range(8))
    dump = ctx.dump
    e1 = P.op("vector", lambda E, o=s1, i=sl: E.reduce_sum(o, i, AX.X), waits=[dep_ev])
    e2a = P.op("scalar", lambda E, o=dump, i=sl, a=s2: E.activation(o[:, 0:512], i[:, 0:512], AF.Square, accum_out=a),
               waits=[dep_ev, ctx.gate("dump")])
    e2b = P.op("scalar", lambda E, o=dump, i=sl, a=s2b: E.activation(o[:, 0:512], i[:, 512:1024], AF.Square, accum_out=a),
               waits=[dep_ev, e2a])
    ctx.set_gate("dump", e2b)
    e3 = P.op("gpsimd", lambda E, o=m, i=s1: E.tensor_scalar(o, i, 1.0 / D, None, ALU.mult), waits=[e1])
    e4a = P.op("gpsimd", lambda E, o=w, a=s2, b=s2b: E.tensor_add(o, a, b), waits=[e2a, e2b])
    e4 = P.op("gpsimd", lambda E, o=w: E.tensor_scalar(o, o, 1.0 / D, None, ALU.mult), waits=[e4a])
    e5 = P.op("gpsimd", lambda E, o=u, i=m: E.tensor_mul(o, i, i), waits=[e3])
    e6 = P.op("gpsimd", lambda E, o=w, a=u: E.tensor_sub(o, o, a), waits=[e4, e5])
    e7 = P.op("scalar", lambda E, o=r, i=w: E.activation(o, i, AF.Sqrt, bias=ctx.eps_col[:, 0:1]),
              waits=[e6, ctx.eps_ready])
    e8 = P.op("vector", lambda E, o=ri, i=r: E.reciprocal(o, i), waits=[e7])
    e9 = P.op("vector", lambda E, o=sl, a=m, b=ri: E.tensor_scalar(o, o, a, b, ALU.subtract, ALU.mult),
              waits=[e8, e3, dep_ev])
    return e9


def mm(P, ctx, bank, lhsT, rhs, n, start, stop, waits=()):
    def f(E, bank=bank, lhsT=lhsT, rhs=rhs, n=n, start=start, stop=stop):
        return E.matmul(bank[:, 0:n], lhsT, rhs, start=start, stop=stop)
    return P.op("tensor", f, waits=waits)



def emit_l1f(P, ctx, seq, x_ready):
    x_tm = ctx.x_tm
    x_fm = ctx.alloc("x_fm", [128, NFC * T], BF16)
    h_fm = ctx.alloc("h_fm", [128, 8 * T], BF16)
    dump = ctx.dump = ctx.alloc("dump", [128, 512], F32)
    w1_sb = ctx.alloc("w1_sb", [128, 8 * 1024], BF16)
    w2_sb = ctx.alloc("w2_sb", [128, 8 * 1024], BF16)
    b1_row = ctx.alloc("b1_row", [1, 4096], BF16)
    b2_row = ctx.alloc("b2_row", [1, 1024], BF16)

    fm_done = emit_tm_to_fm(P, ctx, x_ready, x_fm, T)
    eb1 = P.dma("gpsimd", b1_row[0:1, 0:4096], ctx.d_b1r[:, :], waits=[ctx.gate("rows")])
    eb2 = P.dma("gpsimd", b2_row[0:1, 0:D], ctx.d_b2r[:, :])

    add_evs = []
    for qu in range(4):
        ew1 = P.dma("gpsimd",
                    w1_sb[:].rearrange("p (ci n) -> p ci n", ci=8),
                    ctx.d_w1[:, qu * 1024:(qu + 1) * 1024].rearrange("(ci p) n -> p ci n", p=128),
                    waits=[ctx.gate("w1_free")])
        ew2 = P.dma("gpsimd",
                    w2_sb[:].rearrange("p (ci n) -> p ci n", ci=8),
                    ctx.d_w2[qu * 1024:(qu + 1) * 1024, :].rearrange("(ci p) n -> p ci n", p=128),
                    waits=[ctx.gate("w2_free")])

        h_last = None
        for fc in range(8):
            for tg in range(T // 512):
                bank, bfree = ctx.bank()
                for fi in range(8):
                    w = [ew1, fm_done, bfree, ctx.gate("h_fm")] if fi == 0 else []
                    mm(P, ctx, bank,
                       w1_sb[:, fi * 1024 + fc * 128: fi * 1024 + fc * 128 + 128],
                       x_fm[:, fi * T + tg * 512: fi * T + tg * 512 + 512],
                       512, fi == 0, False, waits=w)
                mev = mm(P, ctx, bank,
                         b1_row[0:1, qu * 1024 + fc * 128: qu * 1024 + fc * 128 + 128],
                         ctx.ones_rowb[0:1, 0:512], 512, False, True,
                         waits=[eb1, ctx.onesb_ready])

                def mkev(fc=fc, tg=tg, bank=bank, h_fm=h_fm):
                    def f(E):
                        return E.activation(
                            h_fm[:, fc * T + tg * 512: fc * T + tg * 512 + 512],
                            bank[:, 0:512], AF.Relu)
                    return f
                h_last = P.op("scalar", mkev(), waits=[mev])
                ctx.pb_ev[id(bank)] = h_last
        ctx.set_gate("w1_free", P.barrier_ev("tensor"))

        for tc in range(NTC):
            for no in range(2):
                bank, bfree = ctx.bank()
                for fi in range(8):
                    w = [ew2, h_last, bfree] if fi == 0 else []
                    mm(P, ctx, bank,
                       h_fm[:, fi * T + tc * 128: fi * T + tc * 128 + 128],
                       w2_sb[:, fi * 1024 + no * 512: fi * 1024 + no * 512 + 512],
                       512, fi == 0, (fi == 7) and qu > 0, waits=w)
                if qu == 0:
                    mev = mm(P, ctx, bank,
                             ctx.ones_rowb[0:1, 0:128],
                             b2_row[0:1, no * 512: no * 512 + 512],
                             512, False, True, waits=[eb2])
                else:
                    mev = P.barrier_ev("tensor")

                def mkadd(tc=tc, no=no, bank=bank, x_tm=x_tm):
                    def f(E):
                        sl = x_tm[:, tc * D + no * 512: tc * D + no * 512 + 512]
                        return E.tensor_tensor(sl, sl, bank[:, 0:512], ALU.add)
                    return f
                prev_add = add_evs[qu * 32 + tc * 2 + no - 32] if qu > 0 else None
                aev = P.op("vector", mkadd(), waits=[mev, prev_add])
                ctx.pb_ev[id(bank)] = aev
                add_evs.append(aev)
        ctx.set_gate("w2_free", P.barrier_ev("tensor"))
        ctx.set_gate("h_fm", P.barrier_ev("tensor"))

    out_evs = []
    for tc in range(NTC):
        dep = add_evs[3 * 32 + tc * 2 + 1]
        out_evs.append(emit_ln_inplace(P, ctx, tc, dep))
    ctx.free(x_fm, h_fm, dump, w1_sb, w2_sb, b1_row, b2_row)
    ctx.new_phase(P)
    return out_evs



def emit_l1a(P, ctx, seq, x_ready):
    x_tm = ctx.x_tm
    q_fm = ctx.alloc("q_fm", [128, NFC * T], BF16)
    k_fm = ctx.alloc("k_fm", [128, NFC * T], BF16)
    v_tm = ctx.alloc("v_tm", [128, NTC * D], BF16)
    x_fm = ctx.alloc("x_fm", [128, NFC * T], BF16)
    w_sb = ctx.alloc("w_sb", [128, 8 * 512], BF16)
    b_row = ctx.alloc("b_row", [1, 1024], BF16)

    fm_done = emit_tm_to_fm(P, ctx, x_ready, x_fm, T)

    projs = [("q", ctx.d_wq, q_fm, "fm", 0), ("k", ctx.d_wk, k_fm, "fm", 1),
             ("v", ctx.d_wv, v_tm, "tm", 2)]
    for name, dw, dst, kind, bi in projs:
        ebr = P.dma("gpsimd", b_row[0:1, 0:D], ctx.d_bqkv[:, bi * D:(bi + 1) * D],
                    waits=[ctx.gate("w1_free"), ctx.gate("rows")])
        for foh in range(2):
            ew = P.dma("gpsimd", w_sb[:].rearrange("p (ci n) -> p ci n", ci=8),
                       dw[:, foh * 512:(foh + 1) * 512].rearrange("(ci p) n -> p ci n", p=128),
                       waits=[ctx.gate("w1_free")])
            if kind == "fm":
                for fo in range(foh * 4, foh * 4 + 4):
                    for tg in range(T // 512):
                        bank, bfree = ctx.bank()
                        for fi in range(8):
                            w = [ew, fm_done, bfree] if fi == 0 else []
                            mm(P, ctx, bank,
                               w_sb[:, fi * 512 + (fo - foh * 4) * 128: fi * 512 + (fo - foh * 4) * 128 + 128],
                               x_fm[:, fi * T + tg * 512: fi * T + tg * 512 + 512],
                               512, fi == 0, False, waits=w)
                        mev = mm(P, ctx, bank,
                                 b_row[0:1, fo * 128: fo * 128 + 128],
                                 ctx.ones_rowb[0:1, 0:512], 512, False, True,
                                 waits=[ebr, ctx.onesb_ready])

                        def mkev(fo=fo, tg=tg, bank=bank, dst=dst, name=name):
                            def f(E):
                                sl = dst[:, fo * T + tg * 512: fo * T + tg * 512 + 512]
                                if name == "q":
                                    return E.tensor_scalar(sl, bank[:, 0:512], 1.0 / 32, None, ALU.mult)
                                return E.tensor_copy(sl, bank[:, 0:512])
                            return f
                        ev = P.op("vector", mkev(), waits=[mev])
                        ctx.pb_ev[id(bank)] = ev
            else:
                no = foh
                for tc in range(NTC):
                    bank, bfree = ctx.bank()
                    for fi in range(8):
                        w = [ew, fm_done, bfree] if fi == 0 else []
                        mm(P, ctx, bank,
                           x_fm[:, fi * T + tc * 128: fi * T + tc * 128 + 128],
                           w_sb[:, fi * 512: fi * 512 + 512],
                           512, fi == 0, False, waits=w)
                    mev = mm(P, ctx, bank,
                             ctx.ones_rowb[0:1, 0:128],
                             b_row[0:1, no * 512: no * 512 + 512],
                             512, False, True, waits=[ebr, ctx.onesb_ready])

                    def mkev(tc=tc, no=no, bank=bank, dst=dst):
                        def f(E):
                            return _copy(E, dst[:, tc * D + no * 512: tc * D + no * 512 + 512],
                                         bank[:, 0:512])
                        return f
                    ev = P.op("scalar", mkev(), waits=[mev])
                    ctx.pb_ev[id(bank)] = ev
            ctx.set_gate("w1_free", P.barrier_ev("tensor"))
    proj_done = P.barrier_ev("tensor")
    ctx.free(x_fm, w_sb, b_row)
    o_sb = ctx.alloc("o_sb", [128, D], F32)
    p_sb = ctx.alloc("p_sb", [128, 512], F32)
    pt_sb = ctx.alloc("pt_sb", [128, 512], BF16)
    dump = ctx.dump = ctx.alloc("dump", [128, 512], F32)

    fl = ctx.stat
    out_evs = []
    for qt in range(NTC):
        s = fl[:, qt * 8:(qt + 1) * 8]
        mneg = [s[:, 0:1], s[:, 1:2]]
        mpos_old = s[:, 2:3]
        alpha = s[:, 3:4]
        lsum = s[:, 4:5]
        rs = s[:, 5:6]
        rcp = s[:, 6:7]
        prev_small = None
        prev_tev = None
        o_evs = [None, None]
        e_al = None
        e_l = None
        prev_em = None
        for kb in range(T // 512):
            sbank, sfree = ctx.bank()
            for fi in range(8):
                w = [proj_done, sfree] if fi == 0 else []
                mm(P, ctx, sbank,
                   q_fm[:, fi * T + qt * 128: fi * T + qt * 128 + 128],
                   k_fm[:, fi * T + kb * 512: fi * T + kb * 512 + 512],
                   512, fi == 0, fi == 7, waits=w)
            smev = P.barrier_ev("tensor")
            e_t = P.op("vector", lambda E, s=sbank, o=rs: E.reduce_max(o, s[:, 0:512], AX.X, negate=True),
                       waits=[smev, prev_small])
            if kb == 0:
                e_m = P.op("vector", lambda E, o=mneg[0], i=rs: E.tensor_copy(o, i), waits=[e_t])
                cur = mneg[0]
            else:
                old = mneg[(kb - 1) % 2]
                cur = mneg[kb % 2]
                e_mp = P.op("gpsimd", lambda E, o=mpos_old, i=old: E.tensor_scalar(o, i, -1.0, None, ALU.mult),
                            waits=[prev_small, prev_em, e_al])
                e_m = P.op("vector", lambda E, o=cur, a=old, b=rs: E.tensor_tensor(o, a, b, ALU.min),
                           waits=[e_t])
                e_al = P.op("scalar", lambda E, o=alpha, i=cur, b=mpos_old: E.activation(o, i, AF.Exp, bias=b[:, 0:1]),
                            waits=[e_m, e_mp])
            e_p = P.op("scalar", lambda E, s=sbank, b=cur, p=p_sb, a=rs: E.activation(
                p[:, 0:512], s[:, 0:512], AF.Exp, bias=b[:, 0:1], accum_out=a),
                waits=[e_m, e_t, prev_tev])
            ctx.pb_ev[id(sbank)] = e_p
            if kb == 0:
                e_l = P.op("gpsimd", lambda E, o=lsum, i=rs: E.tensor_copy(o, i), waits=[e_p])
            else:
                e_l = P.op("vector", lambda E, o=lsum, a=alpha, b=rs: E.scalar_tensor_tensor(
                    o, o, a[:, 0:1], b, ALU.mult, ALU.add), waits=[e_p, e_al, e_l])
            tbank, tfree = ctx.bank()
            tev = None
            for j in range(4):
                w = [e_p, tfree, ctx.ident_ready] if j == 0 else []
                tev = P.op("tensor", lambda E, j=j, b=tbank, p=p_sb: E.transpose(
                    b[:, j * 128:(j + 1) * 128], p[:, j * 128:(j + 1) * 128],
                    ctx.ident_f32[:]), waits=w)
            e_pt = P.op("vector", lambda E, b=tbank, o=pt_sb: E.tensor_copy(o[:, 0:512], b[:, 0:512]),
                        waits=[tev])
            ctx.pb_ev[id(tbank)] = e_pt
            prev_tev = tev
            for no in range(2):
                pvb, pvfree = ctx.bank()
                for j in range(4):
                    w = [e_pt, pvfree] if j == 0 else []
                    mm(P, ctx, pvb,
                       pt_sb[:, j * 128: j * 128 + 128],
                       v_tm[:, (kb * 4 + j) * D + no * 512: (kb * 4 + j) * D + no * 512 + 512],
                       512, j == 0, j == 3, waits=w)
                pvev = P.barrier_ev("tensor")
                if kb == 0:
                    e_o = P.op("vector", lambda E, b=pvb, no=no, o=o_sb: E.tensor_copy(
                        o[:, no * 512:(no + 1) * 512], b[:, 0:512]), waits=[pvev])
                else:
                    e_o = P.op("vector", lambda E, b=pvb, no=no, o=o_sb, a=alpha: E.scalar_tensor_tensor(
                        o[:, no * 512:(no + 1) * 512], o[:, no * 512:(no + 1) * 512],
                        a[:, 0:1], b[:, 0:512], ALU.mult, ALU.add),
                        waits=[pvev, e_al, o_evs[no]])
                ctx.pb_ev[id(pvb)] = e_o
                o_evs[no] = e_o
            prev_small = P.barrier_ev("gpsimd")
            prev_em = e_m
        e_r = P.op("vector", lambda E, o=rcp, i=lsum: E.reciprocal(o, i), waits=[e_l, o_evs[0], o_evs[1]])
        e_x = P.op("vector", lambda E, qt=qt, o=o_sb, r=rcp, x=x_tm: E.scalar_tensor_tensor(
            x[:, qt * D:(qt + 1) * D], o[:, 0:D], r[:, 0:1],
            x[:, qt * D:(qt + 1) * D], ALU.mult, ALU.add), waits=[e_r])
        out_evs.append(emit_ln_inplace(P, ctx, qt, e_x))
    ctx.free(q_fm, k_fm, v_tm, o_sb, p_sb, pt_sb, dump)
    ctx.new_phase(P)
    return out_evs


def setup_common(nc, st, ctx, P, need_xtm=True):
    ctx.x_tm = ctx.alloc("x_tm", [128, NTC * D], F32) if need_xtm else None
    ctx.ident_f32 = ctx.alloc("ident_sb", [128, 128], F32)
    ctx.stat = ctx.alloc("stat", [128, 128], F32)
    ctx.dump = None
    ctx.eps_col = ctx.alloc("eps_col", [128, 1], F32)
    for i in range(6):
        ctx.pb.append(st.enter_context(nc.psum_tensor("pb%d" % i, [128, 512], F32)))
    for i in range(2):
        ctx.pbt.append(st.enter_context(nc.psum_tensor("pbt%d" % i, [128, 1024], BF16)))
    ctx.ident_bf16 = ctx.alloc("ident_bf16", [128, 128], BF16)
    ctx.ones_col = ctx.alloc("ones_col", [128, 1], BF16)
    ctx.ident_ready = P.dma("sync", ctx.ident_f32[:], ctx.d_ident[:, :])
    if hasattr(ctx, "d_maskT"):
        ctx.maskT = ctx.alloc("maskTc", [128, 128], BF16)
        ctx.maskT_ready = P.dma("sync", ctx.maskT[:], ctx.d_maskT[:, :])
    ctx.identb_ready = P.op("vector", lambda E: E.tensor_copy(ctx.ident_bf16[:], ctx.ident_f32[:]),
                            waits=[ctx.ident_ready])
    ctx.ones_rowb = ctx.alloc("ones_rowb", [1, 512], BF16)
    ctx.onesb_ready = P.op("gpsimd", lambda E: E.memset(ctx.ones_rowb[:], 1.0))
    ctx.onesc_ready = P.op("gpsimd", lambda E: E.memset(ctx.ones_col[:], 1.0))
    ctx.eps_ready = P.op("gpsimd", lambda E: E.memset(ctx.eps_col[:], 1e-5))



def emit_l0a(P, ctx, seq, x_ready):
    x_tm = ctx.x_tm
    x_fm = ctx.alloc("x_fm", [128, NFC * 512], BF16)
    w_sb = ctx.alloc("w_sb", [128, 4 * 8 * 1024], BF16)
    b_row = ctx.alloc("b_row", [1, 4 * 1024], BF16)
    maskT = ctx.maskT
    qf = ctx.alloc("qf", [128, 8 * 512], BF16)
    kf = ctx.alloc("kf", [128, 8 * 512], BF16)
    vg = ctx.alloc("vg", [128, 4 * 1040], BF16)
    ktg = ctx.alloc("ktg", [128, 4 * 1024], BF16)
    szb = ctx.alloc("szb", [128, 8 * 65], BF16)
    szf = ctx.alloc("szf", [128, 8 * 65], F32)
    otm = ctx.alloc("otm", [128, 1024], BF16)
    aofc = ctx.alloc("aofc", [128, 8 * 128], BF16)
    smt = ctx.alloc("smt", [128, 8 * 128], BF16)
    es = ctx.alloc("es", [128, 512], F32)
    la = ctx.alloc("la", [128, 16], F32)
    dump = ctx.dump = ctx.alloc("dump", [128, 512], F32)

    ebr = P.dma("gpsimd", b_row[0:1, :], ctx.d_l0a_b[:, :], waits=[ctx.gate("rows")])
    emk = ctx.maskT_ready
    ew = [P.dma("gpsimd", w_sb[:, i * 8192:(i + 1) * 8192].rearrange("p (ci n) -> p ci n", ci=8),
                ctx.d_l0a_w[i].rearrange("(ci p) n -> p ci n", p=128),
                waits=[ctx.gate("l0aw%d" % i)]) for i in range(4)]
    ez = P.op("gpsimd", lambda E: E.memset(szb[:], 0.0), waits=[ctx.gate("szb")])
    ezf = P.op("gpsimd", lambda E: E.memset(szf[:], 0.0), waits=[ctx.gate("szb")])
    evo = P.op("gpsimd", lambda E: E.memset(
        vg[:].rearrange("p (s c) -> p s c", c=65)[:, :, 64:65], 1.0), waits=[ctx.gate("vg")])

    def szsb(h):
        return szb[(h % 2) * 64:(h % 2) * 64 + 64,
                   (h // 8) * 260 + ((h // 2) % 4) * 65: (h // 8) * 260 + ((h // 2) % 4) * 65 + 65]

    out_evs = []
    sz_evac = None
    sz_mm_done = None
    for tg in range(NTC // 4):
        fm_done = None
        for fc in range(NFC):
            bank, bfree = ctx.bank()
            mev = None
            for j, tc in enumerate(range(tg * 4, tg * 4 + 4)):
                def mkt(fc=fc, tc=tc, j=j, bank=bank):
                    def f(E):
                        return E.transpose(
                            bank[:, j * 128:(j + 1) * 128],
                            x_tm[:, tc * D + fc * 128: tc * D + fc * 128 + 128],
                            ctx.ident_f32[:])
                    return f
                w = [x_ready, bfree, ctx.ident_ready, ctx.gate("xfmg")] if j == 0 else []
                mev = P.op("tensor", mkt(), waits=w)

            def mkevb(fc=fc, bank=bank):
                def f(E):
                    return _copy(E, x_fm[:, fc * 512:(fc + 1) * 512], bank[:, 0:512])
                return f
            fm_done = P.op("scalar", mkevb(), waits=[mev])
            ctx.pb_ev[id(bank)] = fm_done
        pevs = {}
        for pi, dst in ((0, qf), (1, kf)):
            for fc in range(8):
                bank, bfree = ctx.bank()
                for fi in range(8):
                    w = [ew[pi], fm_done, bfree, ctx.gate("qkf")] if fi == 0 else []
                    mm(P, ctx, bank,
                       w_sb[:, pi * 8192 + fi * 1024 + fc * 128: pi * 8192 + fi * 1024 + fc * 128 + 128],
                       x_fm[:, fi * 512:(fi + 1) * 512],
                       512, fi == 0, False, waits=w)
                mev = mm(P, ctx, bank, b_row[0:1, pi * 1024 + fc * 128: pi * 1024 + fc * 128 + 128],
                         ctx.ones_rowb[0:1, 0:512], 512, False, True, waits=[ebr, ctx.onesb_ready])
                e1 = P.op("scalar", lambda E, b=bank, o=es: E.activation(o[:, 0:512], b[:, 0:512], AF.Exp),
                          waits=[mev, ctx.gate("es")])
                e2 = P.op("vector", lambda E, b=bank, o=dst, fc=fc: E.tensor_scalar(
                    o[:, fc * 512:(fc + 1) * 512], b[:, 0:512], 0.0, None, ALU.max),
                    waits=[mev, ctx.gate("qkf"), e1])
                e3 = P.op("vector", lambda E, o=dst, fc=fc, a=es: E.scalar_tensor_tensor(
                    o[:, fc * 512:(fc + 1) * 512], a[:, 0:512], 1.0,
                    o[:, fc * 512:(fc + 1) * 512], ALU.min, ALU.add),
                    waits=[e1, e2])
                ctx.set_gate("es", e3)
                ctx.pb_ev[id(bank)] = e3
                pevs[(pi, fc)] = e3
        for tci in range(4):
            for no in range(2):
                bank, bfree = ctx.bank()
                for fi in range(8):
                    w = [ew[2], fm_done, bfree, ctx.gate("vg")] if fi == 0 else []
                    mm(P, ctx, bank,
                       x_fm[:, fi * 512 + tci * 128: fi * 512 + tci * 128 + 128],
                       w_sb[:, 2 * 8192 + fi * 1024 + no * 512: 2 * 8192 + fi * 1024 + no * 512 + 512],
                       512, fi == 0, False, waits=w)
                mev = mm(P, ctx, bank, ctx.ones_rowb[0:1, 0:128],
                         b_row[0:1, 2 * 1024 + no * 512: 2 * 1024 + no * 512 + 512],
                         512, False, True, waits=[ebr, ctx.onesb_ready])
                def mkvev(b=bank, tci=tci, no=no):
                    def f(E):
                        dst = vg[:, tci * 1040 + no * 520: tci * 1040 + no * 520 + 520]
                        dst3 = dst.rearrange("p (h c) -> p h c", c=65)[:, :, 0:64]
                        return E.tensor_copy(dst3, b[:, 0:512].rearrange("p (h c) -> p h c", c=64))
                    return f
                ev = P.op("vector", mkvev(), waits=[mev, evo])
                ctx.pb_ev[id(bank)] = ev
                pevs[("v", tci, no)] = ev
        kt_evs = {}
        for tci in range(4):
            bank, bfree = ctx.bankt()
            tev = None
            for fc in range(8):
                w = [pevs[(1, fc)], bfree, ctx.gate("ktg")] if fc == 0 else [pevs[(1, fc)]]
                tev = P.op("tensor", lambda E, b=bank, fc=fc, tci=tci: E.transpose(
                    b[:, fc * 128:(fc + 1) * 128],
                    kf[:, fc * 512 + tci * 128: fc * 512 + tci * 128 + 128],
                    ctx.ident_bf16[:]), waits=w + ([ctx.identb_ready] if fc == 0 else []))
            ev = P.op("vector", lambda E, b=bank, tci=tci: E.tensor_copy(
                ktg[:, tci * 1024:(tci + 1) * 1024], b[:, 0:1024]), waits=[tev])
            ctx.pb_ev[id(bank)] = ev
            kt_evs[tci] = ev

        for tci in range(4):
            c = tg * 4 + tci
            smt_evs = {}
            num_done = []
            for h in range(16):
                fc, r0 = h // 2, (h % 2) * 64
                bank, bfree = ctx.bank()
                m1 = mm(P, ctx, bank,
                        kf[r0:r0 + 64, fc * 512 + tci * 128: fc * 512 + tci * 128 + 128],
                        qf[r0:r0 + 64, fc * 512 + tci * 128: fc * 512 + tci * 128 + 128],
                        128, True, True, waits=[pevs[(0, fc)], pevs[(1, fc)], bfree])
                smev = P.op("vector", lambda E, b=bank, h=h: E.tensor_tensor(
                    smt[:, (h % 8) * 128:(h % 8 + 1) * 128], b[:, 0:128], maskT[:], ALU.mult),
                    waits=[m1, emk, ctx.gate("smt%d" % (h % 8))])
                ctx.pb_ev[id(bank)] = smev
                smt_evs[h] = smev
                nbank, nbfree = ctx.bank()
                mmev0 = mm(P, ctx, nbank, smt[:, (h % 8) * 128:(h % 8 + 1) * 128],
                           vg[:, tci * 1040 + h * 65: tci * 1040 + h * 65 + 65],
                           65, True, False, waits=[smev, pevs[("v", tci, h // 8)], nbfree])
                def mksz(nb=nbank, h=h, fc=fc, r0=r0, tci=tci):
                    def f(E):
                        return E.matmul(nb[:, 0:65], qf[r0:r0 + 64, fc * 512 + tci * 128: fc * 512 + tci * 128 + 128],
                                        szsb(h), start=False, stop=True)
                    return f
                nev = P.op("tensor", mksz(), waits=[sz_evac, ez] if h == 0 else [])
                ctx.set_gate("smt%d" % (h % 8), nev)
                rev = P.op("vector", lambda E, nb=nbank, h=h: E.reciprocal(la[:, h:h + 1], nb[:, 64:65]),
                           waits=[nev])
                oev = P.op("vector", lambda E, nb=nbank, h=h: E.tensor_scalar(
                    otm[:, h * 64:(h + 1) * 64], nb[:, 0:64], la[:, h:h + 1], None, ALU.mult),
                    waits=[rev, ctx.gate("otm%d" % h)])
                ctx.pb_ev[id(nbank)] = oev
                ctx.set_gate("otm%d" % h, oev)
                num_done.append(oev)
            dbanks = []
            dmms = []
            for half in range(2):
                bk, bkfree = ctx.bank()
                dbanks.append(bk)
                for hh in range(8):
                    h = half * 8 + hh
                    r0b = (h % 2) * 64
                    off = ((h // 2) % 4) * 65
                    w = []
                    if hh == 0:
                        w = [kt_evs[tci], bkfree]
                    def mks(bk=bk, r0b=r0b, off=off, h=h, tci=tci):
                        def f(E):
                            return E.matmul(bk[r0b:r0b + 64, off:off + 65],
                                            ktg[:, tci * 1024 + h * 64: tci * 1024 + h * 64 + 64],
                                            vg[:, tci * 1040 + h * 65: tci * 1040 + h * 65 + 65],
                                            start=True, stop=True, tile_position=(0, r0b))
                        return f
                    dmms.append(P.op("tensor", mks(), waits=w))
            ad1 = P.op("vector", lambda E, b=dbanks[0]: E.tensor_add(szf[:, 0:260], szf[:, 0:260], b[:, 0:260]),
                       waits=[dmms[7], ezf, ctx.gate("szfa")])
            ad2 = P.op("vector", lambda E, b=dbanks[1]: E.tensor_add(szf[:, 260:520], szf[:, 260:520], b[:, 0:260]),
                       waits=[dmms[15], ctx.gate("szfb")])
            ctx.set_gate("szfa", ad1)
            ctx.set_gate("szfb", ad2)
            ctx.pb_ev[id(dbanks[0])] = ad1
            ctx.pb_ev[id(dbanks[1])] = ad2
            cp = P.op("vector", lambda E: E.tensor_copy(szb[:], szf[:]), waits=[ad1, ad2, ez])
            sz_evac = cp
            ctx.set_gate("szb", cp)
            bank, bfree = ctx.bankt()
            tev = None
            for fc2 in range(8):
                w = [num_done[fc2 * 2], num_done[fc2 * 2 + 1]]
                if fc2 == 0:
                    w = w + [bfree, ctx.gate("aofc"), ctx.identb_ready]
                tev = P.op("tensor", lambda E, b=bank, fc2=fc2: E.transpose(
                    b[:, fc2 * 128:(fc2 + 1) * 128],
                    otm[:, fc2 * 128:(fc2 + 1) * 128], ctx.ident_bf16[:]), waits=w)
            aev = P.op("vector", lambda E, b=bank: E.tensor_copy(aofc[:], b[:, 0:1024]),
                       waits=[tev])
            ctx.pb_ev[id(bank)] = aev
            add_evs = []
            for no in range(2):
                bank, bfree = ctx.bank()
                for fi in range(8):
                    w = [aev, ew[3], bfree] if fi == 0 else []
                    mm(P, ctx, bank, aofc[:, fi * 128:(fi + 1) * 128],
                       w_sb[:, 3 * 8192 + fi * 1024 + no * 512: 3 * 8192 + fi * 1024 + no * 512 + 512],
                       512, fi == 0, False, waits=w)
                mev = mm(P, ctx, bank, ctx.ones_rowb[0:1, 0:128],
                         b_row[0:1, 3 * 1024 + no * 512: 3 * 1024 + no * 512 + 512],
                         512, False, True, waits=[ebr, ctx.onesb_ready])
                def mkadd(c=c, no=no, bank=bank):
                    def f(E):
                        sl = x_tm[:, c * D + no * 512: c * D + no * 512 + 512]
                        return E.tensor_tensor(sl, sl, bank[:, 0:512], ALU.add)
                    return f
                ae = P.op("vector", mkadd(), waits=[mev])
                ctx.pb_ev[id(bank)] = ae
                add_evs.append(ae)
            ctx.set_gate("aofc", P.barrier_ev("tensor"))
            out_evs.append(emit_ln_inplace(P, ctx, c, add_evs[-1]))
        gbar = P.barrier_ev("tensor")
        ctx.set_gate("qkf", gbar)
        ctx.set_gate("vg", gbar)
        ctx.set_gate("ktg", gbar)
        ctx.set_gate("xfmg", gbar)

    for i in range(4):
        ctx.set_gate("l0aw%d" % i, P.barrier_ev("tensor"))
    ctx.free(x_fm, w_sb, b_row, qf, kf, vg, ktg, szb, szf, otm, aofc, smt, es, la, dump)
    ctx.new_phase(P)
    return out_evs



NHEAD = 16
W1HALF = [list(range(0, 11)), list(range(11, 22))]


def fc_rows(fc):
    return 42 if fc == 21 else 128


def emit_ln_buf(P, ctx, sl, slot, dep_ev, eps=1e-5):
    st = ctx.stat[:, slot * 8:(slot + 1) * 8]
    s1, s2, m, w, u, r, ri, s2b = (st[:, i:i + 1] for i in range(8))
    dump = ctx.dump
    gst = ctx.gate("lnst%d" % slot)
    e1 = P.op("vector", lambda E, o=s1, i=sl: E.reduce_sum(o, i, AX.X), waits=[dep_ev, gst])
    e2a = P.op("scalar", lambda E, o=dump, i=sl, a=s2: E.activation(o[:, 0:512], i[:, 0:512], AF.Square, accum_out=a),
               waits=[dep_ev, gst, ctx.gate("dump")])
    e2b = P.op("scalar", lambda E, o=dump, i=sl, a=s2b: E.activation(o[:, 0:512], i[:, 512:1024], AF.Square, accum_out=a),
               waits=[e2a])
    ctx.set_gate("dump", e2b)
    e3 = P.op("gpsimd", lambda E, o=m, i=s1: E.tensor_scalar(o, i, 1.0 / D, None, ALU.mult), waits=[e1])
    e4a = P.op("gpsimd", lambda E, o=w, a=s2, b=s2b: E.tensor_add(o, a, b), waits=[e2a, e2b])
    e4 = P.op("gpsimd", lambda E, o=w: E.tensor_scalar(o, o, 1.0 / D, None, ALU.mult), waits=[e4a])
    e5 = P.op("gpsimd", lambda E, o=u, i=m: E.tensor_mul(o, i, i), waits=[e3])
    e6 = P.op("gpsimd", lambda E, o=w, a=u: E.tensor_sub(o, o, a), waits=[e4, e5])
    e7 = P.op("scalar", lambda E, o=r, i=w: E.activation(o, i, AF.Sqrt, bias=ctx.eps_col[:, 0:1]),
              waits=[e6, ctx.eps_ready])
    e8 = P.op("vector", lambda E, o=ri, i=r: E.reciprocal(o, i), waits=[e7])
    e9 = P.op("vector", lambda E, o=sl, a=m, b=ri: E.tensor_scalar(o, o, a, b, ALU.subtract, ALU.mult),
              waits=[e8, e3, dep_ev])
    ctx.set_gate("lnst%d" % slot, e9)
    return e9


def emit_l0f_layer(P, ctx, seq, li, final_ln=False, prev_final=None):
    NT = NTC
    Tp = T + 1
    g = ctx.gate
    sg = ctx.set_gate

    xs_fm = ctx.alloc("xs_fm", [128, NFC * Tp], BF16)
    wqkv = ctx.alloc("wqkv", [128, 8 * 3072], BF16)
    wo = ctx.alloc("wo", [128, 8 * 1024], BF16)
    qkvg = ctx.alloc("qkvg", [128, 4 * 3072], BF16)
    qf = ctx.alloc("qf", [128, 2 * 1024], BF16)
    kf = ctx.alloc("kf", [128, 4 * 1024], BF16)
    vr = ctx.alloc("vr", [128, 4 * 1024], BF16)
    aofm = ctx.alloc("aofm", [128, 2 * 1024], BF16)
    tabs = ctx.alloc("tabs", [128, 4 * NT * 64], F32)
    reps = ctx.alloc("reps", [128, 4 * 64], F32)
    t1 = ctx.alloc("t1", [128, 4 * 1024], F32)
    t2 = ctx.alloc("t2", [128, 2 * 1024], F32)
    lmask = ctx.alloc("lmask", [128, 384], BF16)
    p_sb = ctx.alloc("p_sb", [128, 3 * 384], F32)
    p2_sb = ctx.alloc("p2_sb", [128, 2 * 384], BF16)
    diag = ctx.alloc("diag", [128, 2 * 128], BF16)
    pt_sb = ctx.alloc("pt_sb", [128, 3 * 384], BF16)
    xstage = ctx.alloc("xstage", [128, 2 * 1024], F32)
    rstd = ctx.alloc("rstd", [128, 4 * 16], F32)
    dump = ctx.dump = ctx.alloc("dumpA", [128, 256], F32)

    ewq = P.dma("gpsimd", wqkv[:].rearrange("p (ci n) -> p ci n", ci=8),
                ctx.d_wqkv[li].rearrange("(ci p) n -> p ci n", p=128), waits=[g("wqkv")])
    ewo = P.dma("gpsimd", wo[:].rearrange("p (ci n) -> p ci n", ci=8),
                ctx.d_wo0[li].rearrange("(ci p) n -> p ci n", p=128), waits=[g("wo")])
    etab = P.dma("sync", tabs[:].rearrange("p (j n) -> p j n", j=4),
                 ctx.d_rotab[:, :, :, :].rearrange("j p tc d -> p j (tc d)"), waits=[g("tabs")])
    erep = P.dma("sync", reps[:].rearrange("p (j n) -> p j n", j=4),
                 ctx.d_qs4[li].rearrange("j p d -> p j d"), waits=[g("tabs")])
    emsk = P.dma("gpsimd", lmask[:], ctx.d_lmask[:, :], waits=[g("lmask")])
    tab_evs = []
    for j in range(4):
        def mkc(j=j):
            def f(E):
                r = reps[:, j * 64:(j + 1) * 64]
                rb = bass.AP(r.tensor, r.offset, [[r.ap[0][0], 128], [0, NT], [1, 64]])
                t3 = tabs[:, j * NT * 64:(j + 1) * NT * 64].rearrange("p (tc d) -> p tc d", d=64)
                return E.tensor_tensor(t3, t3, rb, ALU.mult)
            return f
        tab_evs.append(P.op("gpsimd", mkc(), waits=[etab, erep]))
    tabs_done = tab_evs[-1]

    epad = P.op("gpsimd", lambda E: E.memset(
        xs_fm[:].rearrange("p (c t) -> p c t", c=NFC)[:, :, 0:1], 0.0), waits=[g("xsfm")])
    xev = {}
    fm_evs = []
    for tc in range(NT):
        sl = tc % 2
        w0 = [g("xst%d" % sl)]
        if prev_final is not None:
            w0.append(prev_final[tc])
        exl = P.dma("scalar", xstage[:, sl * 1024:(sl + 1) * 1024],
                    ctx.d_xcur[seq, tc * 128:(tc + 1) * 128, :], waits=w0)
        xev[tc] = exl
        for half in range(2):
            bank, bfree = ctx.bank()
            mev = None
            for j in range(4):
                fc = half * 4 + j
                def mkt(fc=fc, j=j, bank=bank, sl=sl):
                    def f(E):
                        return E.transpose(
                            bank[:, j * 128:(j + 1) * 128],
                            xstage[:, sl * 1024 + fc * 128: sl * 1024 + fc * 128 + 128],
                            ctx.ident_f32[:])
                    return f
                w = [exl, bfree, ctx.ident_ready] if j == 0 else []
                mev = P.op("tensor", mkt(), waits=w)

            def mkev(tc=tc, half=half, bank=bank):
                def f(E):
                    dst = xs_fm[:]
                    base = (half * 4) * Tp + 1 + tc * 128
                    d3 = bass.AP(dst.tensor, dst.offset + base, [[dst.ap[0][0], 128], [Tp, 4], [1, 128]])
                    return E.tensor_copy(d3, bank[:, 0:512].rearrange("p (f t) -> p f t", f=4))
                return f
            ev = P.op("vector", mkev(), waits=[mev, epad])
            ctx.pb_ev[id(bank)] = ev
            fm_evs.append(ev)
        sg("xst%d" % sl, fm_evs[-1])
    fm_done = fm_evs[-1]
    a0bar = P.barrier_ev("tensor")
    sg("xs20", a0bar)
    sg("xs21", a0bar)

    pj_evs = {}

    def emit_qkv(c):
        slot = c % 4
        for fot in range(6):
            bank, bfree = ctx.bank()
            for fi in range(8):
                off = 1 if fi < 4 else 0
                w = [ewq, fm_done, bfree, g("qkvg_s%d" % slot)] if fi == 0 else []
                mm(P, ctx, bank,
                   xs_fm[:, fi * Tp + off + c * 128: fi * Tp + off + c * 128 + 128],
                   wqkv[:, fi * 3072 + fot * 512: fi * 3072 + fot * 512 + 512],
                   512, fi == 0, fi == 7, waits=w)
            mev = P.barrier_ev("tensor")

            def mkev(slot=slot, fot=fot, bank=bank):
                def f(E):
                    return E.copy(qkvg[:, slot * 3072 + fot * 512: slot * 3072 + fot * 512 + 512],
                                  bank[:, 0:512])
                return f
            ev = P.op("scalar", mkev(), waits=[mev])
            ctx.pb_ev[id(bank)] = ev
            pj_evs[(c, fot)] = ev

    def emit_chunk(qc):
        tc4 = qc % 4
        rco = (qc % 2) * 32
        qkv_ready = [pj_evs[(qc, f)] for f in range(6)]
        rr_evs = {}
        rot_evs = {}
        for pi in (0, 1):
            base = pi * 1024
            reg = qkvg[:, tc4 * 3072 + base: tc4 * 3072 + base + 1024]
            qw = qkv_ready[2 * pi: 2 * pi + 2]
            sqe = None
            for h4 in range(NHEAD):
                wq4 = qw + [g("rst%d_%d" % (pi, qc % 2)), g("sqd%d" % (h4 % 4))] if h4 == 0 else [g("sqd%d" % (h4 % 4))]
                def mksq(h4=h4, pi=pi, reg=reg, dump=dump, rco=rco):
                    def f(E):
                        return E.activation(
                            dump[:, (h4 % 4) * 64:(h4 % 4) * 64 + 64],
                            reg[:, h4 * 64:(h4 + 1) * 64], AF.Square,
                            accum_out=rstd[:, rco + pi * 16 + h4: rco + pi * 16 + h4 + 1])
                    return f
                sqe = P.op("scalar", mksq(), waits=wq4)
                sg("sqd%d" % (h4 % 4), sqe)
            sqr = P.op("scalar", lambda E, o=rstd, pi=pi, rco=rco: E.activation(
                o[:, rco + pi * 16:rco + (pi + 1) * 16], o[:, rco + pi * 16:rco + (pi + 1) * 16], AF.Sqrt), waits=[sqe])
            cle = P.op("vector", lambda E, o=rstd, pi=pi, rco=rco: E.tensor_scalar(
                o[:, rco + pi * 16:rco + (pi + 1) * 16], o[:, rco + pi * 16:rco + (pi + 1) * 16], 1e-12, None, ALU.max),
                waits=[sqr])
            rre = P.op("vector", lambda E, o=rstd, pi=pi, rco=rco: E.reciprocal(
                o[:, rco + pi * 16:rco + (pi + 1) * 16], o[:, rco + pi * 16:rco + (pi + 1) * 16]), waits=[cle])
            rr_evs[pi] = rre
            ct, st_ = (0, 1) if pi == 0 else (2, 3)
            tb_c = tabs[:, (ct * NT + qc) * 64: (ct * NT + qc) * 64 + 64]
            tb_s0 = tabs[:, (st_ * NT + qc) * 64: (st_ * NT + qc) * 64 + 32]
            tb_s1 = tabs[:, (st_ * NT + qc) * 64 + 32: (st_ * NT + qc) * 64 + 64]
            apc = bass.AP(tb_c.tensor, tb_c.offset, [[tb_c.ap[0][0], 128], [0, 16], [1, 64]])
            aps0 = bass.AP(tb_s0.tensor, tb_s0.offset, [[tb_s0.ap[0][0], 128], [0, 16], [1, 32]])
            aps1 = bass.AP(tb_s1.tensor, tb_s1.offset, [[tb_s1.ap[0][0], 128], [0, 16], [1, 64 - 32]])
            par = (qc % 2) * 2 + pi
            t1s = t1[:, par * 1024:(par + 1) * 1024]
            t2s = t2[:, pi * 1024:(pi + 1) * 1024]
            e_t1 = P.op("vector", lambda E, r=reg, apc=apc, o=t1s: E.tensor_tensor(
                o.rearrange("p (h d) -> p h d", d=64),
                r.rearrange("p (h d) -> p h d", d=64), apc, ALU.mult),
                waits=qw + [tabs_done, g("t1_%d" % par)])
            e_t2a = P.op("gpsimd", lambda E, r=reg, aps0=aps0, o=t2s: E.tensor_tensor(
                o.rearrange("p (h d) -> p h d", d=64)[:, :, 0:32],
                r.rearrange("p (h d) -> p h d", d=64)[:, :, 32:64], aps0, ALU.mult),
                waits=qw + [tabs_done, g("t2_%d" % pi)])
            e_t2b = P.op("gpsimd", lambda E, r=reg, aps1=aps1, o=t2s: E.tensor_tensor(
                o.rearrange("p (h d) -> p h d", d=64)[:, :, 32:64],
                r.rearrange("p (h d) -> p h d", d=64)[:, :, 0:32], aps1, ALU.mult),
                waits=[])
            e_rot = P.op("gpsimd", lambda E, r=reg, a=t1s, b=t2s: E.tensor_add(r, a, b),
                         waits=[e_t1, e_t2a, e_t2b, sqe])
            sg("t1_%d" % par, e_rot)
            sg("t2_%d" % pi, e_rot)
            if pi == 1:
                last = e_rot
                for h in range(NHEAD):
                    def mkr(h=h, r=reg, rco=rco):
                        def f(E):
                            return E.activation(
                                r[:, h * 64:(h + 1) * 64], r[:, h * 64:(h + 1) * 64],
                                AF.Identity, scale=rstd[:, rco + 16 + h:rco + 16 + h + 1])
                        return f
                    last = P.op("scalar", mkr(), waits=[e_rot, rre] if h == 0 else [])
                rot_evs[pi] = last
            else:
                rot_evs[pi] = e_rot
        qslot = qc % 2
        kslot = qc % 4
        tevs = {}
        for pi, dstbuf, slot, gname in ((0, qf, qslot, "qfr"), (1, kf, kslot, "kfr")):
            bank, bfree = ctx.bankt()
            tev = None
            for fc in range(8):
                def mkt(pi=pi, fc=fc, bank=bank, tc4=tc4):
                    def f(E):
                        srcc = qkvg[:, tc4 * 3072 + pi * 1024 + fc * 128: tc4 * 3072 + pi * 1024 + fc * 128 + 128]
                        return E.transpose(bank[:, fc * 128:(fc + 1) * 128], srcc, ctx.ident_bf16[:])
                    return f
                w = [rot_evs[pi], bfree, ctx.identb_ready] if fc == 0 else []
                tev = P.op("tensor", mkt(), waits=w)
            ev = P.op("vector", lambda E, b=bank, dstbuf=dstbuf, slot=slot: E.tensor_copy(
                dstbuf[:, slot * 1024:(slot + 1) * 1024], b[:, 0:1024]),
                waits=[tev, g(gname + str(slot))])
            ctx.pb_ev[id(bank)] = ev
            tevs[pi] = ev
        vslot = qc % 4
        vev = P.op("gpsimd", lambda E, tc4=tc4, vslot=vslot: E.tensor_copy(
            vr[:, vslot * 1024:(vslot + 1) * 1024],
            qkvg[:, tc4 * 3072 + 2048: tc4 * 3072 + 3072]),
            waits=qkv_ready[4:6] + [g("vr%d" % vslot)])
        sg("qkvg_s%d" % tc4, P.barrier_ev("gpsimd"))
        kcs = [k for k in (qc - 2, qc - 1, qc) if k >= 0]
        nk = len(kcs)
        o_evs = []
        st1 = {}
        st2 = {}

        def stage1(h):
            fc, r0 = h // 2, (h % 2) * 64
            sbank, sfree = ctx.bank()
            smev = None
            for j, kc in enumerate(kcs):
                def mks(j=j, kc=kc, fc=fc, r0=r0, sbank=sbank, qslot=qslot):
                    def f(E):
                        return E.matmul(
                            sbank[:, j * 128:(j + 1) * 128],
                            qf[r0:r0 + 64, qslot * 1024 + fc * 128: qslot * 1024 + fc * 128 + 128],
                            kf[r0:r0 + 64, (kc % 4) * 1024 + fc * 128: (kc % 4) * 1024 + fc * 128 + 128],
                            start=True, stop=True)
                    return f
                w = []
                if j == 0:
                    w = [tevs[0], sfree]
                    if h == 0:
                        w.append(tevs[1])
                smev = P.op("tensor", mks(), waits=w)
            st1[h] = (sbank, smev)

        def stage2a(h):
            hs = h % 3
            po = hs * 384
            so = 96 + (h % 4) * 8
            sbank, smev = st1.pop(h)
            sme = P.op("vector", lambda E, b=sbank, h=h, nk=nk, po=po, rco=rco: E.scalar_tensor_tensor(
                p_sb[:, po:po + nk * 128], b[:, 0:nk * 128], rstd[:, rco + h:rco + h + 1],
                lmask[:, (3 - nk) * 128:384], ALU.mult, ALU.add),
                waits=[smev, rr_evs[0], emsk, g("p_sb%d" % hs)])
            ctx.pb_ev[id(sbank)] = sme
            mne = P.op("vector", lambda E, nk=nk, po=po, so=so: E.reduce_max(
                ctx.stat[:, so:so + 1], p_sb[:, po:po + nk * 128], AX.X, negate=True),
                waits=[sme, g("mn%d" % (h % 4))])
            po2 = (h % 2) * 384
            pe = P.op("scalar", lambda E, nk=nk, po=po, po2=po2, so=so: E.activation(
                p2_sb[:, po2:po2 + nk * 128], p_sb[:, po:po + nk * 128], AF.Exp,
                bias=ctx.stat[:, so:so + 1], accum_out=ctx.stat[:, so + 1:so + 2]),
                waits=[mne, g("p2x%d" % (h % 2))])
            sg("mn%d" % (h % 4), pe)
            sg("p_sb%d" % hs, pe)
            st2[h] = pe

        def stage2b(h):
            fc, r0 = h // 2, (h % 2) * 64
            hs = h % 3
            po = hs * 384
            so = 96 + (h % 4) * 8
            po2 = (h % 2) * 384
            pe = st2.pop(h)
            rcpe = P.op("vector", lambda E, so=so: E.reciprocal(
                ctx.stat[:, so + 2:so + 3], ctx.stat[:, so + 1:so + 2]), waits=[pe])
            dge = P.op("vector", lambda E, hs2=h % 2, so=so: E.tensor_scalar(
                diag[:, hs2 * 128:(hs2 + 1) * 128], ctx.ident_bf16[:],
                ctx.stat[:, so + 2:so + 3], None, ALU.mult),
                waits=[rcpe, ctx.identb_ready, g("diag%d" % (h % 2))])
            tbank, tfree = ctx.bank()
            ptev = None
            for j in range(nk):
                def mkt2(j=j, tbank=tbank, po2=po2, hs2=h % 2):
                    def f(E):
                        return E.matmul(tbank[:, j * 128:(j + 1) * 128],
                                        p2_sb[:, po2 + j * 128: po2 + (j + 1) * 128],
                                        diag[:, hs2 * 128:(hs2 + 1) * 128],
                                        start=True, stop=True)
                    return f
                w = [pe, dge, tfree] if j == 0 else []
                ptev = P.op("tensor", mkt2(), waits=w)
            sg("p2x%d" % (h % 2), ptev)
            sg("diag%d" % (h % 2), ptev)
            pte = P.op("scalar", lambda E, b=tbank, nk=nk, po=po: E.copy(
                pt_sb[:, po:po + nk * 128], b[:, 0:nk * 128]),
                waits=[ptev, g("pt%d" % hs)])
            ctx.pb_ev[id(tbank)] = pte
            obank, ofree = ctx.bank()
            omev = None
            for j, kc in enumerate(kcs):
                def mko(j=j, kc=kc, h=h, obank=obank, nk=nk, po=po):
                    def f(E):
                        return E.matmul(obank[0:64, 0:128],
                                        vr[:, (kc % 4) * 1024 + h * 64: (kc % 4) * 1024 + h * 64 + 64],
                                        pt_sb[:, po + j * 128: po + j * 128 + 128],
                                        start=(j == 0), stop=(j == nk - 1))
                    return f
                w = [pte] if j == 0 else []
                if j == 0:
                    w += [ofree, vev]
                omev = P.op("tensor", mko(), waits=w)
            sg("pt%d" % hs, omev)
            oev = P.op("scalar", lambda E, b=obank, fc=fc, r0=r0, qp=qc % 2: E.copy(
                aofm[r0:r0 + 64, qp * 1024 + fc * 128: qp * 1024 + fc * 128 + 128],
                b[0:64, 0:128]), waits=[omev, g("aofm%d" % h)])
            ctx.pb_ev[id(obank)] = oev
            o_evs.append(oev)
            sg("aofm%d" % h, oev)

        for h in range(NHEAD + 2):
            if h < NHEAD:
                stage1(h)
            if 1 <= h <= NHEAD:
                stage2a(h - 1)
            if h >= 2:
                stage2b(h - 2)
        sg("qfr%d" % qslot, P.barrier_ev("tensor"))
        sg("rst0_%d" % (qc % 2), P.barrier_ev("vector"))
        sg("rst1_%d" % (qc % 2), P.barrier_ev("vector"))
        tb_ = P.barrier_ev("tensor")
        gb_ = P.barrier_ev("gpsimd")
        ctx.gates["qkvg_s%d" % tc4] = [tb_, gb_]
        sl2 = qc % 2
        exl2 = P.dma("scalar", xstage[:, sl2 * 1024:(sl2 + 1) * 1024],
                     ctx.d_xcur[seq, qc * 128:(qc + 1) * 128, :], waits=[g("xs2%d" % sl2)])
        add_evs = []
        for no in range(2):
            bank, bfree = ctx.bank()
            for fi in range(8):
                w = [o_evs[-1], ewo, bfree] if fi == 0 else []
                mm(P, ctx, bank, aofm[:, (qc % 2) * 1024 + fi * 128: (qc % 2) * 1024 + fi * 128 + 128],
                   wo[:, fi * 1024 + no * 512: fi * 1024 + no * 512 + 512],
                   512, fi == 0, fi == 7, waits=w)
            mev = P.barrier_ev("tensor")

            def mka(no=no, bank=bank, sl2=sl2):
                def f(E):
                    slx = xstage[:, sl2 * 1024 + no * 512: sl2 * 1024 + no * 512 + 512]
                    return E.tensor_tensor(slx, slx, bank[:, 0:512], ALU.add)
                return f
            ae = P.op("vector", mka(), waits=[mev, exl2])
            ctx.pb_ev[id(bank)] = ae
            add_evs.append(ae)
        est = P.dma("gpsimd", ctx.d_xnxt[seq, qc * 128:(qc + 1) * 128, :],
                    xstage[:, sl2 * 1024:(sl2 + 1) * 1024], waits=add_evs)
        sg("xs2%d" % sl2, est)
        xev[("a", qc)] = est

    for c in range(NT + 2):
        if c < NT:
            emit_qkv(c)
        if c >= 2:
            emit_chunk(c - 2)
    a_done_t = P.barrier_ev("tensor")
    ctx.free(xs_fm, wqkv, wo, qkvg, qf, kf, vr, aofm, tabs, reps, t1, t2,
             lmask, p_sb, p2_sb, diag, pt_sb, xstage, rstd, dump)
    ctx.new_phase(P)

    xn_fm = ctx.alloc("xn_fm", [128, NFC * T], BF16)
    prod = ctx.alloc("prod", [128, 11 * T], BF16)
    w2h = ctx.alloc("w2h", [128, 11 * 1024], BF16)
    w1c = ctx.alloc("w1c", [128, 2 * 8 * 128], BF16)
    b1r = ctx.alloc("b1r", [1, 5460], BF16)
    b2r = ctx.alloc("b2r", [1, 1024], BF16)
    grep = ctx.alloc("grep", [128, 1024], F32)
    brep = ctx.alloc("brep", [128, 1024], F32)
    xst = ctx.alloc("xstB", [128, 2 * 1024], F32)
    xs_t = ctx.alloc("xs_t", [128, 2 * 1024], F32)
    ggb = ctx.alloc("gg", [128, 2 * 512], F32)
    x0st = ctx.alloc("x0st", [128, 1024], F32) if final_ln else None
    dump = ctx.dump = ctx.alloc("dumpB", [128, 512], F32)

    eb1 = P.dma("gpsimd", b1r[0:1, :], ctx.d_b1[li:li + 1, :], waits=[g("b1r")])
    eb2 = P.dma("gpsimd", b2r[0:1, :], ctx.d_b2[li:li + 1, :], waits=[g("b2r")])
    egr = P.dma("sync", grep[:], ctx.d_lng[li], waits=[g("grep")])
    ebr_ = P.dma("sync", brep[:], ctx.d_lnb[li], waits=[g("brep")])

    xnevs = []
    prev_stage = None
    z1_prev = None
    for tc in range(NT):
        sl = tc % 2
        exl = P.dma("scalar", xst[:, sl * 1024:(sl + 1) * 1024],
                    ctx.d_xnxt[seq, tc * 128:(tc + 1) * 128, :],
                    waits=[xev[("a", tc)], g("xb%d" % sl), z1_prev])
        c1 = P.op("vector", lambda E, sl=sl: E.tensor_copy(
            xs_t[:, sl * 1024: sl * 1024 + 512], xst[:, sl * 1024: sl * 1024 + 512]),
            waits=[exl, g("xt%d" % sl)])
        if tc == 0:
            z1 = P.op("gpsimd", lambda E, sl=sl: E.memset(xs_t[0:1, sl * 1024 + 512: sl * 1024 + 1024], 0.0),
                      waits=[g("xt%d" % sl)])
        else:
            z1 = P.dma("sync", xs_t[0:1, sl * 1024 + 512: sl * 1024 + 1024],
                       xst[127:128, (1 - sl) * 1024 + 512: (1 - sl) * 1024 + 1024],
                       waits=[prev_stage, g("xt%d" % sl)])
        sh = P.dma("sync", xs_t[1:128, sl * 1024 + 512: sl * 1024 + 1024],
                   xst[0:127, sl * 1024 + 512: sl * 1024 + 1024],
                   waits=[exl, g("xt%d" % sl)])
        prev_stage = exl
        z1_prev = z1
        sg("xb%d" % sl, exl)
        sle = xs_t[:, sl * 1024:(sl + 1) * 1024]
        dep = Ev(sh.sem, sh.count)
        e0 = P.op("vector", lambda E: E.nop(), waits=[c1, z1, sh])
        lne = emit_ln_buf(P, ctx, sle, tc % 8, e0)
        e10 = P.op("gpsimd", lambda E, o=sle: E.tensor_mul(o, o, grep[:]), waits=[lne, egr])
        e11 = P.op("gpsimd", lambda E, o=sle: E.tensor_add(o, o, brep[:]), waits=[e10, ebr_])
        for half in range(2):
            bank, bfree = ctx.bank()
            mev = None
            for j in range(4):
                fc = half * 4 + j
                def mkt(fc=fc, j=j, bank=bank, sl=sl):
                    def f(E):
                        return E.transpose(bank[:, j * 128:(j + 1) * 128],
                                           xs_t[:, sl * 1024 + fc * 128: sl * 1024 + fc * 128 + 128],
                                           ctx.ident_f32[:])
                    return f
                w = [e11, bfree] if j == 0 else []
                mev = P.op("tensor", mkt(), waits=w)

            def mkev(tc=tc, half=half, bank=bank):
                def f(E):
                    dst = xn_fm[:]
                    base = (half * 4) * T + tc * 128
                    d3 = bass.AP(dst.tensor, dst.offset + base, [[dst.ap[0][0], 128], [T, 4], [1, 128]])
                    return E.tensor_copy(d3, bank[:, 0:512].rearrange("p (f t) -> p f t", f=4))
                return f
            ev = P.op("vector", mkev(), waits=[mev, g("xnfm")])
            ctx.pb_ev[id(bank)] = ev
            xnevs.append(ev)
        sg("xt%d" % sl, P.barrier_ev("tensor"))
    xn_done = xnevs[-1]

    final_evs = []
    bevs = {}
    for hi, half in enumerate(W1HALF):
        prod_evs = []
        for idx, fcl in enumerate(half):
            rows = fc_rows(fcl)
            ea = P.dma("gpsimd", w1c[0:rows, 0:8 * 128].rearrange("p (ci n) -> p ci n", ci=8)
                       if rows == 128 else w1c[:, 0:8 * rows].rearrange("p (ci n) -> p ci n", ci=8),
                       ctx.d_w1[li][:, fcl * 128: fcl * 128 + rows].rearrange("(ci p) n -> p ci n", p=128),
                       waits=[g("w1a")])
            eg_ = P.dma("gpsimd", w1c[:, 8 * 128:8 * 128 + 8 * rows].rearrange("p (ci n) -> p ci n", ci=8),
                        ctx.d_w1[li][:, 2730 + fcl * 128: 2730 + fcl * 128 + rows].rearrange("(ci p) n -> p ci n", p=128),
                        waits=[g("w1g")])
            for tg in range(T // 512):
                abank, afree = ctx.bank()
                gbank, gfree = ctx.bank()
                amms = []
                for fi in range(8):
                    w = [ea, xn_done, afree, g("prodh")] if fi == 0 else []
                    def mk1(fi=fi, rows=rows, abank=abank, tg=tg):
                        def f(E):
                            return E.matmul(abank[0:rows, 0:512],
                                            w1c[:, fi * rows: fi * rows + rows] if rows != 128 else w1c[:, fi * 128: fi * 128 + 128],
                                            xn_fm[:, fi * T + tg * 512: fi * T + tg * 512 + 512],
                                            start=(fi == 0), stop=False)
                        return f
                    amms.append(P.op("tensor", mk1(), waits=w))
                def mkba(rows=rows, abank=abank, fcl=fcl):
                    def f(E):
                        return E.matmul(abank[0:rows, 0:512], b1r[0:1, fcl * 128: fcl * 128 + rows],
                                        ctx.ones_rowb[0:1, 0:512], start=False, stop=True)
                    return f
                mma = P.op("tensor", mkba(), waits=[eb1, ctx.onesb_ready])
                for fi in range(8):
                    w = [eg_, xn_done, gfree] if fi == 0 else []
                    def mk1g(fi=fi, rows=rows, gbank=gbank, tg=tg):
                        def f(E):
                            return E.matmul(gbank[0:rows, 0:512],
                                            w1c[:, 8 * 128 + fi * rows: 8 * 128 + fi * rows + rows],
                                            xn_fm[:, fi * T + tg * 512: fi * T + tg * 512 + 512],
                                            start=(fi == 0), stop=False)
                        return f
                    P.op("tensor", mk1g(), waits=w)
                def mkbg(rows=rows, gbank=gbank, fcl=fcl):
                    def f(E):
                        return E.matmul(gbank[0:rows, 0:512], b1r[0:1, 2730 + fcl * 128: 2730 + fcl * 128 + rows],
                                        ctx.ones_rowb[0:1, 0:512], start=False, stop=True)
                    return f
                mmg = P.op("tensor", mkbg(), waits=[eb1])
                gsl = tg % 2
                gge = P.op("scalar", lambda E, b=gbank, rows=rows, gsl=gsl: E.activation(
                    ggb[0:rows, gsl * 512:(gsl + 1) * 512], b[0:rows, 0:512], AF.Gelu),
                    waits=[mmg, g("gg%d" % gsl)])

                def mkp(idx=idx, tg=tg, rows=rows, abank=abank, gsl=gsl):
                    def f(E):
                        return E.tensor_tensor(
                            prod[0:rows, idx * T + tg * 512: idx * T + tg * 512 + 512],
                            abank[0:rows, 0:512], ggb[0:rows, gsl * 512:(gsl + 1) * 512], ALU.mult)
                    return f
                pev = P.op("vector", mkp(), waits=[mma, gge])
                sg("gg%d" % gsl, pev)
                ctx.pb_ev[id(abank)] = pev
                ctx.pb_ev[id(gbank)] = pev
                prod_evs.append(pev)
            sg("w1a", P.barrier_ev("tensor"))
            sg("w1g", P.barrier_ev("tensor"))
        prod_done = prod_evs[-1]
        nfull = 11 if hi == 0 else 10
        row0 = half[0] * 128
        ew2 = P.dma("gpsimd", w2h[:, 0:nfull * 1024].rearrange("p (ci n) -> p ci n", ci=nfull),
                    ctx.d_w2[li][row0: row0 + nfull * 128, :].rearrange("(ci p) n -> p ci n", p=128),
                    waits=[g("w2h")])
        if hi == 1:
            ew2b = P.dma("gpsimd", w2h[0:42, 10 * 1024:11 * 1024],
                         ctx.d_w2[li][2688:2730, :], waits=[g("w2h")])
        else:
            ew2b = ew2
        for tc in range(NT):
            sl = tc % 2
            exl = P.dma("scalar", xst[:, sl * 1024:(sl + 1) * 1024],
                        ctx.d_xnxt[seq, tc * 128:(tc + 1) * 128, :],
                        waits=[g("xb%d" % sl)] + ([bevs[tc]] if hi == 1 else []))
            adds = []
            for no in range(2):
                bank, bfree = ctx.bank()
                for j, fcl in enumerate(half):
                    rows = fc_rows(fcl)
                    w = [ew2, ew2b, prod_done, bfree] if j == 0 else []
                    def mk2(j=j, rows=rows, tc=tc, no=no, bank=bank, hi=hi, lh=len(half)):
                        def f(E):
                            return E.matmul(bank[:, 0:512],
                                            prod[0:rows, j * T + tc * 128: j * T + tc * 128 + 128],
                                            w2h[0:rows, j * 1024 + no * 512: j * 1024 + no * 512 + 512],
                                            start=(j == 0), stop=(j == lh - 1 and hi == 1))
                        return f
                    P.op("tensor", mk2(), waits=w)
                if hi == 0:
                    mev = mm(P, ctx, bank, ctx.ones_rowb[0:1, 0:128],
                             b2r[0:1, no * 512: no * 512 + 512], 512, False, True, waits=[eb2])
                else:
                    mev = P.barrier_ev("tensor")

                def mka(no=no, bank=bank, sl=sl):
                    def f(E):
                        slx = xst[:, sl * 1024 + no * 512: sl * 1024 + no * 512 + 512]
                        return E.tensor_tensor(slx, slx, bank[:, 0:512], ALU.add)
                    return f
                ae = P.op("vector", mka(), waits=[mev, exl])
                ctx.pb_ev[id(bank)] = ae
                adds.append(ae)
            if hi == 0 or not final_ln:
                est = P.dma("gpsimd", ctx.d_xnxt[seq, tc * 128:(tc + 1) * 128, :],
                            xst[:, sl * 1024:(sl + 1) * 1024], waits=adds)
                sg("xb%d" % sl, est)
                bevs[tc] = est
                if hi == 1:
                    final_evs.append(est)
            else:
                ex0 = P.dma("sync", x0st[:], ctx.d_x0[seq, tc * 128:(tc + 1) * 128, :],
                            waits=[g("x0st")])
                a0 = P.op("vector", lambda E, sl=sl: E.tensor_tensor(
                    xst[:, sl * 1024:(sl + 1) * 1024], xst[:, sl * 1024:(sl + 1) * 1024],
                    x0st[:], ALU.add), waits=adds + [ex0])
                sg("x0st", a0)
                lne = emit_ln_buf(P, ctx, xst[:, sl * 1024:(sl + 1) * 1024], tc % 8, a0)
                est = P.dma("gpsimd", ctx.xout[seq, tc * 128:(tc + 1) * 128, :],
                            xst[:, sl * 1024:(sl + 1) * 1024], waits=[lne])
                sg("xb%d" % sl, est)
                final_evs.append(est)
        sg("prodh", P.barrier_ev("tensor"))
        sg("w2h", P.barrier_ev("tensor"))
    frees = [xn_fm, prod, w2h, w1c, b1r, b2r, grep, brep, xst, xs_t, ggb]
    if final_ln:
        frees.append(x0st)
    frees.append(dump)
    ctx.free(*frees)
    ctx.new_phase(P)
    return final_evs



def host_rotab(n=None):
    n = n or T
    DIM_HEAD, WINDOW = 64, 256
    half = np.arange(0, DIM_HEAD, 2, dtype=np.float32)
    inv_freq = (1.0 / 10000.0 ** (half / DIM_HEAD)).astype(np.float32)
    t = np.arange(n, dtype=np.float32)
    freqs = t[:, None] * inv_freq[None, :]
    freqs = np.concatenate([freqs, freqs], -1)
    base_scale = (half + 0.4 * DIM_HEAD) / (1.4 * DIM_HEAD)
    power = (t - n // 2) / (WINDOW // 2)
    scale = (base_scale[None, :].astype(np.float32) ** power[:, None].astype(np.float32)).astype(np.float32)
    scale = np.concatenate([scale, scale], -1)
    cos, sin = np.cos(freqs), np.sin(freqs)
    sign = np.where(np.arange(64) < 32, -1.0, 1.0).astype(np.float32)
    CS8 = (8.0 * cos * scale).astype(np.float32)
    SS8s = (8.0 * sin * scale * sign[None, :]).astype(np.float32)
    CI = (cos / scale).astype(np.float32)
    SIs = (sin / scale * sign[None, :]).astype(np.float32)
    out = np.stack([CS8, SS8s, CI, SIs])
    ntc = n // 128
    return np.ascontiguousarray(out.reshape(4, ntc, 128, 64).transpose(0, 2, 1, 3))


def host_lmask():
    q = np.arange(128)[:, None]
    k = np.arange(128)[None, :]
    m0 = np.where(q <= k, 0.0, -1e38).astype(np.float32)
    m1 = np.zeros((128, 128), np.float32)
    m2 = np.where(k <= q, 0.0, -1e38).astype(np.float32)
    return np.ascontiguousarray(np.concatenate([m0, m1, m2], axis=1))


def host_qs4(q_scale, k_scale):
    out = np.empty((6, 4, 128, 64), np.float32)
    for li in range(6):
        qs = q_scale[li]
        ks = k_scale[li]
        out[li, 0] = np.tile(qs, (128, 1))
        out[li, 1] = np.tile(np.concatenate([qs[32:], qs[:32]]), (128, 1))
        out[li, 2] = np.tile(ks, (128, 1))
        out[li, 3] = np.tile(np.concatenate([ks[32:], ks[:32]]), (128, 1))
    return out



N_CORES = 8


def _build_l0a():
    nc = bass.Bass("TRN2", target_bir_lowering=False)
    st = ExitStack()
    ctx = Ctx(nc)
    ctx.xin = nc.dram_tensor("xin", [2, T, D], F32, kind="ExternalInput")
    ctx.d_l0a_w = nc.dram_tensor("l0aw", [4, 1024, 1024], F32, kind="ExternalInput")
    ctx.d_l0a_b = nc.dram_tensor("l0ab", [1, 4 * 1024], F32, kind="ExternalInput")
    ctx.d_maskT = nc.dram_tensor("maskT", [128, 128], BF16, kind="ExternalInput")
    ctx.d_ident = nc.dram_tensor("ident", [128, 128], F32, kind="ExternalInput")
    ctx.xout = nc.dram_tensor("xout", [2, T, D], F32, kind="ExternalOutput")
    P = Prog(nc, st)
    setup_common(nc, st, ctx, P)
    prev_store = None
    for seq in range(2):
        exl = emit_load_x(P, ctx, seq, waits=[prev_store])
        out_evs = emit_l0a(P, ctx, seq, exl)
        prev_store = emit_store_x(P, ctx, seq, waits=out_evs)
    P.op("sync", lambda E: E.nop(), waits=[prev_store])
    P.replay()
    st.close()
    return nc


def _build_l0f():
    nc = bass.Bass("TRN2", target_bir_lowering=False)
    st = ExitStack()
    ctx = Ctx(nc)
    ctx.d_x0 = nc.dram_tensor("x0", [2, T, D], F32, kind="ExternalInput")
    ctx.d_wqkv = nc.dram_tensor("wqkv", [6, 1024, 3072], F32, kind="ExternalInput")
    ctx.d_wo0 = nc.dram_tensor("wo0", [6, 1024, 1024], F32, kind="ExternalInput")
    ctx.d_w1 = nc.dram_tensor("w1", [6, 1024, 5460], F32, kind="ExternalInput")
    ctx.d_b1 = nc.dram_tensor("b1", [6, 5460], F32, kind="ExternalInput")
    ctx.d_w2 = nc.dram_tensor("w2", [6, 2730, 1024], F32, kind="ExternalInput")
    ctx.d_b2 = nc.dram_tensor("b2", [6, 1024], F32, kind="ExternalInput")
    ctx.d_lng = nc.dram_tensor("lng", [6, 128, 1024], F32, kind="ExternalInput")
    ctx.d_lnb = nc.dram_tensor("lnb", [6, 128, 1024], F32, kind="ExternalInput")
    ctx.d_qs4 = nc.dram_tensor("qs4", [6, 4, 128, 64], F32, kind="ExternalInput")
    ctx.d_rotab = nc.dram_tensor("rotab", [4, 128, T // 128, 64], F32, kind="ExternalInput")
    ctx.d_lmask = nc.dram_tensor("lmask", [128, 384], F32, kind="ExternalInput")
    ctx.d_ident = nc.dram_tensor("ident", [128, 128], F32, kind="ExternalInput")
    ctx.xout = nc.dram_tensor("xout", [2, T, D], F32, kind="ExternalOutput")
    xs1 = nc.dram_tensor("xs1", [2, T, D], F32)
    xs2 = nc.dram_tensor("xs2", [2, T, D], F32)
    P = Prog(nc, st)
    setup_common(nc, st, ctx, P, need_xtm=False)
    prev = {0: None, 1: None}
    for seq in range(2):
        for li in range(6):
            ctx.d_xcur = ctx.d_x0 if li == 0 else (xs1 if li % 2 == 1 else xs2)
            ctx.d_xnxt = xs1 if li % 2 == 0 else xs2
            evs = emit_l0f_layer(P, ctx, seq, li, final_ln=(li == 5), prev_final=prev[seq])
            prev[seq] = evs
    P.op("sync", lambda E: E.nop(), waits=[prev[0][-1], prev[1][-1]])
    P.replay()
    st.close()
    return nc


def _build_l1a():
    nc = bass.Bass("TRN2", target_bir_lowering=False)
    st = ExitStack()
    ctx = Ctx(nc)
    ctx.xin = nc.dram_tensor("xin", [2, T, D], F32, kind="ExternalInput")
    ctx.d_wq = nc.dram_tensor("wq", [1024, 1024], F32, kind="ExternalInput")
    ctx.d_wk = nc.dram_tensor("wk", [1024, 1024], F32, kind="ExternalInput")
    ctx.d_wv = nc.dram_tensor("wv", [1024, 1024], F32, kind="ExternalInput")
    ctx.d_bqkv = nc.dram_tensor("bqkv", [1, 3 * 1024], F32, kind="ExternalInput")
    ctx.d_ident = nc.dram_tensor("ident", [128, 128], F32, kind="ExternalInput")
    ctx.xout = nc.dram_tensor("xout", [2, T, D], F32, kind="ExternalOutput")
    P = Prog(nc, st)
    setup_common(nc, st, ctx, P)
    prev_store = None
    for seq in range(2):
        exl = emit_load_x(P, ctx, seq, waits=[prev_store])
        out_evs = emit_l1a(P, ctx, seq, exl)
        prev_store = emit_store_x(P, ctx, seq, waits=out_evs)
    P.op("sync", lambda E: E.nop(), waits=[prev_store])
    P.replay()
    st.close()
    return nc


def _build_l1f():
    nc = bass.Bass("TRN2", target_bir_lowering=False)
    st = ExitStack()
    ctx = Ctx(nc)
    ctx.xin = nc.dram_tensor("xin", [2, T, D], F32, kind="ExternalInput")
    ctx.d_w1 = nc.dram_tensor("w1", [1024, 4096], F32, kind="ExternalInput")
    ctx.d_b1r = nc.dram_tensor("b1r", [1, 4096], F32, kind="ExternalInput")
    ctx.d_w2 = nc.dram_tensor("w2", [4096, 1024], F32, kind="ExternalInput")
    ctx.d_b2r = nc.dram_tensor("b2r", [1, 1024], F32, kind="ExternalInput")
    ctx.d_ident = nc.dram_tensor("ident", [128, 128], F32, kind="ExternalInput")
    ctx.xout = nc.dram_tensor("xout", [2, T, D], F32, kind="ExternalOutput")
    P = Prog(nc, st)
    setup_common(nc, st, ctx, P)
    prev_store = None
    for seq in range(2):
        exl = emit_load_x(P, ctx, seq, waits=[prev_store])
        out_evs = emit_l1f(P, ctx, seq, exl)
        prev_store = emit_store_x(P, ctx, seq, waits=out_evs)
    P.op("sync", lambda E: E.nop(), waits=[prev_store])
    P.replay()
    st.close()
    return nc


def _run(nc, shared, xs, xkey="xin"):
    in_maps = [dict(shared, **{xkey: x}) for x in xs]
    res = run_bass_kernel_spmd(nc, in_maps, core_ids=list(range(N_CORES)))
    return [r["xout"] for r in res.results]


def kernel_unfused(**inputs):
    i = inputs
    ident = np.eye(128, dtype=np.float32)
    x = np.ascontiguousarray(i["L0_attn_x"])
    xs = [np.ascontiguousarray(x[2 * c:2 * c + 2]) for c in range(N_CORES)]

    wstack = np.ascontiguousarray(np.stack([
        i["L0_attn_Wq"].T, i["L0_attn_Wk"].T, i["L0_attn_Wv"].T, i["L0_attn_Wo"].T]))
    l0ab = np.ascontiguousarray(np.concatenate(
        [i["L0_attn_bq"], i["L0_attn_bk"], i["L0_attn_bv"], i["L0_attn_bo"]])[None, :])
    maskT = np.triu(np.ones((128, 128))).astype(ml_dtypes.bfloat16)
    shared = {"l0aw": wstack, "l0ab": l0ab, "maskT": maskT, "ident": ident}
    xs = _run(_build_l0a(), shared, xs)

    shared = {
        "wqkv": np.ascontiguousarray(i["L0_ffn_Wqkv"]),
        "wo0": np.ascontiguousarray(i["L0_ffn_Wo"]),
        "w1": np.ascontiguousarray(i["L0_ffn_W1"]),
        "b1": np.ascontiguousarray(i["L0_ffn_b1"]),
        "w2": np.ascontiguousarray(i["L0_ffn_W2"]),
        "b2": np.ascontiguousarray(i["L0_ffn_b2"]),
        "lng": np.ascontiguousarray(np.tile(np.asarray(i["L0_ffn_ln_g"])[:, None, :], (1, 128, 1))),
        "lnb": np.ascontiguousarray(np.tile(np.asarray(i["L0_ffn_ln_b"])[:, None, :], (1, 128, 1))),
        "qs4": host_qs4(np.asarray(i["L0_ffn_q_scale"]), np.asarray(i["L0_ffn_k_scale"])),
        "rotab": host_rotab(T),
        "lmask": host_lmask(),
        "ident": ident,
    }
    xs = _run(_build_l0f(), shared, xs, xkey="x0")

    shared = {
        "wq": np.ascontiguousarray(i["L1_attn_Wq"]),
        "wk": np.ascontiguousarray(i["L1_attn_Wk"]),
        "wv": np.ascontiguousarray(i["L1_attn_Wv"]),
        "bqkv": np.ascontiguousarray(np.concatenate(
            [i["L1_attn_bq"], i["L1_attn_bk"], i["L1_attn_bv"]])[None, :]),
        "ident": ident,
    }
    xs = _run(_build_l1a(), shared, xs)

    shared = {
        "w1": np.ascontiguousarray(i["L1_ffn_w1"]),
        "b1r": np.ascontiguousarray(np.asarray(i["L1_ffn_b1"])[None, :]),
        "w2": np.ascontiguousarray(i["L1_ffn_w2"]),
        "b2r": np.ascontiguousarray(np.asarray(i["L1_ffn_b2"])[None, :]),
        "ident": ident,
    }
    xs = _run(_build_l1f(), shared, xs)

    return np.concatenate(xs, axis=0).astype(np.float32)


def _build_fused():
    nc = bass.Bass("TRN2", target_bir_lowering=False)
    st = ExitStack()
    ctx = Ctx(nc)
    ctx.d_x0in = nc.dram_tensor("x0in", [2, T, D], F32, kind="ExternalInput")
    ctx.d_l0a_w = nc.dram_tensor("l0aw", [4, 1024, 1024], F32, kind="ExternalInput")
    ctx.d_l0a_b = nc.dram_tensor("l0ab", [1, 4 * 1024], F32, kind="ExternalInput")
    ctx.d_maskT = nc.dram_tensor("maskT", [128, 128], BF16, kind="ExternalInput")
    ctx.d_wqkv = nc.dram_tensor("wqkv", [6, 1024, 3072], F32, kind="ExternalInput")
    ctx.d_wo0 = nc.dram_tensor("wo0", [6, 1024, 1024], F32, kind="ExternalInput")
    ctx.d_w1 = nc.dram_tensor("w1", [6, 1024, 5460], F32, kind="ExternalInput")
    ctx.d_b1 = nc.dram_tensor("b1", [6, 5460], F32, kind="ExternalInput")
    ctx.d_w2 = nc.dram_tensor("w2", [6, 2730, 1024], F32, kind="ExternalInput")
    ctx.d_b2 = nc.dram_tensor("b2", [6, 1024], F32, kind="ExternalInput")
    ctx.d_lng = nc.dram_tensor("lng", [6, 128, 1024], F32, kind="ExternalInput")
    ctx.d_lnb = nc.dram_tensor("lnb", [6, 128, 1024], F32, kind="ExternalInput")
    ctx.d_qs4 = nc.dram_tensor("qs4", [6, 4, 128, 64], F32, kind="ExternalInput")
    ctx.d_rotab = nc.dram_tensor("rotab", [4, 128, T // 128, 64], F32, kind="ExternalInput")
    ctx.d_lmask = nc.dram_tensor("lmask", [128, 384], F32, kind="ExternalInput")
    ctx.d_wq = nc.dram_tensor("wq", [1024, 1024], F32, kind="ExternalInput")
    ctx.d_wk = nc.dram_tensor("wk", [1024, 1024], F32, kind="ExternalInput")
    ctx.d_wv = nc.dram_tensor("wv", [1024, 1024], F32, kind="ExternalInput")
    ctx.d_bqkv = nc.dram_tensor("bqkv", [1, 3 * 1024], F32, kind="ExternalInput")
    ctx.d_w1b = nc.dram_tensor("w1b", [1024, 4096], F32, kind="ExternalInput")
    ctx.d_b1rb = nc.dram_tensor("b1rb", [1, 4096], F32, kind="ExternalInput")
    ctx.d_w2b = nc.dram_tensor("w2b", [4096, 1024], F32, kind="ExternalInput")
    ctx.d_b2rb = nc.dram_tensor("b2rb", [1, 1024], F32, kind="ExternalInput")
    ctx.d_ident = nc.dram_tensor("ident", [128, 128], F32, kind="ExternalInput")
    real_out = nc.dram_tensor("xout", [2, T, D], F32, kind="ExternalOutput")
    xa = nc.dram_tensor("xa", [2, T, D], F32)
    xs1 = nc.dram_tensor("xs1", [2, T, D], F32)
    xs2 = nc.dram_tensor("xs2", [2, T, D], F32)
    xb = nc.dram_tensor("xb", [2, T, D], F32)

    P = Prog(nc, st)
    setup_common(nc, st, ctx, P, need_xtm=False)

    ctx.x_tm = ctx.alloc("x_tm", [128, NTC * D], F32)
    ctx.xin = ctx.d_x0in
    ctx.xout = xa
    prev_store = None
    for seq in range(2):
        exl = emit_load_x(P, ctx, seq, waits=[prev_store])
        out_evs = emit_l0a(P, ctx, seq, exl)
        prev_store = emit_store_x(P, ctx, seq, waits=out_evs)
    ctx.free(ctx.x_tm)
    ctx.x_tm = None
    ctx.new_phase(P)

    ctx.d_x0 = xa
    ctx.xout = xb
    prev = {0: None, 1: None}
    for seq in range(2):
        for li in range(6):
            ctx.d_xcur = xa if li == 0 else (xs1 if li % 2 == 1 else xs2)
            ctx.d_xnxt = xs1 if li % 2 == 0 else xs2
            evs = emit_l0f_layer(P, ctx, seq, li, final_ln=(li == 5), prev_final=prev[seq])
            prev[seq] = evs

    ctx.x_tm = ctx.alloc("x_tm2", [128, NTC * D], F32)
    ctx.xin = xb
    ctx.xout = real_out
    ctx.d_w1, ctx.d_b1r, ctx.d_w2, ctx.d_b2r = ctx.d_w1b, ctx.d_b1rb, ctx.d_w2b, ctx.d_b2rb
    prev_store = None
    for seq in range(2):
        exl = emit_load_x(P, ctx, seq, waits=[prev_store, prev[seq]], src=xb)
        evs_a = emit_l1a(P, ctx, seq, exl)
        evs_f = emit_l1f(P, ctx, seq, evs_a)
        prev_store = emit_store_x(P, ctx, seq, waits=evs_f, dst=real_out)
    P.op("sync", lambda E: E.nop(), waits=[prev_store])
    P.replay()
    st.close()
    return nc


def kernel(**inputs):
    i = inputs
    ident = np.eye(128, dtype=np.float32)
    x = np.ascontiguousarray(i["L0_attn_x"])
    xs = [np.ascontiguousarray(x[2 * c:2 * c + 2]) for c in range(N_CORES)]
    wstack = np.ascontiguousarray(np.stack([
        i["L0_attn_Wq"].T, i["L0_attn_Wk"].T, i["L0_attn_Wv"].T, i["L0_attn_Wo"].T]))
    shared = {
        "l0aw": wstack,
        "l0ab": np.ascontiguousarray(np.concatenate(
            [i["L0_attn_bq"], i["L0_attn_bk"], i["L0_attn_bv"], i["L0_attn_bo"]])[None, :]),
        "maskT": np.triu(np.ones((128, 128))).astype(ml_dtypes.bfloat16),
        "wqkv": np.ascontiguousarray(i["L0_ffn_Wqkv"]),
        "wo0": np.ascontiguousarray(i["L0_ffn_Wo"]),
        "w1": np.ascontiguousarray(i["L0_ffn_W1"]),
        "b1": np.ascontiguousarray(i["L0_ffn_b1"]),
        "w2": np.ascontiguousarray(i["L0_ffn_W2"]),
        "b2": np.ascontiguousarray(i["L0_ffn_b2"]),
        "lng": np.ascontiguousarray(np.tile(np.asarray(i["L0_ffn_ln_g"])[:, None, :], (1, 128, 1))),
        "lnb": np.ascontiguousarray(np.tile(np.asarray(i["L0_ffn_ln_b"])[:, None, :], (1, 128, 1))),
        "qs4": host_qs4(np.asarray(i["L0_ffn_q_scale"]), np.asarray(i["L0_ffn_k_scale"])),
        "rotab": host_rotab(T),
        "lmask": host_lmask(),
        "wq": np.ascontiguousarray(i["L1_attn_Wq"]),
        "wk": np.ascontiguousarray(i["L1_attn_Wk"]),
        "wv": np.ascontiguousarray(i["L1_attn_Wv"]),
        "bqkv": np.ascontiguousarray(np.concatenate(
            [i["L1_attn_bq"], i["L1_attn_bk"], i["L1_attn_bv"]])[None, :]),
        "w1b": np.ascontiguousarray(i["L1_ffn_w1"]),
        "b1rb": np.ascontiguousarray(np.asarray(i["L1_ffn_b1"])[None, :]),
        "w2b": np.ascontiguousarray(i["L1_ffn_w2"]),
        "b2rb": np.ascontiguousarray(np.asarray(i["L1_ffn_b2"])[None, :]),
        "ident": ident,
    }
    nc = _build_fused()
    in_maps = [dict(shared, x0in=s) for s in xs]
    res = run_bass_kernel_spmd(nc, in_maps, core_ids=list(range(N_CORES)))
    return np.concatenate([r["xout"] for r in res.results], axis=0).astype(np.float32)


# version 15
# speedup vs baseline: 1.1556x; 1.1556x over previous
import numpy as np
import ml_dtypes
from contextlib import ExitStack
import concourse.bass as bass
import concourse.mybir as mybir
from concourse.bass_utils import run_bass_kernel_spmd

"""Core framework + slot emitters for the mk_seq_postln_d2 megakernel.

Layout conventions:
  - token-major (tm): [tokens(part) x features(free)]; residual x lives here, f32.
    x_tm sbuf tensor [128, NTC*D]: t-chunk tc at cols tc*D.
  - feature-major (fm): [features(part) x tokens(free)] bf16, matmul operand.
    fm tensor [128, n_fc*Tpad]: f-chunk fc at cols fc*Tpad (+pad lead zero cols).
  - matmul: out[M,N] = lhsT.T @ rhs, lhsT [K<=128, M<=128], rhs [K, N<=512].
    * fm emission: lhsT = W[fi,fo] natural, rhs = act_fm -> out [fo, t]
    * tm emission: lhsT = act_fm[fi, tchunk], rhs = W[fi, fo] -> out [t, fo]
  - biases live as [1, len] f32 rows; injected into PSUM with K=1 matmuls:
    * fm emission: lhsT = b_row[0:1, sl], rhs = ones_row[0:1, 0:N]
    * tm emission: lhsT = ones_row[0:1, 0:128], rhs = b_row[0:1, sl]

Sync model: every instruction increments its engine's progress semaphore
(+1 compute, +16 DMA); Ev(sem,count) events; consumers wait explicitly.
Same-engine RAW REQUIRES an event wait (hardware pipelines do not interlock).
"""

F32 = mybir.dt.float32
BF16 = mybir.dt.bfloat16
AF = mybir.ActivationFunctionType
ALU = mybir.AluOpType
AX = mybir.AxisListType

D = 1024
T = 2048
NTC = T // 128
NFC = D // 128

ENGINES = ("tensor", "vector", "scalar", "gpsimd", "sync")


class Ev:
    __slots__ = ("sem", "count")

    def __init__(self, sem, count):
        self.sem = sem
        self.count = count


class Prog:
    def __init__(self, nc, st: ExitStack):
        self.nc = nc
        self.ops = {e: [] for e in ENGINES}
        self.sems = {}
        self.counts = {}
        for e in ENGINES:
            s = st.enter_context(nc.semaphore(f"sp_{e}"))
            self.sems[e] = s
            self.counts[e] = 0
        self.dsems = {}
        self.dcounts = {}
        for e in ("sync", "scalar", "gpsimd", "vector"):
            self.dsems[e] = [st.enter_context(nc.semaphore(f"sd_{e}{j}")) for j in range(4)]
            self.dcounts[e] = [0, 0, 0, 0]
            setattr(self, "_rr_" + e, 0)

    @staticmethod
    def _flat(waits):
        out = []
        for w in waits:
            if w is None:
                continue
            if isinstance(w, (list, tuple)):
                out.extend(x for x in w if x is not None)
            else:
                out.append(w)
        return out

    def op(self, eng, fn, waits=()):
        for w in self._flat(waits):
            self.ops[eng].append(("w", w.sem, w.count))
        sem = self.sems[eng]
        self.counts[eng] += 1
        self.ops[eng].append(("op", fn, sem, 1))
        return Ev(sem, self.counts[eng])

    def dma(self, eng, out_ap, in_ap, waits=(), **kw):
        for w in self._flat(waits):
            self.ops[eng].append(("w", w.sem, w.count))
        j = getattr(self, "_rr_" + eng) % 4
        setattr(self, "_rr_" + eng, j + 1)
        sem = self.dsems[eng][j]
        if self.dcounts[eng][j] > 0:
            self.ops[eng].append(("w", sem, self.dcounts[eng][j]))
        self.dcounts[eng][j] += 16
        self.ops[eng].append(("op", lambda E: E.dma_start(out_ap, in_ap, **kw), sem, 16))
        return Ev(sem, self.dcounts[eng][j])

    def barrier_ev(self, eng):
        return Ev(self.sems[eng], self.counts[eng])

    def full_barrier(self):
        evs = [Ev(self.sems[e], self.counts[e]) for e in ENGINES if self.counts[e] > 0]
        for e in self.dsems:
            for j in range(4):
                if self.dcounts[e][j] > 0:
                    evs.append(Ev(self.dsems[e][j], self.dcounts[e][j]))
        return evs

    def replay(self):
        nc = self.nc
        with nc.Block() as block:
            for eng in ENGINES:
                seq = self.ops[eng]

                def make(seq):
                    def body(E):
                        waited = {}
                        for item in seq:
                            if item[0] == "w":
                                _, sem, count = item
                                key = id(sem)
                                if waited.get(key, -1) < count:
                                    E.wait_ge(sem, count)
                                    waited[key] = count
                            else:
                                _, fn, sem, inc = item
                                fn(E).then_inc(sem, inc)
                    return body

                getattr(block, eng)(make(seq))


class Ctx:

    def __init__(self, nc):
        self.nc = nc
        self.pb = []
        self.pb_rr = 0
        self.pbt = []
        self.pbt_rr = 0
        self.pb_ev = {}
        self.gates = {}
        self.phase_list = []
        self._cms = {}

    def alloc(self, name, shape, dtype):
        self._uid = getattr(self, "_uid", 0) + 1
        cm = self.nc.sbuf_tensor(f"{name}_{self._uid}", shape, dtype)
        h = cm.__enter__()
        self._cms[id(h)] = cm
        return h

    def free(self, *hs):
        for h in reversed(hs):
            self._cms.pop(id(h)).__exit__(None, None, None)

    def bank(self):
        b = self.pb[self.pb_rr % len(self.pb)]
        self.pb_rr += 1
        return b, self.pb_ev.get(id(b))

    def bankt(self):
        b = self.pbt[self.pbt_rr % len(self.pbt)]
        self.pbt_rr += 1
        return b, self.pb_ev.get(id(b))

    def gate(self, name):
        return self.gates.get(name, self.phase_list)

    def new_phase(self, P):
        self.gates = {}
        self.phase_list = P.full_barrier()

    def set_gate(self, name, ev):
        self.gates[name] = ev


def ceil_div(a, b):
    return (a + b - 1) // b


def _copy(E, out, in_):
    if hasattr(E, "tensor_copy"):
        return E.tensor_copy(out, in_)
    return E.copy(out, in_)



def emit_load_x(P, ctx, seq, waits=(), src=None):
    xin = src if src is not None else ctx.xin
    ap = xin[seq].rearrange("(tc p) d -> p tc d", p=128)
    dst = ctx.x_tm[:].rearrange("p (tc d) -> p tc d", tc=NTC)
    return P.dma("scalar", dst, ap, waits=list(waits))


def emit_store_x(P, ctx, seq, waits, dst=None):
    xout = dst if dst is not None else ctx.xout
    ap = xout[seq].rearrange("(tc p) d -> p tc d", p=128)
    src = ctx.x_tm[:].rearrange("p (tc d) -> p tc d", tc=NTC)
    return P.dma("scalar", ap, src, waits=list(waits))


def emit_tm_to_fm(P, ctx, x_ready, dst_fm, Tpad, pad=0, evac_eng="scalar"):
    x_tm = ctx.x_tm
    ident = ctx.ident_f32
    ev = None
    for fc in range(NFC):
        for tg in range(NTC // 4):
            bank, bfree = ctx.bank()
            mev = None
            for j, tc in enumerate(range(tg * 4, tg * 4 + 4)):
                def mk(fc=fc, tc=tc, j=j, bank=bank, x_tm=x_tm, ident=ident):
                    def f(E):
                        return E.transpose(
                            bank[:, j * 128:(j + 1) * 128],
                            x_tm[:, tc * D + fc * 128: tc * D + fc * 128 + 128],
                            ident[:])
                    return f
                w = [x_ready, bfree, ctx.ident_ready] if j == 0 else []
                mev = P.op("tensor", mk(), waits=w)

            def mkev(fc=fc, tg=tg, bank=bank, dst_fm=dst_fm):
                def f(E):
                    return _copy(E,
                        dst_fm[:, fc * Tpad + pad + tg * 512: fc * Tpad + pad + tg * 512 + 512],
                        bank[:, 0:512])
                return f
            ev = P.op(evac_eng, mkev(), waits=[mev])
            ctx.pb_ev[id(bank)] = ev
    return ev


def emit_ln_inplace(P, ctx, tc, dep_ev, eps=1e-5):
    x = ctx.x_tm
    sl = x[:, tc * D:(tc + 1) * D]
    st = ctx.stat[:, tc * 8:(tc + 1) * 8]
    s1, s2, m, w, u, r, ri, s2b = (st[:, i:i + 1] for i in range(8))
    dump = ctx.dump
    e1 = P.op("vector", lambda E, o=s1, i=sl: E.reduce_sum(o, i, AX.X), waits=[dep_ev])
    e2a = P.op("scalar", lambda E, o=dump, i=sl, a=s2: E.activation(o[:, 0:512], i[:, 0:512], AF.Square, accum_out=a),
               waits=[dep_ev, ctx.gate("dump")])
    e2b = P.op("scalar", lambda E, o=dump, i=sl, a=s2b: E.activation(o[:, 0:512], i[:, 512:1024], AF.Square, accum_out=a),
               waits=[dep_ev, e2a])
    ctx.set_gate("dump", e2b)
    e3 = P.op("gpsimd", lambda E, o=m, i=s1: E.tensor_scalar(o, i, 1.0 / D, None, ALU.mult), waits=[e1])
    e4a = P.op("gpsimd", lambda E, o=w, a=s2, b=s2b: E.tensor_add(o, a, b), waits=[e2a, e2b])
    e4 = P.op("gpsimd", lambda E, o=w: E.tensor_scalar(o, o, 1.0 / D, None, ALU.mult), waits=[e4a])
    e5 = P.op("gpsimd", lambda E, o=u, i=m: E.tensor_mul(o, i, i), waits=[e3])
    e6 = P.op("gpsimd", lambda E, o=w, a=u: E.tensor_sub(o, o, a), waits=[e4, e5])
    e7 = P.op("scalar", lambda E, o=r, i=w: E.activation(o, i, AF.Sqrt, bias=ctx.eps_col[:, 0:1]),
              waits=[e6, ctx.eps_ready])
    e8 = P.op("vector", lambda E, o=ri, i=r: E.reciprocal(o, i), waits=[e7])
    e9 = P.op("vector", lambda E, o=sl, a=m, b=ri: E.tensor_scalar(o, o, a, b, ALU.subtract, ALU.mult),
              waits=[e8, e3, dep_ev])
    return e9


def mm(P, ctx, bank, lhsT, rhs, n, start, stop, waits=()):
    def f(E, bank=bank, lhsT=lhsT, rhs=rhs, n=n, start=start, stop=stop):
        return E.matmul(bank[:, 0:n], lhsT, rhs, start=start, stop=stop)
    return P.op("tensor", f, waits=waits)



def emit_l1f(P, ctx, seq, x_ready):
    x_tm = ctx.x_tm
    x_fm = ctx.alloc("x_fm", [128, NFC * T], BF16)
    h_fm = ctx.alloc("h_fm", [128, 8 * T], BF16)
    dump = ctx.dump = ctx.alloc("dump", [128, 512], F32)
    w1_sb = ctx.alloc("w1_sb", [128, 8 * 1024], BF16)
    w2_sb = ctx.alloc("w2_sb", [128, 8 * 1024], BF16)
    b1_row = ctx.alloc("b1_row", [1, 4096], BF16)
    b2_row = ctx.alloc("b2_row", [1, 1024], BF16)

    fm_done = emit_tm_to_fm(P, ctx, x_ready, x_fm, T)
    eb1 = P.dma("gpsimd", b1_row[0:1, 0:4096], ctx.d_b1r[:, :], waits=[ctx.gate("rows")])
    eb2 = P.dma("gpsimd", b2_row[0:1, 0:D], ctx.d_b2r[:, :])

    add_evs = []
    for qu in range(4):
        ew1 = P.dma("gpsimd",
                    w1_sb[:].rearrange("p (ci n) -> p ci n", ci=8),
                    ctx.d_w1[:, qu * 1024:(qu + 1) * 1024].rearrange("(ci p) n -> p ci n", p=128),
                    waits=[ctx.gate("w1_free")])
        ew2 = P.dma("gpsimd",
                    w2_sb[:].rearrange("p (ci n) -> p ci n", ci=8),
                    ctx.d_w2[qu * 1024:(qu + 1) * 1024, :].rearrange("(ci p) n -> p ci n", p=128),
                    waits=[ctx.gate("w2_free")])

        h_last = None
        for fc in range(8):
            for tg in range(T // 512):
                bank, bfree = ctx.bank()
                for fi in range(8):
                    w = [ew1, fm_done, bfree, ctx.gate("h_fm")] if fi == 0 else []
                    mm(P, ctx, bank,
                       w1_sb[:, fi * 1024 + fc * 128: fi * 1024 + fc * 128 + 128],
                       x_fm[:, fi * T + tg * 512: fi * T + tg * 512 + 512],
                       512, fi == 0, False, waits=w)
                mev = mm(P, ctx, bank,
                         b1_row[0:1, qu * 1024 + fc * 128: qu * 1024 + fc * 128 + 128],
                         ctx.ones_rowb[0:1, 0:512], 512, False, True,
                         waits=[eb1, ctx.onesb_ready])

                def mkev(fc=fc, tg=tg, bank=bank, h_fm=h_fm):
                    def f(E):
                        return E.activation(
                            h_fm[:, fc * T + tg * 512: fc * T + tg * 512 + 512],
                            bank[:, 0:512], AF.Relu)
                    return f
                h_last = P.op("scalar", mkev(), waits=[mev])
                ctx.pb_ev[id(bank)] = h_last
        ctx.set_gate("w1_free", P.barrier_ev("tensor"))

        for tc in range(NTC):
            for no in range(2):
                bank, bfree = ctx.bank()
                for fi in range(8):
                    w = [ew2, h_last, bfree] if fi == 0 else []
                    mm(P, ctx, bank,
                       h_fm[:, fi * T + tc * 128: fi * T + tc * 128 + 128],
                       w2_sb[:, fi * 1024 + no * 512: fi * 1024 + no * 512 + 512],
                       512, fi == 0, (fi == 7) and qu > 0, waits=w)
                if qu == 0:
                    mev = mm(P, ctx, bank,
                             ctx.ones_rowb[0:1, 0:128],
                             b2_row[0:1, no * 512: no * 512 + 512],
                             512, False, True, waits=[eb2])
                else:
                    mev = P.barrier_ev("tensor")

                def mkadd(tc=tc, no=no, bank=bank, x_tm=x_tm):
                    def f(E):
                        sl = x_tm[:, tc * D + no * 512: tc * D + no * 512 + 512]
                        return E.tensor_tensor(sl, sl, bank[:, 0:512], ALU.add)
                    return f
                prev_add = add_evs[qu * 32 + tc * 2 + no - 32] if qu > 0 else None
                aev = P.op("vector", mkadd(), waits=[mev, prev_add])
                ctx.pb_ev[id(bank)] = aev
                add_evs.append(aev)
        ctx.set_gate("w2_free", P.barrier_ev("tensor"))
        ctx.set_gate("h_fm", P.barrier_ev("tensor"))

    out_evs = []
    for tc in range(NTC):
        dep = add_evs[3 * 32 + tc * 2 + 1]
        out_evs.append(emit_ln_inplace(P, ctx, tc, dep))
    ctx.free(x_fm, h_fm, dump, w1_sb, w2_sb, b1_row, b2_row)
    ctx.new_phase(P)
    return out_evs



def emit_l1a(P, ctx, seq, x_ready):
    x_tm = ctx.x_tm
    q_fm = ctx.alloc("q_fm", [128, NFC * T], BF16)
    k_fm = ctx.alloc("k_fm", [128, NFC * T], BF16)
    v_tm = ctx.alloc("v_tm", [128, NTC * D], BF16)
    x_fm = ctx.alloc("x_fm", [128, NFC * T], BF16)
    w_sb = ctx.alloc("w_sb", [128, 8 * 512], BF16)
    b_row = ctx.alloc("b_row", [1, 1024], BF16)

    fm_done = emit_tm_to_fm(P, ctx, x_ready, x_fm, T)

    projs = [("q", ctx.d_wq, q_fm, "fm", 0), ("k", ctx.d_wk, k_fm, "fm", 1),
             ("v", ctx.d_wv, v_tm, "tm", 2)]
    for name, dw, dst, kind, bi in projs:
        ebr = P.dma("gpsimd", b_row[0:1, 0:D], ctx.d_bqkv[:, bi * D:(bi + 1) * D],
                    waits=[ctx.gate("w1_free"), ctx.gate("rows")])
        for foh in range(2):
            ew = P.dma("gpsimd", w_sb[:].rearrange("p (ci n) -> p ci n", ci=8),
                       dw[:, foh * 512:(foh + 1) * 512].rearrange("(ci p) n -> p ci n", p=128),
                       waits=[ctx.gate("w1_free")])
            if kind == "fm":
                for fo in range(foh * 4, foh * 4 + 4):
                    for tg in range(T // 512):
                        bank, bfree = ctx.bank()
                        for fi in range(8):
                            w = [ew, fm_done, bfree] if fi == 0 else []
                            mm(P, ctx, bank,
                               w_sb[:, fi * 512 + (fo - foh * 4) * 128: fi * 512 + (fo - foh * 4) * 128 + 128],
                               x_fm[:, fi * T + tg * 512: fi * T + tg * 512 + 512],
                               512, fi == 0, False, waits=w)
                        mev = mm(P, ctx, bank,
                                 b_row[0:1, fo * 128: fo * 128 + 128],
                                 ctx.ones_rowb[0:1, 0:512], 512, False, True,
                                 waits=[ebr, ctx.onesb_ready])

                        def mkev(fo=fo, tg=tg, bank=bank, dst=dst, name=name):
                            def f(E):
                                sl = dst[:, fo * T + tg * 512: fo * T + tg * 512 + 512]
                                if name == "q":
                                    return E.tensor_scalar(sl, bank[:, 0:512], 1.0 / 32, None, ALU.mult)
                                return E.tensor_copy(sl, bank[:, 0:512])
                            return f
                        ev = P.op("vector", mkev(), waits=[mev])
                        ctx.pb_ev[id(bank)] = ev
            else:
                no = foh
                for tc in range(NTC):
                    bank, bfree = ctx.bank()
                    for fi in range(8):
                        w = [ew, fm_done, bfree] if fi == 0 else []
                        mm(P, ctx, bank,
                           x_fm[:, fi * T + tc * 128: fi * T + tc * 128 + 128],
                           w_sb[:, fi * 512: fi * 512 + 512],
                           512, fi == 0, False, waits=w)
                    mev = mm(P, ctx, bank,
                             ctx.ones_rowb[0:1, 0:128],
                             b_row[0:1, no * 512: no * 512 + 512],
                             512, False, True, waits=[ebr, ctx.onesb_ready])

                    def mkev(tc=tc, no=no, bank=bank, dst=dst):
                        def f(E):
                            return _copy(E, dst[:, tc * D + no * 512: tc * D + no * 512 + 512],
                                         bank[:, 0:512])
                        return f
                    ev = P.op("scalar", mkev(), waits=[mev])
                    ctx.pb_ev[id(bank)] = ev
            ctx.set_gate("w1_free", P.barrier_ev("tensor"))
    proj_done = P.barrier_ev("tensor")
    ctx.free(x_fm, w_sb, b_row)
    o_sb = ctx.alloc("o_sb", [128, 2 * D], F32)
    p_sb = ctx.alloc("p_sb", [128, 2 * 512], F32)
    pt_sb = ctx.alloc("pt_sb", [128, 2 * 512], BF16)
    dump = ctx.dump = ctx.alloc("dump", [128, 512], F32)

    fl = ctx.stat
    out_evs = []
    state = {}

    def flash_step(qt, kb):
        st8 = fl[:, qt % 8 * 8: qt % 8 * 8 + 8]
        mneg = [st8[:, 0:1], st8[:, 1:2]]
        mpos_old = st8[:, 2:3]
        alpha = st8[:, 3:4]
        lsum = st8[:, 4:5]
        rs = st8[:, 5:6]
        qp = qt % 2
        ps = p_sb[:, qp * 512:(qp + 1) * 512]
        pts = pt_sb[:, qp * 512:(qp + 1) * 512]
        os_ = o_sb[:, qp * 1024:(qp + 1) * 1024]
        stv = state[qt]
        sbank, sfree = ctx.bank()
        for fi in range(8):
            w = [proj_done, sfree] if fi == 0 else []
            mm(P, ctx, sbank,
               q_fm[:, fi * T + qt * 128: fi * T + qt * 128 + 128],
               k_fm[:, fi * T + kb * 512: fi * T + kb * 512 + 512],
               512, fi == 0, fi == 7, waits=w)
        smev = P.barrier_ev("tensor")
        e_t = P.op("vector", lambda E, s=sbank, o=rs: E.reduce_max(o, s[:, 0:512], AX.X, negate=True),
                   waits=[smev, stv.get("small")])
        if kb == 0:
            e_m = P.op("vector", lambda E, o=mneg[0], i=rs: E.tensor_copy(o, i), waits=[e_t])
            cur = mneg[0]
            e_al = None
        else:
            old = mneg[(kb - 1) % 2]
            cur = mneg[kb % 2]
            e_mp = P.op("gpsimd", lambda E, o=mpos_old, i=old: E.tensor_scalar(o, i, -1.0, None, ALU.mult),
                        waits=[stv.get("small"), stv.get("em"), stv.get("eal")])
            e_m = P.op("vector", lambda E, o=cur, a=old, b=rs: E.tensor_tensor(o, a, b, ALU.min),
                       waits=[e_t])
            e_al = P.op("scalar", lambda E, o=alpha, i=cur, b=mpos_old: E.activation(o, i, AF.Exp, bias=b[:, 0:1]),
                        waits=[e_m, e_mp])
        e_p = P.op("scalar", lambda E, s=sbank, b=cur, p=ps, a=rs: E.activation(
            p[:, 0:512], s[:, 0:512], AF.Exp, bias=b[:, 0:1], accum_out=a),
            waits=[e_m, e_t, stv.get("tev")])
        ctx.pb_ev[id(sbank)] = e_p
        if kb == 0:
            e_l = P.op("gpsimd", lambda E, o=lsum, i=rs: E.tensor_copy(o, i), waits=[e_p])
        else:
            e_l = P.op("vector", lambda E, o=lsum, a=alpha, b=rs: E.scalar_tensor_tensor(
                o, o, a[:, 0:1], b, ALU.mult, ALU.add), waits=[e_p, e_al, stv.get("el")])
        tbank, tfree = ctx.bank()
        tev = None
        for j in range(4):
            w = [e_p, tfree, ctx.ident_ready] if j == 0 else []
            tev = P.op("tensor", lambda E, j=j, b=tbank, p=ps: E.transpose(
                b[:, j * 128:(j + 1) * 128], p[:, j * 128:(j + 1) * 128],
                ctx.ident_f32[:]), waits=w)
        e_pt = P.op("scalar", lambda E, b=tbank, o=pts: E.copy(o[:, 0:512], b[:, 0:512]),
                    waits=[tev, stv.get("pvm")])
        ctx.pb_ev[id(tbank)] = e_pt
        o_new = [None, None]
        pvml = None
        for no in range(2):
            pvb, pvfree = ctx.bank()
            for j in range(4):
                w = [e_pt, pvfree] if j == 0 else []
                mm(P, ctx, pvb,
                   pts[:, j * 128: j * 128 + 128],
                   v_tm[:, (kb * 4 + j) * D + no * 512: (kb * 4 + j) * D + no * 512 + 512],
                   512, j == 0, j == 3, waits=w)
            pvml = P.barrier_ev("tensor")
            if kb == 0:
                e_o = P.op("vector", lambda E, b=pvb, no=no, o=os_: E.tensor_copy(
                    o[:, no * 512:(no + 1) * 512], b[:, 0:512]), waits=[pvml, stv.get("eo%d" % no)])
            else:
                e_o = P.op("vector", lambda E, b=pvb, no=no, o=os_, a=alpha: E.scalar_tensor_tensor(
                    o[:, no * 512:(no + 1) * 512], o[:, no * 512:(no + 1) * 512],
                    a[:, 0:1], b[:, 0:512], ALU.mult, ALU.add),
                    waits=[pvml, e_al, stv.get("eo%d" % no)])
            ctx.pb_ev[id(pvb)] = e_o
            o_new[no] = e_o
        state[qt] = {"small": P.barrier_ev("gpsimd"), "em": e_m, "eal": e_al,
                     "el": e_l, "tev": tev, "pvm": pvml,
                     "eo0": o_new[0], "eo1": o_new[1],
                     "lsum": lsum, "rs": rs, "os": os_}

    def flash_fin(qt):
        stv = state.pop(qt)
        st8 = fl[:, qt % 8 * 8: qt % 8 * 8 + 8]
        rcp = st8[:, 6:7]
        e_r = P.op("vector", lambda E, o=rcp, i=stv["lsum"]: E.reciprocal(o, i),
                   waits=[stv["el"], stv["eo0"], stv["eo1"]])
        e_x = P.op("vector", lambda E, qt=qt, o=stv["os"], r=rcp: E.scalar_tensor_tensor(
            ctx.x_tm[:, qt * D:(qt + 1) * D], o[:, 0:D], r[:, 0:1],
            ctx.x_tm[:, qt * D:(qt + 1) * D], ALU.mult, ALU.add), waits=[e_r])
        out_evs.append(emit_ln_inplace(P, ctx, qt, e_x))

    for qpair in range(NTC // 2):
        qa, qb = 2 * qpair, 2 * qpair + 1
        state[qa] = {}
        state[qb] = {}
        for kb in range(T // 512):
            flash_step(qa, kb)
            flash_step(qb, kb)
        flash_fin(qa)
        flash_fin(qb)
    ctx.free(q_fm, k_fm, v_tm, o_sb, p_sb, pt_sb, dump)
    ctx.new_phase(P)
    return out_evs


def setup_common(nc, st, ctx, P, need_xtm=True):
    ctx.x_tm = ctx.alloc("x_tm", [128, NTC * D], F32) if need_xtm else None
    ctx.ident_f32 = ctx.alloc("ident_sb", [128, 128], F32)
    ctx.stat = ctx.alloc("stat", [128, 128], F32)
    ctx.dump = None
    ctx.eps_col = ctx.alloc("eps_col", [128, 1], F32)
    for i in range(6):
        ctx.pb.append(st.enter_context(nc.psum_tensor("pb%d" % i, [128, 512], F32)))
    for i in range(2):
        ctx.pbt.append(st.enter_context(nc.psum_tensor("pbt%d" % i, [128, 1024], BF16)))
    ctx.ident_bf16 = ctx.alloc("ident_bf16", [128, 128], BF16)
    ctx.ones_col = ctx.alloc("ones_col", [128, 1], BF16)
    ctx.ident_ready = P.dma("sync", ctx.ident_f32[:], ctx.d_ident[:, :])
    if hasattr(ctx, "d_maskT"):
        ctx.maskT = ctx.alloc("maskTc", [128, 128], BF16)
        ctx.maskT_ready = P.dma("sync", ctx.maskT[:], ctx.d_maskT[:, :])
    ctx.identb_ready = P.op("vector", lambda E: E.tensor_copy(ctx.ident_bf16[:], ctx.ident_f32[:]),
                            waits=[ctx.ident_ready])
    ctx.ones_rowb = ctx.alloc("ones_rowb", [1, 512], BF16)
    ctx.onesb_ready = P.op("gpsimd", lambda E: E.memset(ctx.ones_rowb[:], 1.0))
    ctx.onesc_ready = P.op("gpsimd", lambda E: E.memset(ctx.ones_col[:], 1.0))
    ctx.eps_ready = P.op("gpsimd", lambda E: E.memset(ctx.eps_col[:], 1e-5))



def emit_l0a(P, ctx, seq, x_ready):
    x_tm = ctx.x_tm
    x_fm = ctx.alloc("x_fm", [128, NFC * 512], BF16)
    w_sb = ctx.alloc("w_sb", [128, 4 * 8 * 1024], BF16)
    b_row = ctx.alloc("b_row", [1, 4 * 1024], BF16)
    maskT = ctx.maskT
    qf = ctx.alloc("qf", [128, 8 * 512], BF16)
    kf = ctx.alloc("kf", [128, 8 * 512], BF16)
    vg = ctx.alloc("vg", [128, 4 * 1040], BF16)
    ktg = ctx.alloc("ktg", [128, 4 * 1024], BF16)
    szb = ctx.alloc("szb", [128, 8 * 65], BF16)
    szf = ctx.alloc("szf", [128, 8 * 65], F32)
    otm = ctx.alloc("otm", [128, 1024], BF16)
    aofc = ctx.alloc("aofc", [128, 8 * 128], BF16)
    smt = ctx.alloc("smt", [128, 8 * 128], BF16)
    es = ctx.alloc("es", [128, 512], F32)
    la = ctx.alloc("la", [128, 16], F32)
    dump = ctx.dump = ctx.alloc("dump", [128, 512], F32)

    ebr = P.dma("gpsimd", b_row[0:1, :], ctx.d_l0a_b[:, :], waits=[ctx.gate("rows")])
    emk = ctx.maskT_ready
    ew = [P.dma("gpsimd", w_sb[:, i * 8192:(i + 1) * 8192].rearrange("p (ci n) -> p ci n", ci=8),
                ctx.d_l0a_w[i].rearrange("(ci p) n -> p ci n", p=128),
                waits=[ctx.gate("l0aw%d" % i)]) for i in range(4)]
    ez = P.op("gpsimd", lambda E: E.memset(szb[:], 0.0), waits=[ctx.gate("szb")])
    ezf = P.op("gpsimd", lambda E: E.memset(szf[:], 0.0), waits=[ctx.gate("szb")])
    evo = P.op("gpsimd", lambda E: E.memset(
        vg[:].rearrange("p (s c) -> p s c", c=65)[:, :, 64:65], 1.0), waits=[ctx.gate("vg")])

    def szsb(h):
        return szb[(h % 2) * 64:(h % 2) * 64 + 64,
                   (h // 8) * 260 + ((h // 2) % 4) * 65: (h // 8) * 260 + ((h // 2) % 4) * 65 + 65]

    out_evs = []
    sz_evac = None
    sz_mm_done = None
    for tg in range(NTC // 4):
        fm_done = None
        for fc in range(NFC):
            bank, bfree = ctx.bank()
            mev = None
            for j, tc in enumerate(range(tg * 4, tg * 4 + 4)):
                def mkt(fc=fc, tc=tc, j=j, bank=bank):
                    def f(E):
                        return E.transpose(
                            bank[:, j * 128:(j + 1) * 128],
                            x_tm[:, tc * D + fc * 128: tc * D + fc * 128 + 128],
                            ctx.ident_f32[:])
                    return f
                w = [x_ready, bfree, ctx.ident_ready, ctx.gate("xfmg")] if j == 0 else []
                mev = P.op("tensor", mkt(), waits=w)

            def mkevb(fc=fc, bank=bank):
                def f(E):
                    return _copy(E, x_fm[:, fc * 512:(fc + 1) * 512], bank[:, 0:512])
                return f
            fm_done = P.op("scalar", mkevb(), waits=[mev])
            ctx.pb_ev[id(bank)] = fm_done
        pevs = {}
        for pi, dst in ((0, qf), (1, kf)):
            for fc in range(8):
                bank, bfree = ctx.bank()
                for fi in range(8):
                    w = [ew[pi], fm_done, bfree, ctx.gate("qkf")] if fi == 0 else []
                    mm(P, ctx, bank,
                       w_sb[:, pi * 8192 + fi * 1024 + fc * 128: pi * 8192 + fi * 1024 + fc * 128 + 128],
                       x_fm[:, fi * 512:(fi + 1) * 512],
                       512, fi == 0, False, waits=w)
                mev = mm(P, ctx, bank, b_row[0:1, pi * 1024 + fc * 128: pi * 1024 + fc * 128 + 128],
                         ctx.ones_rowb[0:1, 0:512], 512, False, True, waits=[ebr, ctx.onesb_ready])
                e1 = P.op("scalar", lambda E, b=bank, o=es: E.activation(o[:, 0:512], b[:, 0:512], AF.Exp),
                          waits=[mev, ctx.gate("es")])
                e2 = P.op("vector", lambda E, b=bank, o=dst, fc=fc: E.tensor_scalar(
                    o[:, fc * 512:(fc + 1) * 512], b[:, 0:512], 0.0, None, ALU.max),
                    waits=[mev, ctx.gate("qkf"), e1])
                e3 = P.op("vector", lambda E, o=dst, fc=fc, a=es: E.scalar_tensor_tensor(
                    o[:, fc * 512:(fc + 1) * 512], a[:, 0:512], 1.0,
                    o[:, fc * 512:(fc + 1) * 512], ALU.min, ALU.add),
                    waits=[e1, e2])
                ctx.set_gate("es", e3)
                ctx.pb_ev[id(bank)] = e3
                pevs[(pi, fc)] = e3
        for tci in range(4):
            for no in range(2):
                bank, bfree = ctx.bank()
                for fi in range(8):
                    w = [ew[2], fm_done, bfree, ctx.gate("vg")] if fi == 0 else []
                    mm(P, ctx, bank,
                       x_fm[:, fi * 512 + tci * 128: fi * 512 + tci * 128 + 128],
                       w_sb[:, 2 * 8192 + fi * 1024 + no * 512: 2 * 8192 + fi * 1024 + no * 512 + 512],
                       512, fi == 0, False, waits=w)
                mev = mm(P, ctx, bank, ctx.ones_rowb[0:1, 0:128],
                         b_row[0:1, 2 * 1024 + no * 512: 2 * 1024 + no * 512 + 512],
                         512, False, True, waits=[ebr, ctx.onesb_ready])
                def mkvev(b=bank, tci=tci, no=no):
                    def f(E):
                        dst = vg[:, tci * 1040 + no * 520: tci * 1040 + no * 520 + 520]
                        dst3 = dst.rearrange("p (h c) -> p h c", c=65)[:, :, 0:64]
                        return E.tensor_copy(dst3, b[:, 0:512].rearrange("p (h c) -> p h c", c=64))
                    return f
                ev = P.op("vector", mkvev(), waits=[mev, evo])
                ctx.pb_ev[id(bank)] = ev
                pevs[("v", tci, no)] = ev
        kt_evs = {}
        for tci in range(4):
            bank, bfree = ctx.bankt()
            tev = None
            for fc in range(8):
                w = [pevs[(1, fc)], bfree, ctx.gate("ktg")] if fc == 0 else [pevs[(1, fc)]]
                tev = P.op("tensor", lambda E, b=bank, fc=fc, tci=tci: E.transpose(
                    b[:, fc * 128:(fc + 1) * 128],
                    kf[:, fc * 512 + tci * 128: fc * 512 + tci * 128 + 128],
                    ctx.ident_bf16[:]), waits=w + ([ctx.identb_ready] if fc == 0 else []))
            ev = P.op("vector", lambda E, b=bank, tci=tci: E.tensor_copy(
                ktg[:, tci * 1024:(tci + 1) * 1024], b[:, 0:1024]), waits=[tev])
            ctx.pb_ev[id(bank)] = ev
            kt_evs[tci] = ev

        for tci in range(4):
            c = tg * 4 + tci
            smt_evs = {}
            num_done = []
            for h in range(16):
                fc, r0 = h // 2, (h % 2) * 64
                bank, bfree = ctx.bank()
                m1 = mm(P, ctx, bank,
                        kf[r0:r0 + 64, fc * 512 + tci * 128: fc * 512 + tci * 128 + 128],
                        qf[r0:r0 + 64, fc * 512 + tci * 128: fc * 512 + tci * 128 + 128],
                        128, True, True, waits=[pevs[(0, fc)], pevs[(1, fc)], bfree])
                smev = P.op("vector", lambda E, b=bank, h=h: E.tensor_tensor(
                    smt[:, (h % 8) * 128:(h % 8 + 1) * 128], b[:, 0:128], maskT[:], ALU.mult),
                    waits=[m1, emk, ctx.gate("smt%d" % (h % 8))])
                ctx.pb_ev[id(bank)] = smev
                smt_evs[h] = smev
                nbank, nbfree = ctx.bank()
                mmev0 = mm(P, ctx, nbank, smt[:, (h % 8) * 128:(h % 8 + 1) * 128],
                           vg[:, tci * 1040 + h * 65: tci * 1040 + h * 65 + 65],
                           65, True, False, waits=[smev, pevs[("v", tci, h // 8)], nbfree])
                def mksz(nb=nbank, h=h, fc=fc, r0=r0, tci=tci):
                    def f(E):
                        return E.matmul(nb[:, 0:65], qf[r0:r0 + 64, fc * 512 + tci * 128: fc * 512 + tci * 128 + 128],
                                        szsb(h), start=False, stop=True)
                    return f
                nev = P.op("tensor", mksz(), waits=[sz_evac, ez] if h == 0 else [])
                ctx.set_gate("smt%d" % (h % 8), nev)
                rev = P.op("vector", lambda E, nb=nbank, h=h: E.reciprocal(la[:, h:h + 1], nb[:, 64:65]),
                           waits=[nev])
                oev = P.op("vector", lambda E, nb=nbank, h=h: E.tensor_scalar(
                    otm[:, h * 64:(h + 1) * 64], nb[:, 0:64], la[:, h:h + 1], None, ALU.mult),
                    waits=[rev, ctx.gate("otm%d" % h)])
                ctx.pb_ev[id(nbank)] = oev
                ctx.set_gate("otm%d" % h, oev)
                num_done.append(oev)
            dbanks = []
            dmms = []
            for half in range(2):
                bk, bkfree = ctx.bank()
                dbanks.append(bk)
                for hh in range(8):
                    h = half * 8 + hh
                    r0b = (h % 2) * 64
                    off = ((h // 2) % 4) * 65
                    w = []
                    if hh == 0:
                        w = [kt_evs[tci], bkfree]
                    def mks(bk=bk, r0b=r0b, off=off, h=h, tci=tci):
                        def f(E):
                            return E.matmul(bk[r0b:r0b + 64, off:off + 65],
                                            ktg[:, tci * 1024 + h * 64: tci * 1024 + h * 64 + 64],
                                            vg[:, tci * 1040 + h * 65: tci * 1040 + h * 65 + 65],
                                            start=True, stop=True, tile_position=(0, r0b))
                        return f
                    dmms.append(P.op("tensor", mks(), waits=w))
            ad1 = P.op("vector", lambda E, b=dbanks[0]: E.tensor_add(szf[:, 0:260], szf[:, 0:260], b[:, 0:260]),
                       waits=[dmms[7], ezf, ctx.gate("szfa")])
            ad2 = P.op("vector", lambda E, b=dbanks[1]: E.tensor_add(szf[:, 260:520], szf[:, 260:520], b[:, 0:260]),
                       waits=[dmms[15], ctx.gate("szfb")])
            ctx.set_gate("szfa", ad1)
            ctx.set_gate("szfb", ad2)
            ctx.pb_ev[id(dbanks[0])] = ad1
            ctx.pb_ev[id(dbanks[1])] = ad2
            cp = P.op("vector", lambda E: E.tensor_copy(szb[:], szf[:]), waits=[ad1, ad2, ez])
            sz_evac = cp
            ctx.set_gate("szb", cp)
            bank, bfree = ctx.bankt()
            tev = None
            for fc2 in range(8):
                w = [num_done[fc2 * 2], num_done[fc2 * 2 + 1]]
                if fc2 == 0:
                    w = w + [bfree, ctx.gate("aofc"), ctx.identb_ready]
                tev = P.op("tensor", lambda E, b=bank, fc2=fc2: E.transpose(
                    b[:, fc2 * 128:(fc2 + 1) * 128],
                    otm[:, fc2 * 128:(fc2 + 1) * 128], ctx.ident_bf16[:]), waits=w)
            aev = P.op("vector", lambda E, b=bank: E.tensor_copy(aofc[:], b[:, 0:1024]),
                       waits=[tev])
            ctx.pb_ev[id(bank)] = aev
            add_evs = []
            for no in range(2):
                bank, bfree = ctx.bank()
                for fi in range(8):
                    w = [aev, ew[3], bfree] if fi == 0 else []
                    mm(P, ctx, bank, aofc[:, fi * 128:(fi + 1) * 128],
                       w_sb[:, 3 * 8192 + fi * 1024 + no * 512: 3 * 8192 + fi * 1024 + no * 512 + 512],
                       512, fi == 0, False, waits=w)
                mev = mm(P, ctx, bank, ctx.ones_rowb[0:1, 0:128],
                         b_row[0:1, 3 * 1024 + no * 512: 3 * 1024 + no * 512 + 512],
                         512, False, True, waits=[ebr, ctx.onesb_ready])
                def mkadd(c=c, no=no, bank=bank):
                    def f(E):
                        sl = x_tm[:, c * D + no * 512: c * D + no * 512 + 512]
                        return E.tensor_tensor(sl, sl, bank[:, 0:512], ALU.add)
                    return f
                ae = P.op("vector", mkadd(), waits=[mev])
                ctx.pb_ev[id(bank)] = ae
                add_evs.append(ae)
            ctx.set_gate("aofc", P.barrier_ev("tensor"))
            out_evs.append(emit_ln_inplace(P, ctx, c, add_evs[-1]))
        gbar = P.barrier_ev("tensor")
        ctx.set_gate("qkf", gbar)
        ctx.set_gate("vg", gbar)
        ctx.set_gate("ktg", gbar)
        ctx.set_gate("xfmg", gbar)

    for i in range(4):
        ctx.set_gate("l0aw%d" % i, P.barrier_ev("tensor"))
    ctx.free(x_fm, w_sb, b_row, qf, kf, vg, ktg, szb, szf, otm, aofc, smt, es, la, dump)
    ctx.new_phase(P)
    return out_evs



NHEAD = 16
W1HALF = [list(range(0, 11)), list(range(11, 22))]


def fc_rows(fc):
    return 42 if fc == 21 else 128


def emit_ln_buf(P, ctx, sl, slot, dep_ev, eps=1e-5):
    st = ctx.stat[:, slot * 8:(slot + 1) * 8]
    s1, s2, m, w, u, r, ri, s2b = (st[:, i:i + 1] for i in range(8))
    dump = ctx.dump
    gst = ctx.gate("lnst%d" % slot)
    e1 = P.op("vector", lambda E, o=s1, i=sl: E.reduce_sum(o, i, AX.X), waits=[dep_ev, gst])
    e2a = P.op("scalar", lambda E, o=dump, i=sl, a=s2: E.activation(o[:, 0:512], i[:, 0:512], AF.Square, accum_out=a),
               waits=[dep_ev, gst, ctx.gate("dump")])
    e2b = P.op("scalar", lambda E, o=dump, i=sl, a=s2b: E.activation(o[:, 0:512], i[:, 512:1024], AF.Square, accum_out=a),
               waits=[e2a])
    ctx.set_gate("dump", e2b)
    e3 = P.op("gpsimd", lambda E, o=m, i=s1: E.tensor_scalar(o, i, 1.0 / D, None, ALU.mult), waits=[e1])
    e4a = P.op("gpsimd", lambda E, o=w, a=s2, b=s2b: E.tensor_add(o, a, b), waits=[e2a, e2b])
    e4 = P.op("gpsimd", lambda E, o=w: E.tensor_scalar(o, o, 1.0 / D, None, ALU.mult), waits=[e4a])
    e5 = P.op("gpsimd", lambda E, o=u, i=m: E.tensor_mul(o, i, i), waits=[e3])
    e6 = P.op("gpsimd", lambda E, o=w, a=u: E.tensor_sub(o, o, a), waits=[e4, e5])
    e7 = P.op("scalar", lambda E, o=r, i=w: E.activation(o, i, AF.Sqrt, bias=ctx.eps_col[:, 0:1]),
              waits=[e6, ctx.eps_ready])
    e8 = P.op("vector", lambda E, o=ri, i=r: E.reciprocal(o, i), waits=[e7])
    e9 = P.op("vector", lambda E, o=sl, a=m, b=ri: E.tensor_scalar(o, o, a, b, ALU.subtract, ALU.mult),
              waits=[e8, e3, dep_ev])
    ctx.set_gate("lnst%d" % slot, e9)
    return e9


def emit_l0f_layer(P, ctx, seq, li, final_ln=False, prev_final=None):
    NT = NTC
    Tp = T + 1
    g = ctx.gate
    sg = ctx.set_gate

    xs_fm = ctx.alloc("xs_fm", [128, NFC * Tp], BF16)
    wqkv = ctx.alloc("wqkv", [128, 8 * 3072], BF16)
    wo = ctx.alloc("wo", [128, 8 * 1024], BF16)
    qkvg = ctx.alloc("qkvg", [128, 4 * 3072], BF16)
    qf = ctx.alloc("qf", [128, 2 * 1024], BF16)
    kf = ctx.alloc("kf", [128, 4 * 1024], BF16)
    vr = ctx.alloc("vr", [128, 4 * 1024], BF16)
    aofm = ctx.alloc("aofm", [128, 2 * 1024], BF16)
    tabs = ctx.alloc("tabs", [128, 4 * NT * 64], F32)
    reps = ctx.alloc("reps", [128, 4 * 64], F32)
    t1 = ctx.alloc("t1", [128, 4 * 1024], F32)
    t2 = ctx.alloc("t2", [128, 2 * 1024], F32)
    lmask = ctx.alloc("lmask", [128, 384], BF16)
    p_sb = ctx.alloc("p_sb", [128, 3 * 384], F32)
    p2_sb = ctx.alloc("p2_sb", [128, 2 * 384], BF16)
    diag = ctx.alloc("diag", [128, 2 * 128], BF16)
    pt_sb = ctx.alloc("pt_sb", [128, 3 * 384], BF16)
    xstage = ctx.alloc("xstage", [128, 2 * 1024], F32)
    rstd = ctx.alloc("rstd", [128, 4 * 16], F32)
    dump = ctx.dump = ctx.alloc("dumpA", [128, 256], F32)

    ewq = P.dma("gpsimd", wqkv[:].rearrange("p (ci n) -> p ci n", ci=8),
                ctx.d_wqkv[li].rearrange("(ci p) n -> p ci n", p=128), waits=[g("wqkv")])
    ewo = P.dma("gpsimd", wo[:].rearrange("p (ci n) -> p ci n", ci=8),
                ctx.d_wo0[li].rearrange("(ci p) n -> p ci n", p=128), waits=[g("wo")])
    etab = P.dma("sync", tabs[:].rearrange("p (j n) -> p j n", j=4),
                 ctx.d_rotab[:, :, :, :].rearrange("j p tc d -> p j (tc d)"), waits=[g("tabs")])
    erep = P.dma("sync", reps[:].rearrange("p (j n) -> p j n", j=4),
                 ctx.d_qs4[li].rearrange("j p d -> p j d"), waits=[g("tabs")])
    emsk = P.dma("gpsimd", lmask[:], ctx.d_lmask[:, :], waits=[g("lmask")])
    tab_evs = []
    for j in range(4):
        def mkc(j=j):
            def f(E):
                r = reps[:, j * 64:(j + 1) * 64]
                rb = bass.AP(r.tensor, r.offset, [[r.ap[0][0], 128], [0, NT], [1, 64]])
                t3 = tabs[:, j * NT * 64:(j + 1) * NT * 64].rearrange("p (tc d) -> p tc d", d=64)
                return E.tensor_tensor(t3, t3, rb, ALU.mult)
            return f
        tab_evs.append(P.op("gpsimd", mkc(), waits=[etab, erep]))
    tabs_done = tab_evs[-1]

    epad = P.op("gpsimd", lambda E: E.memset(
        xs_fm[:].rearrange("p (c t) -> p c t", c=NFC)[:, :, 0:1], 0.0), waits=[g("xsfm")])
    xev = {}
    fm_evs = []
    for tc in range(NT):
        sl = tc % 2
        w0 = [g("xst%d" % sl)]
        if prev_final is not None:
            w0.append(prev_final[tc])
        exl = P.dma("scalar", xstage[:, sl * 1024:(sl + 1) * 1024],
                    ctx.d_xcur[seq, tc * 128:(tc + 1) * 128, :], waits=w0)
        xev[tc] = exl
        for half in range(2):
            bank, bfree = ctx.bank()
            mev = None
            for j in range(4):
                fc = half * 4 + j
                def mkt(fc=fc, j=j, bank=bank, sl=sl):
                    def f(E):
                        return E.transpose(
                            bank[:, j * 128:(j + 1) * 128],
                            xstage[:, sl * 1024 + fc * 128: sl * 1024 + fc * 128 + 128],
                            ctx.ident_f32[:])
                    return f
                w = [exl, bfree, ctx.ident_ready] if j == 0 else []
                mev = P.op("tensor", mkt(), waits=w)

            def mkev(tc=tc, half=half, bank=bank):
                def f(E):
                    dst = xs_fm[:]
                    base = (half * 4) * Tp + 1 + tc * 128
                    d3 = bass.AP(dst.tensor, dst.offset + base, [[dst.ap[0][0], 128], [Tp, 4], [1, 128]])
                    return E.tensor_copy(d3, bank[:, 0:512].rearrange("p (f t) -> p f t", f=4))
                return f
            ev = P.op("vector", mkev(), waits=[mev, epad])
            ctx.pb_ev[id(bank)] = ev
            fm_evs.append(ev)
        sg("xst%d" % sl, fm_evs[-1])
    fm_done = fm_evs[-1]
    a0bar = P.barrier_ev("tensor")
    sg("xs20", a0bar)
    sg("xs21", a0bar)

    pj_evs = {}

    def emit_qkv(c):
        slot = c % 4
        for fot in range(6):
            bank, bfree = ctx.bank()
            for fi in range(8):
                off = 1 if fi < 4 else 0
                w = [ewq, fm_done, bfree, g("qkvg_s%d" % slot)] if fi == 0 else []
                mm(P, ctx, bank,
                   xs_fm[:, fi * Tp + off + c * 128: fi * Tp + off + c * 128 + 128],
                   wqkv[:, fi * 3072 + fot * 512: fi * 3072 + fot * 512 + 512],
                   512, fi == 0, fi == 7, waits=w)
            mev = P.barrier_ev("tensor")

            def mkev(slot=slot, fot=fot, bank=bank):
                def f(E):
                    return E.copy(qkvg[:, slot * 3072 + fot * 512: slot * 3072 + fot * 512 + 512],
                                  bank[:, 0:512])
                return f
            ev = P.op("scalar", mkev(), waits=[mev])
            ctx.pb_ev[id(bank)] = ev
            pj_evs[(c, fot)] = ev

    def emit_chunk(qc):
        tc4 = qc % 4
        rco = (qc % 2) * 32
        qkv_ready = [pj_evs[(qc, f)] for f in range(6)]
        rr_evs = {}
        rot_evs = {}
        for pi in (0, 1):
            base = pi * 1024
            reg = qkvg[:, tc4 * 3072 + base: tc4 * 3072 + base + 1024]
            qw = qkv_ready[2 * pi: 2 * pi + 2]
            sqe = None
            for h4 in range(NHEAD):
                wq4 = qw + [g("rst%d_%d" % (pi, qc % 2)), g("sqd%d" % (h4 % 4))] if h4 == 0 else [g("sqd%d" % (h4 % 4))]
                def mksq(h4=h4, pi=pi, reg=reg, dump=dump, rco=rco):
                    def f(E):
                        return E.activation(
                            dump[:, (h4 % 4) * 64:(h4 % 4) * 64 + 64],
                            reg[:, h4 * 64:(h4 + 1) * 64], AF.Square,
                            accum_out=rstd[:, rco + pi * 16 + h4: rco + pi * 16 + h4 + 1])
                    return f
                sqe = P.op("scalar", mksq(), waits=wq4)
                sg("sqd%d" % (h4 % 4), sqe)
            sqr = P.op("scalar", lambda E, o=rstd, pi=pi, rco=rco: E.activation(
                o[:, rco + pi * 16:rco + (pi + 1) * 16], o[:, rco + pi * 16:rco + (pi + 1) * 16], AF.Sqrt), waits=[sqe])
            cle = P.op("vector", lambda E, o=rstd, pi=pi, rco=rco: E.tensor_scalar(
                o[:, rco + pi * 16:rco + (pi + 1) * 16], o[:, rco + pi * 16:rco + (pi + 1) * 16], 1e-12, None, ALU.max),
                waits=[sqr])
            rre = P.op("vector", lambda E, o=rstd, pi=pi, rco=rco: E.reciprocal(
                o[:, rco + pi * 16:rco + (pi + 1) * 16], o[:, rco + pi * 16:rco + (pi + 1) * 16]), waits=[cle])
            rr_evs[pi] = rre
            ct, st_ = (0, 1) if pi == 0 else (2, 3)
            tb_c = tabs[:, (ct * NT + qc) * 64: (ct * NT + qc) * 64 + 64]
            tb_s0 = tabs[:, (st_ * NT + qc) * 64: (st_ * NT + qc) * 64 + 32]
            tb_s1 = tabs[:, (st_ * NT + qc) * 64 + 32: (st_ * NT + qc) * 64 + 64]
            apc = bass.AP(tb_c.tensor, tb_c.offset, [[tb_c.ap[0][0], 128], [0, 16], [1, 64]])
            aps0 = bass.AP(tb_s0.tensor, tb_s0.offset, [[tb_s0.ap[0][0], 128], [0, 16], [1, 32]])
            aps1 = bass.AP(tb_s1.tensor, tb_s1.offset, [[tb_s1.ap[0][0], 128], [0, 16], [1, 64 - 32]])
            par = (qc % 2) * 2 + pi
            t1s = t1[:, par * 1024:(par + 1) * 1024]
            t2s = t2[:, pi * 1024:(pi + 1) * 1024]
            e_t1 = P.op("vector", lambda E, r=reg, apc=apc, o=t1s: E.tensor_tensor(
                o.rearrange("p (h d) -> p h d", d=64),
                r.rearrange("p (h d) -> p h d", d=64), apc, ALU.mult),
                waits=qw + [tabs_done, g("t1_%d" % par)])
            e_t2a = P.op("gpsimd", lambda E, r=reg, aps0=aps0, o=t2s: E.tensor_tensor(
                o.rearrange("p (h d) -> p h d", d=64)[:, :, 0:32],
                r.rearrange("p (h d) -> p h d", d=64)[:, :, 32:64], aps0, ALU.mult),
                waits=qw + [tabs_done, g("t2_%d" % pi)])
            e_t2b = P.op("gpsimd", lambda E, r=reg, aps1=aps1, o=t2s: E.tensor_tensor(
                o.rearrange("p (h d) -> p h d", d=64)[:, :, 32:64],
                r.rearrange("p (h d) -> p h d", d=64)[:, :, 0:32], aps1, ALU.mult),
                waits=[])
            e_rot = P.op("gpsimd", lambda E, r=reg, a=t1s, b=t2s: E.tensor_add(r, a, b),
                         waits=[e_t1, e_t2a, e_t2b, sqe])
            sg("t1_%d" % par, e_rot)
            sg("t2_%d" % pi, e_rot)
            if pi == 1:
                last = e_rot
                for h in range(NHEAD):
                    def mkr(h=h, r=reg, rco=rco):
                        def f(E):
                            return E.activation(
                                r[:, h * 64:(h + 1) * 64], r[:, h * 64:(h + 1) * 64],
                                AF.Identity, scale=rstd[:, rco + 16 + h:rco + 16 + h + 1])
                        return f
                    last = P.op("scalar", mkr(), waits=[e_rot, rre] if h == 0 else [])
                rot_evs[pi] = last
            else:
                rot_evs[pi] = e_rot
        qslot = qc % 2
        kslot = qc % 4
        tevs = {}
        for pi, dstbuf, slot, gname in ((0, qf, qslot, "qfr"), (1, kf, kslot, "kfr")):
            bank, bfree = ctx.bankt()
            tev = None
            for fc in range(8):
                def mkt(pi=pi, fc=fc, bank=bank, tc4=tc4):
                    def f(E):
                        srcc = qkvg[:, tc4 * 3072 + pi * 1024 + fc * 128: tc4 * 3072 + pi * 1024 + fc * 128 + 128]
                        return E.transpose(bank[:, fc * 128:(fc + 1) * 128], srcc, ctx.ident_bf16[:])
                    return f
                w = [rot_evs[pi], bfree, ctx.identb_ready] if fc == 0 else []
                tev = P.op("tensor", mkt(), waits=w)
            ev = P.op("vector", lambda E, b=bank, dstbuf=dstbuf, slot=slot: E.tensor_copy(
                dstbuf[:, slot * 1024:(slot + 1) * 1024], b[:, 0:1024]),
                waits=[tev, g(gname + str(slot))])
            ctx.pb_ev[id(bank)] = ev
            tevs[pi] = ev
        vslot = qc % 4
        vev = P.op("gpsimd", lambda E, tc4=tc4, vslot=vslot: E.tensor_copy(
            vr[:, vslot * 1024:(vslot + 1) * 1024],
            qkvg[:, tc4 * 3072 + 2048: tc4 * 3072 + 3072]),
            waits=qkv_ready[4:6] + [g("vr%d" % vslot)])
        sg("qkvg_s%d" % tc4, P.barrier_ev("gpsimd"))
        kcs = [k for k in (qc - 2, qc - 1, qc) if k >= 0]
        nk = len(kcs)
        o_evs = []
        st1 = {}
        st2 = {}

        def stage1(h):
            fc, r0 = h // 2, (h % 2) * 64
            sbank, sfree = ctx.bank()
            smev = None
            for j, kc in enumerate(kcs):
                def mks(j=j, kc=kc, fc=fc, r0=r0, sbank=sbank, qslot=qslot):
                    def f(E):
                        return E.matmul(
                            sbank[:, j * 128:(j + 1) * 128],
                            qf[r0:r0 + 64, qslot * 1024 + fc * 128: qslot * 1024 + fc * 128 + 128],
                            kf[r0:r0 + 64, (kc % 4) * 1024 + fc * 128: (kc % 4) * 1024 + fc * 128 + 128],
                            start=True, stop=True)
                    return f
                w = []
                if j == 0:
                    w = [tevs[0], sfree]
                    if h == 0:
                        w.append(tevs[1])
                smev = P.op("tensor", mks(), waits=w)
            st1[h] = (sbank, smev)

        def stage2a(h):
            hs = h % 3
            po = hs * 384
            so = 96 + (h % 4) * 8
            sbank, smev = st1.pop(h)
            sme = P.op("vector", lambda E, b=sbank, h=h, nk=nk, po=po, rco=rco: E.scalar_tensor_tensor(
                p_sb[:, po:po + nk * 128], b[:, 0:nk * 128], rstd[:, rco + h:rco + h + 1],
                lmask[:, (3 - nk) * 128:384], ALU.mult, ALU.add),
                waits=[smev, rr_evs[0], emsk, g("p_sb%d" % hs)])
            ctx.pb_ev[id(sbank)] = sme
            mne = P.op("vector", lambda E, nk=nk, po=po, so=so: E.reduce_max(
                ctx.stat[:, so:so + 1], p_sb[:, po:po + nk * 128], AX.X, negate=True),
                waits=[sme, g("mn%d" % (h % 4))])
            po2 = (h % 2) * 384
            pe = P.op("scalar", lambda E, nk=nk, po=po, po2=po2, so=so: E.activation(
                p2_sb[:, po2:po2 + nk * 128], p_sb[:, po:po + nk * 128], AF.Exp,
                bias=ctx.stat[:, so:so + 1], accum_out=ctx.stat[:, so + 1:so + 2]),
                waits=[mne, g("p2x%d" % (h % 2))])
            sg("mn%d" % (h % 4), pe)
            sg("p_sb%d" % hs, pe)
            st2[h] = pe

        def stage2b(h):
            fc, r0 = h // 2, (h % 2) * 64
            hs = h % 3
            po = hs * 384
            so = 96 + (h % 4) * 8
            po2 = (h % 2) * 384
            pe = st2.pop(h)
            rcpe = P.op("vector", lambda E, so=so: E.reciprocal(
                ctx.stat[:, so + 2:so + 3], ctx.stat[:, so + 1:so + 2]), waits=[pe])
            dge = P.op("vector", lambda E, hs2=h % 2, so=so: E.tensor_scalar(
                diag[:, hs2 * 128:(hs2 + 1) * 128], ctx.ident_bf16[:],
                ctx.stat[:, so + 2:so + 3], None, ALU.mult),
                waits=[rcpe, ctx.identb_ready, g("diag%d" % (h % 2))])
            tbank, tfree = ctx.bank()
            ptev = None
            for j in range(nk):
                def mkt2(j=j, tbank=tbank, po2=po2, hs2=h % 2):
                    def f(E):
                        return E.matmul(tbank[:, j * 128:(j + 1) * 128],
                                        p2_sb[:, po2 + j * 128: po2 + (j + 1) * 128],
                                        diag[:, hs2 * 128:(hs2 + 1) * 128],
                                        start=True, stop=True)
                    return f
                w = [pe, dge, tfree] if j == 0 else []
                ptev = P.op("tensor", mkt2(), waits=w)
            sg("p2x%d" % (h % 2), ptev)
            sg("diag%d" % (h % 2), ptev)
            pte = P.op("scalar", lambda E, b=tbank, nk=nk, po=po: E.copy(
                pt_sb[:, po:po + nk * 128], b[:, 0:nk * 128]),
                waits=[ptev, g("pt%d" % hs)])
            ctx.pb_ev[id(tbank)] = pte
            obank, ofree = ctx.bank()
            omev = None
            for j, kc in enumerate(kcs):
                def mko(j=j, kc=kc, h=h, obank=obank, nk=nk, po=po):
                    def f(E):
                        return E.matmul(obank[0:64, 0:128],
                                        vr[:, (kc % 4) * 1024 + h * 64: (kc % 4) * 1024 + h * 64 + 64],
                                        pt_sb[:, po + j * 128: po + j * 128 + 128],
                                        start=(j == 0), stop=(j == nk - 1))
                    return f
                w = [pte] if j == 0 else []
                if j == 0:
                    w += [ofree, vev]
                omev = P.op("tensor", mko(), waits=w)
            sg("pt%d" % hs, omev)
            oev = P.op("scalar", lambda E, b=obank, fc=fc, r0=r0, qp=qc % 2: E.copy(
                aofm[r0:r0 + 64, qp * 1024 + fc * 128: qp * 1024 + fc * 128 + 128],
                b[0:64, 0:128]), waits=[omev, g("aofm%d" % h)])
            ctx.pb_ev[id(obank)] = oev
            o_evs.append(oev)
            sg("aofm%d" % h, oev)

        for h in range(NHEAD + 2):
            if h < NHEAD:
                stage1(h)
            if 1 <= h <= NHEAD:
                stage2a(h - 1)
            if h >= 2:
                stage2b(h - 2)
        sg("qfr%d" % qslot, P.barrier_ev("tensor"))
        sg("rst0_%d" % (qc % 2), P.barrier_ev("vector"))
        sg("rst1_%d" % (qc % 2), P.barrier_ev("vector"))
        tb_ = P.barrier_ev("tensor")
        gb_ = P.barrier_ev("gpsimd")
        ctx.gates["qkvg_s%d" % tc4] = [tb_, gb_]
        sl2 = qc % 2
        exl2 = P.dma("scalar", xstage[:, sl2 * 1024:(sl2 + 1) * 1024],
                     ctx.d_xcur[seq, qc * 128:(qc + 1) * 128, :], waits=[g("xs2%d" % sl2)])
        add_evs = []
        for no in range(2):
            bank, bfree = ctx.bank()
            for fi in range(8):
                w = [o_evs[-1], ewo, bfree] if fi == 0 else []
                mm(P, ctx, bank, aofm[:, (qc % 2) * 1024 + fi * 128: (qc % 2) * 1024 + fi * 128 + 128],
                   wo[:, fi * 1024 + no * 512: fi * 1024 + no * 512 + 512],
                   512, fi == 0, fi == 7, waits=w)
            mev = P.barrier_ev("tensor")

            def mka(no=no, bank=bank, sl2=sl2):
                def f(E):
                    slx = xstage[:, sl2 * 1024 + no * 512: sl2 * 1024 + no * 512 + 512]
                    return E.tensor_tensor(slx, slx, bank[:, 0:512], ALU.add)
                return f
            ae = P.op("vector", mka(), waits=[mev, exl2])
            ctx.pb_ev[id(bank)] = ae
            add_evs.append(ae)
        est = P.dma("gpsimd", ctx.d_xnxt[seq, qc * 128:(qc + 1) * 128, :],
                    xstage[:, sl2 * 1024:(sl2 + 1) * 1024], waits=add_evs)
        sg("xs2%d" % sl2, est)
        xev[("a", qc)] = est

    for c in range(NT + 2):
        if c < NT:
            emit_qkv(c)
        if c >= 2:
            emit_chunk(c - 2)
    a_done_t = P.barrier_ev("tensor")
    ctx.free(xs_fm, wqkv, wo, qkvg, qf, kf, vr, aofm, tabs, reps, t1, t2,
             lmask, p_sb, p2_sb, diag, pt_sb, xstage, rstd, dump)
    ctx.new_phase(P)

    xn_fm = ctx.alloc("xn_fm", [128, NFC * T], BF16)
    prod = ctx.alloc("prod", [128, 11 * T], BF16)
    w2h = ctx.alloc("w2h", [128, 11 * 1024], BF16)
    w1c = ctx.alloc("w1c", [128, 2 * 8 * 128], BF16)
    b1r = ctx.alloc("b1r", [1, 5460], BF16)
    b2r = ctx.alloc("b2r", [1, 1024], BF16)
    grep = ctx.alloc("grep", [128, 1024], F32)
    brep = ctx.alloc("brep", [128, 1024], F32)
    xst = ctx.alloc("xstB", [128, 2 * 1024], F32)
    xs_t = ctx.alloc("xs_t", [128, 2 * 1024], F32)
    ggb = ctx.alloc("gg", [128, 2 * 512], F32)
    x0st = ctx.alloc("x0st", [128, 1024], F32) if final_ln else None
    dump = ctx.dump = ctx.alloc("dumpB", [128, 512], F32)

    eb1 = P.dma("gpsimd", b1r[0:1, :], ctx.d_b1[li:li + 1, :], waits=[g("b1r")])
    eb2 = P.dma("gpsimd", b2r[0:1, :], ctx.d_b2[li:li + 1, :], waits=[g("b2r")])
    egr = P.dma("sync", grep[:], ctx.d_lng[li], waits=[g("grep")])
    ebr_ = P.dma("sync", brep[:], ctx.d_lnb[li], waits=[g("brep")])

    xnevs = []
    prev_stage = None
    z1_prev = None
    for tc in range(NT):
        sl = tc % 2
        exl = P.dma("scalar", xst[:, sl * 1024:(sl + 1) * 1024],
                    ctx.d_xnxt[seq, tc * 128:(tc + 1) * 128, :],
                    waits=[xev[("a", tc)], g("xb%d" % sl), z1_prev])
        c1 = P.op("vector", lambda E, sl=sl: E.tensor_copy(
            xs_t[:, sl * 1024: sl * 1024 + 512], xst[:, sl * 1024: sl * 1024 + 512]),
            waits=[exl, g("xt%d" % sl)])
        if tc == 0:
            z1 = P.op("gpsimd", lambda E, sl=sl: E.memset(xs_t[0:1, sl * 1024 + 512: sl * 1024 + 1024], 0.0),
                      waits=[g("xt%d" % sl)])
        else:
            z1 = P.dma("sync", xs_t[0:1, sl * 1024 + 512: sl * 1024 + 1024],
                       xst[127:128, (1 - sl) * 1024 + 512: (1 - sl) * 1024 + 1024],
                       waits=[prev_stage, g("xt%d" % sl)])
        sh = P.dma("sync", xs_t[1:128, sl * 1024 + 512: sl * 1024 + 1024],
                   xst[0:127, sl * 1024 + 512: sl * 1024 + 1024],
                   waits=[exl, g("xt%d" % sl)])
        prev_stage = exl
        z1_prev = z1
        sg("xb%d" % sl, exl)
        sle = xs_t[:, sl * 1024:(sl + 1) * 1024]
        dep = Ev(sh.sem, sh.count)
        e0 = P.op("vector", lambda E: E.nop(), waits=[c1, z1, sh])
        lne = emit_ln_buf(P, ctx, sle, tc % 8, e0)
        e10 = P.op("gpsimd", lambda E, o=sle: E.tensor_mul(o, o, grep[:]), waits=[lne, egr])
        e11 = P.op("gpsimd", lambda E, o=sle: E.tensor_add(o, o, brep[:]), waits=[e10, ebr_])
        for half in range(2):
            bank, bfree = ctx.bank()
            mev = None
            for j in range(4):
                fc = half * 4 + j
                def mkt(fc=fc, j=j, bank=bank, sl=sl):
                    def f(E):
                        return E.transpose(bank[:, j * 128:(j + 1) * 128],
                                           xs_t[:, sl * 1024 + fc * 128: sl * 1024 + fc * 128 + 128],
                                           ctx.ident_f32[:])
                    return f
                w = [e11, bfree] if j == 0 else []
                mev = P.op("tensor", mkt(), waits=w)

            def mkev(tc=tc, half=half, bank=bank):
                def f(E):
                    dst = xn_fm[:]
                    base = (half * 4) * T + tc * 128
                    d3 = bass.AP(dst.tensor, dst.offset + base, [[dst.ap[0][0], 128], [T, 4], [1, 128]])
                    return E.tensor_copy(d3, bank[:, 0:512].rearrange("p (f t) -> p f t", f=4))
                return f
            ev = P.op("vector", mkev(), waits=[mev, g("xnfm")])
            ctx.pb_ev[id(bank)] = ev
            xnevs.append(ev)
        sg("xt%d" % sl, P.barrier_ev("tensor"))
    xn_done = xnevs[-1]

    final_evs = []
    bevs = {}
    for hi, half in enumerate(W1HALF):
        prod_evs = []
        for idx, fcl in enumerate(half):
            rows = fc_rows(fcl)
            ea = P.dma("gpsimd", w1c[0:rows, 0:8 * 128].rearrange("p (ci n) -> p ci n", ci=8)
                       if rows == 128 else w1c[:, 0:8 * rows].rearrange("p (ci n) -> p ci n", ci=8),
                       ctx.d_w1[li][:, fcl * 128: fcl * 128 + rows].rearrange("(ci p) n -> p ci n", p=128),
                       waits=[g("w1a")])
            eg_ = P.dma("gpsimd", w1c[:, 8 * 128:8 * 128 + 8 * rows].rearrange("p (ci n) -> p ci n", ci=8),
                        ctx.d_w1[li][:, 2730 + fcl * 128: 2730 + fcl * 128 + rows].rearrange("(ci p) n -> p ci n", p=128),
                        waits=[g("w1g")])
            for tg in range(T // 512):
                abank, afree = ctx.bank()
                gbank, gfree = ctx.bank()
                amms = []
                for fi in range(8):
                    w = [ea, xn_done, afree, g("prodh")] if fi == 0 else []
                    def mk1(fi=fi, rows=rows, abank=abank, tg=tg):
                        def f(E):
                            return E.matmul(abank[0:rows, 0:512],
                                            w1c[:, fi * rows: fi * rows + rows] if rows != 128 else w1c[:, fi * 128: fi * 128 + 128],
                                            xn_fm[:, fi * T + tg * 512: fi * T + tg * 512 + 512],
                                            start=(fi == 0), stop=False)
                        return f
                    amms.append(P.op("tensor", mk1(), waits=w))
                def mkba(rows=rows, abank=abank, fcl=fcl):
                    def f(E):
                        return E.matmul(abank[0:rows, 0:512], b1r[0:1, fcl * 128: fcl * 128 + rows],
                                        ctx.ones_rowb[0:1, 0:512], start=False, stop=True)
                    return f
                mma = P.op("tensor", mkba(), waits=[eb1, ctx.onesb_ready])
                for fi in range(8):
                    w = [eg_, xn_done, gfree] if fi == 0 else []
                    def mk1g(fi=fi, rows=rows, gbank=gbank, tg=tg):
                        def f(E):
                            return E.matmul(gbank[0:rows, 0:512],
                                            w1c[:, 8 * 128 + fi * rows: 8 * 128 + fi * rows + rows],
                                            xn_fm[:, fi * T + tg * 512: fi * T + tg * 512 + 512],
                                            start=(fi == 0), stop=False)
                        return f
                    P.op("tensor", mk1g(), waits=w)
                def mkbg(rows=rows, gbank=gbank, fcl=fcl):
                    def f(E):
                        return E.matmul(gbank[0:rows, 0:512], b1r[0:1, 2730 + fcl * 128: 2730 + fcl * 128 + rows],
                                        ctx.ones_rowb[0:1, 0:512], start=False, stop=True)
                    return f
                mmg = P.op("tensor", mkbg(), waits=[eb1])
                gsl = tg % 2
                gge = P.op("scalar", lambda E, b=gbank, rows=rows, gsl=gsl: E.activation(
                    ggb[0:rows, gsl * 512:(gsl + 1) * 512], b[0:rows, 0:512], AF.Gelu),
                    waits=[mmg, g("gg%d" % gsl)])

                def mkp(idx=idx, tg=tg, rows=rows, abank=abank, gsl=gsl):
                    def f(E):
                        return E.tensor_tensor(
                            prod[0:rows, idx * T + tg * 512: idx * T + tg * 512 + 512],
                            abank[0:rows, 0:512], ggb[0:rows, gsl * 512:(gsl + 1) * 512], ALU.mult)
                    return f
                pev = P.op("vector", mkp(), waits=[mma, gge])
                sg("gg%d" % gsl, pev)
                ctx.pb_ev[id(abank)] = pev
                ctx.pb_ev[id(gbank)] = pev
                prod_evs.append(pev)
            sg("w1a", P.barrier_ev("tensor"))
            sg("w1g", P.barrier_ev("tensor"))
        prod_done = prod_evs[-1]
        nfull = 11 if hi == 0 else 10
        row0 = half[0] * 128
        ew2 = P.dma("gpsimd", w2h[:, 0:nfull * 1024].rearrange("p (ci n) -> p ci n", ci=nfull),
                    ctx.d_w2[li][row0: row0 + nfull * 128, :].rearrange("(ci p) n -> p ci n", p=128),
                    waits=[g("w2h")])
        if hi == 1:
            ew2b = P.dma("gpsimd", w2h[0:42, 10 * 1024:11 * 1024],
                         ctx.d_w2[li][2688:2730, :], waits=[g("w2h")])
        else:
            ew2b = ew2
        for tc in range(NT):
            sl = tc % 2
            exl = P.dma("scalar", xst[:, sl * 1024:(sl + 1) * 1024],
                        ctx.d_xnxt[seq, tc * 128:(tc + 1) * 128, :],
                        waits=[g("xb%d" % sl)] + ([bevs[tc]] if hi == 1 else []))
            adds = []
            for no in range(2):
                bank, bfree = ctx.bank()
                for j, fcl in enumerate(half):
                    rows = fc_rows(fcl)
                    w = [ew2, ew2b, prod_done, bfree] if j == 0 else []
                    def mk2(j=j, rows=rows, tc=tc, no=no, bank=bank, hi=hi, lh=len(half)):
                        def f(E):
                            return E.matmul(bank[:, 0:512],
                                            prod[0:rows, j * T + tc * 128: j * T + tc * 128 + 128],
                                            w2h[0:rows, j * 1024 + no * 512: j * 1024 + no * 512 + 512],
                                            start=(j == 0), stop=(j == lh - 1 and hi == 1))
                        return f
                    P.op("tensor", mk2(), waits=w)
                if hi == 0:
                    mev = mm(P, ctx, bank, ctx.ones_rowb[0:1, 0:128],
                             b2r[0:1, no * 512: no * 512 + 512], 512, False, True, waits=[eb2])
                else:
                    mev = P.barrier_ev("tensor")

                def mka(no=no, bank=bank, sl=sl):
                    def f(E):
                        slx = xst[:, sl * 1024 + no * 512: sl * 1024 + no * 512 + 512]
                        return E.tensor_tensor(slx, slx, bank[:, 0:512], ALU.add)
                    return f
                ae = P.op("vector", mka(), waits=[mev, exl])
                ctx.pb_ev[id(bank)] = ae
                adds.append(ae)
            if hi == 0 or not final_ln:
                est = P.dma("gpsimd", ctx.d_xnxt[seq, tc * 128:(tc + 1) * 128, :],
                            xst[:, sl * 1024:(sl + 1) * 1024], waits=adds)
                sg("xb%d" % sl, est)
                bevs[tc] = est
                if hi == 1:
                    final_evs.append(est)
            else:
                ex0 = P.dma("sync", x0st[:], ctx.d_x0[seq, tc * 128:(tc + 1) * 128, :],
                            waits=[g("x0st")])
                a0 = P.op("vector", lambda E, sl=sl: E.tensor_tensor(
                    xst[:, sl * 1024:(sl + 1) * 1024], xst[:, sl * 1024:(sl + 1) * 1024],
                    x0st[:], ALU.add), waits=adds + [ex0])
                sg("x0st", a0)
                lne = emit_ln_buf(P, ctx, xst[:, sl * 1024:(sl + 1) * 1024], tc % 8, a0)
                est = P.dma("gpsimd", ctx.xout[seq, tc * 128:(tc + 1) * 128, :],
                            xst[:, sl * 1024:(sl + 1) * 1024], waits=[lne])
                sg("xb%d" % sl, est)
                final_evs.append(est)
        sg("prodh", P.barrier_ev("tensor"))
        sg("w2h", P.barrier_ev("tensor"))
    frees = [xn_fm, prod, w2h, w1c, b1r, b2r, grep, brep, xst, xs_t, ggb]
    if final_ln:
        frees.append(x0st)
    frees.append(dump)
    ctx.free(*frees)
    ctx.new_phase(P)
    return final_evs



def host_rotab(n=None):
    n = n or T
    DIM_HEAD, WINDOW = 64, 256
    half = np.arange(0, DIM_HEAD, 2, dtype=np.float32)
    inv_freq = (1.0 / 10000.0 ** (half / DIM_HEAD)).astype(np.float32)
    t = np.arange(n, dtype=np.float32)
    freqs = t[:, None] * inv_freq[None, :]
    freqs = np.concatenate([freqs, freqs], -1)
    base_scale = (half + 0.4 * DIM_HEAD) / (1.4 * DIM_HEAD)
    power = (t - n // 2) / (WINDOW // 2)
    scale = (base_scale[None, :].astype(np.float32) ** power[:, None].astype(np.float32)).astype(np.float32)
    scale = np.concatenate([scale, scale], -1)
    cos, sin = np.cos(freqs), np.sin(freqs)
    sign = np.where(np.arange(64) < 32, -1.0, 1.0).astype(np.float32)
    CS8 = (8.0 * cos * scale).astype(np.float32)
    SS8s = (8.0 * sin * scale * sign[None, :]).astype(np.float32)
    CI = (cos / scale).astype(np.float32)
    SIs = (sin / scale * sign[None, :]).astype(np.float32)
    out = np.stack([CS8, SS8s, CI, SIs])
    ntc = n // 128
    return np.ascontiguousarray(out.reshape(4, ntc, 128, 64).transpose(0, 2, 1, 3))


def host_lmask():
    q = np.arange(128)[:, None]
    k = np.arange(128)[None, :]
    m0 = np.where(q <= k, 0.0, -1e38).astype(np.float32)
    m1 = np.zeros((128, 128), np.float32)
    m2 = np.where(k <= q, 0.0, -1e38).astype(np.float32)
    return np.ascontiguousarray(np.concatenate([m0, m1, m2], axis=1))


def host_qs4(q_scale, k_scale):
    out = np.empty((6, 4, 128, 64), np.float32)
    for li in range(6):
        qs = q_scale[li]
        ks = k_scale[li]
        out[li, 0] = np.tile(qs, (128, 1))
        out[li, 1] = np.tile(np.concatenate([qs[32:], qs[:32]]), (128, 1))
        out[li, 2] = np.tile(ks, (128, 1))
        out[li, 3] = np.tile(np.concatenate([ks[32:], ks[:32]]), (128, 1))
    return out



N_CORES = 8


def _build_l0a():
    nc = bass.Bass("TRN2", target_bir_lowering=False)
    st = ExitStack()
    ctx = Ctx(nc)
    ctx.xin = nc.dram_tensor("xin", [2, T, D], F32, kind="ExternalInput")
    ctx.d_l0a_w = nc.dram_tensor("l0aw", [4, 1024, 1024], F32, kind="ExternalInput")
    ctx.d_l0a_b = nc.dram_tensor("l0ab", [1, 4 * 1024], F32, kind="ExternalInput")
    ctx.d_maskT = nc.dram_tensor("maskT", [128, 128], BF16, kind="ExternalInput")
    ctx.d_ident = nc.dram_tensor("ident", [128, 128], F32, kind="ExternalInput")
    ctx.xout = nc.dram_tensor("xout", [2, T, D], F32, kind="ExternalOutput")
    P = Prog(nc, st)
    setup_common(nc, st, ctx, P)
    prev_store = None
    for seq in range(2):
        exl = emit_load_x(P, ctx, seq, waits=[prev_store])
        out_evs = emit_l0a(P, ctx, seq, exl)
        prev_store = emit_store_x(P, ctx, seq, waits=out_evs)
    P.op("sync", lambda E: E.nop(), waits=[prev_store])
    P.replay()
    st.close()
    return nc


def _build_l0f():
    nc = bass.Bass("TRN2", target_bir_lowering=False)
    st = ExitStack()
    ctx = Ctx(nc)
    ctx.d_x0 = nc.dram_tensor("x0", [2, T, D], F32, kind="ExternalInput")
    ctx.d_wqkv = nc.dram_tensor("wqkv", [6, 1024, 3072], F32, kind="ExternalInput")
    ctx.d_wo0 = nc.dram_tensor("wo0", [6, 1024, 1024], F32, kind="ExternalInput")
    ctx.d_w1 = nc.dram_tensor("w1", [6, 1024, 5460], F32, kind="ExternalInput")
    ctx.d_b1 = nc.dram_tensor("b1", [6, 5460], F32, kind="ExternalInput")
    ctx.d_w2 = nc.dram_tensor("w2", [6, 2730, 1024], F32, kind="ExternalInput")
    ctx.d_b2 = nc.dram_tensor("b2", [6, 1024], F32, kind="ExternalInput")
    ctx.d_lng = nc.dram_tensor("lng", [6, 128, 1024], F32, kind="ExternalInput")
    ctx.d_lnb = nc.dram_tensor("lnb", [6, 128, 1024], F32, kind="ExternalInput")
    ctx.d_qs4 = nc.dram_tensor("qs4", [6, 4, 128, 64], F32, kind="ExternalInput")
    ctx.d_rotab = nc.dram_tensor("rotab", [4, 128, T // 128, 64], F32, kind="ExternalInput")
    ctx.d_lmask = nc.dram_tensor("lmask", [128, 384], F32, kind="ExternalInput")
    ctx.d_ident = nc.dram_tensor("ident", [128, 128], F32, kind="ExternalInput")
    ctx.xout = nc.dram_tensor("xout", [2, T, D], F32, kind="ExternalOutput")
    xs1 = nc.dram_tensor("xs1", [2, T, D], F32)
    xs2 = nc.dram_tensor("xs2", [2, T, D], F32)
    P = Prog(nc, st)
    setup_common(nc, st, ctx, P, need_xtm=False)
    prev = {0: None, 1: None}
    for seq in range(2):
        for li in range(6):
            ctx.d_xcur = ctx.d_x0 if li == 0 else (xs1 if li % 2 == 1 else xs2)
            ctx.d_xnxt = xs1 if li % 2 == 0 else xs2
            evs = emit_l0f_layer(P, ctx, seq, li, final_ln=(li == 5), prev_final=prev[seq])
            prev[seq] = evs
    P.op("sync", lambda E: E.nop(), waits=[prev[0][-1], prev[1][-1]])
    P.replay()
    st.close()
    return nc


def _build_l1a():
    nc = bass.Bass("TRN2", target_bir_lowering=False)
    st = ExitStack()
    ctx = Ctx(nc)
    ctx.xin = nc.dram_tensor("xin", [2, T, D], F32, kind="ExternalInput")
    ctx.d_wq = nc.dram_tensor("wq", [1024, 1024], F32, kind="ExternalInput")
    ctx.d_wk = nc.dram_tensor("wk", [1024, 1024], F32, kind="ExternalInput")
    ctx.d_wv = nc.dram_tensor("wv", [1024, 1024], F32, kind="ExternalInput")
    ctx.d_bqkv = nc.dram_tensor("bqkv", [1, 3 * 1024], F32, kind="ExternalInput")
    ctx.d_ident = nc.dram_tensor("ident", [128, 128], F32, kind="ExternalInput")
    ctx.xout = nc.dram_tensor("xout", [2, T, D], F32, kind="ExternalOutput")
    P = Prog(nc, st)
    setup_common(nc, st, ctx, P)
    prev_store = None
    for seq in range(2):
        exl = emit_load_x(P, ctx, seq, waits=[prev_store])
        out_evs = emit_l1a(P, ctx, seq, exl)
        prev_store = emit_store_x(P, ctx, seq, waits=out_evs)
    P.op("sync", lambda E: E.nop(), waits=[prev_store])
    P.replay()
    st.close()
    return nc


def _build_l1f():
    nc = bass.Bass("TRN2", target_bir_lowering=False)
    st = ExitStack()
    ctx = Ctx(nc)
    ctx.xin = nc.dram_tensor("xin", [2, T, D], F32, kind="ExternalInput")
    ctx.d_w1 = nc.dram_tensor("w1", [1024, 4096], F32, kind="ExternalInput")
    ctx.d_b1r = nc.dram_tensor("b1r", [1, 4096], F32, kind="ExternalInput")
    ctx.d_w2 = nc.dram_tensor("w2", [4096, 1024], F32, kind="ExternalInput")
    ctx.d_b2r = nc.dram_tensor("b2r", [1, 1024], F32, kind="ExternalInput")
    ctx.d_ident = nc.dram_tensor("ident", [128, 128], F32, kind="ExternalInput")
    ctx.xout = nc.dram_tensor("xout", [2, T, D], F32, kind="ExternalOutput")
    P = Prog(nc, st)
    setup_common(nc, st, ctx, P)
    prev_store = None
    for seq in range(2):
        exl = emit_load_x(P, ctx, seq, waits=[prev_store])
        out_evs = emit_l1f(P, ctx, seq, exl)
        prev_store = emit_store_x(P, ctx, seq, waits=out_evs)
    P.op("sync", lambda E: E.nop(), waits=[prev_store])
    P.replay()
    st.close()
    return nc


def _run(nc, shared, xs, xkey="xin"):
    in_maps = [dict(shared, **{xkey: x}) for x in xs]
    res = run_bass_kernel_spmd(nc, in_maps, core_ids=list(range(N_CORES)))
    return [r["xout"] for r in res.results]


def kernel_unfused(**inputs):
    i = inputs
    ident = np.eye(128, dtype=np.float32)
    x = np.ascontiguousarray(i["L0_attn_x"])
    xs = [np.ascontiguousarray(x[2 * c:2 * c + 2]) for c in range(N_CORES)]

    wstack = np.ascontiguousarray(np.stack([
        i["L0_attn_Wq"].T, i["L0_attn_Wk"].T, i["L0_attn_Wv"].T, i["L0_attn_Wo"].T]))
    l0ab = np.ascontiguousarray(np.concatenate(
        [i["L0_attn_bq"], i["L0_attn_bk"], i["L0_attn_bv"], i["L0_attn_bo"]])[None, :])
    maskT = np.triu(np.ones((128, 128))).astype(ml_dtypes.bfloat16)
    shared = {"l0aw": wstack, "l0ab": l0ab, "maskT": maskT, "ident": ident}
    xs = _run(_build_l0a(), shared, xs)

    shared = {
        "wqkv": np.ascontiguousarray(i["L0_ffn_Wqkv"]),
        "wo0": np.ascontiguousarray(i["L0_ffn_Wo"]),
        "w1": np.ascontiguousarray(i["L0_ffn_W1"]),
        "b1": np.ascontiguousarray(i["L0_ffn_b1"]),
        "w2": np.ascontiguousarray(i["L0_ffn_W2"]),
        "b2": np.ascontiguousarray(i["L0_ffn_b2"]),
        "lng": np.ascontiguousarray(np.tile(np.asarray(i["L0_ffn_ln_g"])[:, None, :], (1, 128, 1))),
        "lnb": np.ascontiguousarray(np.tile(np.asarray(i["L0_ffn_ln_b"])[:, None, :], (1, 128, 1))),
        "qs4": host_qs4(np.asarray(i["L0_ffn_q_scale"]), np.asarray(i["L0_ffn_k_scale"])),
        "rotab": host_rotab(T),
        "lmask": host_lmask(),
        "ident": ident,
    }
    xs = _run(_build_l0f(), shared, xs, xkey="x0")

    shared = {
        "wq": np.ascontiguousarray(i["L1_attn_Wq"]),
        "wk": np.ascontiguousarray(i["L1_attn_Wk"]),
        "wv": np.ascontiguousarray(i["L1_attn_Wv"]),
        "bqkv": np.ascontiguousarray(np.concatenate(
            [i["L1_attn_bq"], i["L1_attn_bk"], i["L1_attn_bv"]])[None, :]),
        "ident": ident,
    }
    xs = _run(_build_l1a(), shared, xs)

    shared = {
        "w1": np.ascontiguousarray(i["L1_ffn_w1"]),
        "b1r": np.ascontiguousarray(np.asarray(i["L1_ffn_b1"])[None, :]),
        "w2": np.ascontiguousarray(i["L1_ffn_w2"]),
        "b2r": np.ascontiguousarray(np.asarray(i["L1_ffn_b2"])[None, :]),
        "ident": ident,
    }
    xs = _run(_build_l1f(), shared, xs)

    return np.concatenate(xs, axis=0).astype(np.float32)


def _build_fused():
    nc = bass.Bass("TRN2", target_bir_lowering=False)
    st = ExitStack()
    ctx = Ctx(nc)
    ctx.d_x0in = nc.dram_tensor("x0in", [2, T, D], F32, kind="ExternalInput")
    ctx.d_l0a_w = nc.dram_tensor("l0aw", [4, 1024, 1024], F32, kind="ExternalInput")
    ctx.d_l0a_b = nc.dram_tensor("l0ab", [1, 4 * 1024], F32, kind="ExternalInput")
    ctx.d_maskT = nc.dram_tensor("maskT", [128, 128], BF16, kind="ExternalInput")
    ctx.d_wqkv = nc.dram_tensor("wqkv", [6, 1024, 3072], F32, kind="ExternalInput")
    ctx.d_wo0 = nc.dram_tensor("wo0", [6, 1024, 1024], F32, kind="ExternalInput")
    ctx.d_w1 = nc.dram_tensor("w1", [6, 1024, 5460], F32, kind="ExternalInput")
    ctx.d_b1 = nc.dram_tensor("b1", [6, 5460], F32, kind="ExternalInput")
    ctx.d_w2 = nc.dram_tensor("w2", [6, 2730, 1024], F32, kind="ExternalInput")
    ctx.d_b2 = nc.dram_tensor("b2", [6, 1024], F32, kind="ExternalInput")
    ctx.d_lng = nc.dram_tensor("lng", [6, 128, 1024], F32, kind="ExternalInput")
    ctx.d_lnb = nc.dram_tensor("lnb", [6, 128, 1024], F32, kind="ExternalInput")
    ctx.d_qs4 = nc.dram_tensor("qs4", [6, 4, 128, 64], F32, kind="ExternalInput")
    ctx.d_rotab = nc.dram_tensor("rotab", [4, 128, T // 128, 64], F32, kind="ExternalInput")
    ctx.d_lmask = nc.dram_tensor("lmask", [128, 384], F32, kind="ExternalInput")
    ctx.d_wq = nc.dram_tensor("wq", [1024, 1024], F32, kind="ExternalInput")
    ctx.d_wk = nc.dram_tensor("wk", [1024, 1024], F32, kind="ExternalInput")
    ctx.d_wv = nc.dram_tensor("wv", [1024, 1024], F32, kind="ExternalInput")
    ctx.d_bqkv = nc.dram_tensor("bqkv", [1, 3 * 1024], F32, kind="ExternalInput")
    ctx.d_w1b = nc.dram_tensor("w1b", [1024, 4096], F32, kind="ExternalInput")
    ctx.d_b1rb = nc.dram_tensor("b1rb", [1, 4096], F32, kind="ExternalInput")
    ctx.d_w2b = nc.dram_tensor("w2b", [4096, 1024], F32, kind="ExternalInput")
    ctx.d_b2rb = nc.dram_tensor("b2rb", [1, 1024], F32, kind="ExternalInput")
    ctx.d_ident = nc.dram_tensor("ident", [128, 128], F32, kind="ExternalInput")
    real_out = nc.dram_tensor("xout", [2, T, D], F32, kind="ExternalOutput")
    xa = nc.dram_tensor("xa", [2, T, D], F32)
    xs1 = nc.dram_tensor("xs1", [2, T, D], F32)
    xs2 = nc.dram_tensor("xs2", [2, T, D], F32)
    xb = nc.dram_tensor("xb", [2, T, D], F32)

    P = Prog(nc, st)
    setup_common(nc, st, ctx, P, need_xtm=False)

    ctx.x_tm = ctx.alloc("x_tm", [128, NTC * D], F32)
    ctx.xin = ctx.d_x0in
    ctx.xout = xa
    prev_store = None
    for seq in range(2):
        exl = emit_load_x(P, ctx, seq, waits=[prev_store])
        out_evs = emit_l0a(P, ctx, seq, exl)
        prev_store = emit_store_x(P, ctx, seq, waits=out_evs)
    ctx.free(ctx.x_tm)
    ctx.x_tm = None
    ctx.new_phase(P)

    ctx.d_x0 = xa
    ctx.xout = xb
    prev = {0: None, 1: None}
    for seq in range(2):
        for li in range(6):
            ctx.d_xcur = xa if li == 0 else (xs1 if li % 2 == 1 else xs2)
            ctx.d_xnxt = xs1 if li % 2 == 0 else xs2
            evs = emit_l0f_layer(P, ctx, seq, li, final_ln=(li == 5), prev_final=prev[seq])
            prev[seq] = evs

    ctx.x_tm = ctx.alloc("x_tm2", [128, NTC * D], F32)
    ctx.xin = xb
    ctx.xout = real_out
    ctx.d_w1, ctx.d_b1r, ctx.d_w2, ctx.d_b2r = ctx.d_w1b, ctx.d_b1rb, ctx.d_w2b, ctx.d_b2rb
    prev_store = None
    for seq in range(2):
        exl = emit_load_x(P, ctx, seq, waits=[prev_store, prev[seq]], src=xb)
        evs_a = emit_l1a(P, ctx, seq, exl)
        evs_f = emit_l1f(P, ctx, seq, evs_a)
        prev_store = emit_store_x(P, ctx, seq, waits=evs_f, dst=real_out)
    P.op("sync", lambda E: E.nop(), waits=[prev_store])
    P.replay()
    st.close()
    return nc


def kernel(**inputs):
    i = inputs
    ident = np.eye(128, dtype=np.float32)
    x = np.ascontiguousarray(i["L0_attn_x"])
    xs = [np.ascontiguousarray(x[2 * c:2 * c + 2]) for c in range(N_CORES)]
    wstack = np.ascontiguousarray(np.stack([
        i["L0_attn_Wq"].T, i["L0_attn_Wk"].T, i["L0_attn_Wv"].T, i["L0_attn_Wo"].T]))
    shared = {
        "l0aw": wstack,
        "l0ab": np.ascontiguousarray(np.concatenate(
            [i["L0_attn_bq"], i["L0_attn_bk"], i["L0_attn_bv"], i["L0_attn_bo"]])[None, :]),
        "maskT": np.triu(np.ones((128, 128))).astype(ml_dtypes.bfloat16),
        "wqkv": np.ascontiguousarray(i["L0_ffn_Wqkv"]),
        "wo0": np.ascontiguousarray(i["L0_ffn_Wo"]),
        "w1": np.ascontiguousarray(i["L0_ffn_W1"]),
        "b1": np.ascontiguousarray(i["L0_ffn_b1"]),
        "w2": np.ascontiguousarray(i["L0_ffn_W2"]),
        "b2": np.ascontiguousarray(i["L0_ffn_b2"]),
        "lng": np.ascontiguousarray(np.tile(np.asarray(i["L0_ffn_ln_g"])[:, None, :], (1, 128, 1))),
        "lnb": np.ascontiguousarray(np.tile(np.asarray(i["L0_ffn_ln_b"])[:, None, :], (1, 128, 1))),
        "qs4": host_qs4(np.asarray(i["L0_ffn_q_scale"]), np.asarray(i["L0_ffn_k_scale"])),
        "rotab": host_rotab(T),
        "lmask": host_lmask(),
        "wq": np.ascontiguousarray(i["L1_attn_Wq"]),
        "wk": np.ascontiguousarray(i["L1_attn_Wk"]),
        "wv": np.ascontiguousarray(i["L1_attn_Wv"]),
        "bqkv": np.ascontiguousarray(np.concatenate(
            [i["L1_attn_bq"], i["L1_attn_bk"], i["L1_attn_bv"]])[None, :]),
        "w1b": np.ascontiguousarray(i["L1_ffn_w1"]),
        "b1rb": np.ascontiguousarray(np.asarray(i["L1_ffn_b1"])[None, :]),
        "w2b": np.ascontiguousarray(i["L1_ffn_w2"]),
        "b2rb": np.ascontiguousarray(np.asarray(i["L1_ffn_b2"])[None, :]),
        "ident": ident,
    }
    nc = _build_fused()
    in_maps = [dict(shared, x0in=s) for s in xs]
    res = run_bass_kernel_spmd(nc, in_maps, core_ids=list(range(N_CORES)))
    return np.concatenate([r["xout"] for r in res.results], axis=0).astype(np.float32)


# version 18
# speedup vs baseline: 1.1672x; 1.0101x over previous
import numpy as np
import ml_dtypes
from contextlib import ExitStack
import concourse.bass as bass
import concourse.mybir as mybir
from concourse.bass_utils import run_bass_kernel_spmd

"""Core framework + slot emitters for the mk_seq_postln_d2 megakernel.

Layout conventions:
  - token-major (tm): [tokens(part) x features(free)]; residual x lives here, f32.
    x_tm sbuf tensor [128, NTC*D]: t-chunk tc at cols tc*D.
  - feature-major (fm): [features(part) x tokens(free)] bf16, matmul operand.
    fm tensor [128, n_fc*Tpad]: f-chunk fc at cols fc*Tpad (+pad lead zero cols).
  - matmul: out[M,N] = lhsT.T @ rhs, lhsT [K<=128, M<=128], rhs [K, N<=512].
    * fm emission: lhsT = W[fi,fo] natural, rhs = act_fm -> out [fo, t]
    * tm emission: lhsT = act_fm[fi, tchunk], rhs = W[fi, fo] -> out [t, fo]
  - biases live as [1, len] f32 rows; injected into PSUM with K=1 matmuls:
    * fm emission: lhsT = b_row[0:1, sl], rhs = ones_row[0:1, 0:N]
    * tm emission: lhsT = ones_row[0:1, 0:128], rhs = b_row[0:1, sl]

Sync model: every instruction increments its engine's progress semaphore
(+1 compute, +16 DMA); Ev(sem,count) events; consumers wait explicitly.
Same-engine RAW REQUIRES an event wait (hardware pipelines do not interlock).
"""

F32 = mybir.dt.float32
BF16 = mybir.dt.bfloat16
AF = mybir.ActivationFunctionType
ALU = mybir.AluOpType
AX = mybir.AxisListType

D = 1024
T = 2048
NTC = T // 128
NFC = D // 128

ENGINES = ("tensor", "vector", "scalar", "gpsimd", "sync")


class Ev:
    __slots__ = ("sem", "count")

    def __init__(self, sem, count):
        self.sem = sem
        self.count = count


class Prog:
    def __init__(self, nc, st: ExitStack):
        self.nc = nc
        self.ops = {e: [] for e in ENGINES}
        self.sems = {}
        self.counts = {}
        for e in ENGINES:
            s = st.enter_context(nc.semaphore(f"sp_{e}"))
            self.sems[e] = s
            self.counts[e] = 0
        self.dsems = {}
        self.dcounts = {}
        for e in ("sync", "scalar", "gpsimd", "vector"):
            self.dsems[e] = [st.enter_context(nc.semaphore(f"sd_{e}{j}")) for j in range(4)]
            self.dcounts[e] = [0, 0, 0, 0]
            setattr(self, "_rr_" + e, 0)

    @staticmethod
    def _flat(waits):
        out = []
        for w in waits:
            if w is None:
                continue
            if isinstance(w, (list, tuple)):
                out.extend(x for x in w if x is not None)
            else:
                out.append(w)
        return out

    def op(self, eng, fn, waits=()):
        for w in self._flat(waits):
            self.ops[eng].append(("w", w.sem, w.count))
        sem = self.sems[eng]
        self.counts[eng] += 1
        self.ops[eng].append(("op", fn, sem, 1))
        return Ev(sem, self.counts[eng])

    def dma(self, eng, out_ap, in_ap, waits=(), **kw):
        for w in self._flat(waits):
            self.ops[eng].append(("w", w.sem, w.count))
        j = getattr(self, "_rr_" + eng) % 4
        setattr(self, "_rr_" + eng, j + 1)
        sem = self.dsems[eng][j]
        if self.dcounts[eng][j] > 0:
            self.ops[eng].append(("w", sem, self.dcounts[eng][j]))
        self.dcounts[eng][j] += 16
        self.ops[eng].append(("op", lambda E: E.dma_start(out_ap, in_ap, **kw), sem, 16))
        return Ev(sem, self.dcounts[eng][j])

    def barrier_ev(self, eng):
        return Ev(self.sems[eng], self.counts[eng])

    def full_barrier(self):
        evs = [Ev(self.sems[e], self.counts[e]) for e in ENGINES if self.counts[e] > 0]
        for e in self.dsems:
            for j in range(4):
                if self.dcounts[e][j] > 0:
                    evs.append(Ev(self.dsems[e][j], self.dcounts[e][j]))
        return evs

    def replay(self):
        nc = self.nc
        with nc.Block() as block:
            for eng in ENGINES:
                seq = self.ops[eng]

                def make(seq):
                    def body(E):
                        waited = {}
                        for item in seq:
                            if item[0] == "w":
                                _, sem, count = item
                                key = id(sem)
                                if waited.get(key, -1) < count:
                                    E.wait_ge(sem, count)
                                    waited[key] = count
                            else:
                                _, fn, sem, inc = item
                                fn(E).then_inc(sem, inc)
                    return body

                getattr(block, eng)(make(seq))


class Ctx:

    def __init__(self, nc):
        self.nc = nc
        self.pb = []
        self.pb_rr = 0
        self.pbt = []
        self.pbt_rr = 0
        self.pb_ev = {}
        self.gates = {}
        self.phase_list = []
        self._cms = {}

    def alloc(self, name, shape, dtype):
        self._uid = getattr(self, "_uid", 0) + 1
        cm = self.nc.sbuf_tensor(f"{name}_{self._uid}", shape, dtype)
        h = cm.__enter__()
        self._cms[id(h)] = cm
        return h

    def free(self, *hs):
        for h in reversed(hs):
            self._cms.pop(id(h)).__exit__(None, None, None)

    def bank(self):
        b = self.pb[self.pb_rr % len(self.pb)]
        self.pb_rr += 1
        return b, self.pb_ev.get(id(b))

    def bankt(self):
        b = self.pbt[self.pbt_rr % len(self.pbt)]
        self.pbt_rr += 1
        return b, self.pb_ev.get(id(b))

    def gate(self, name):
        return self.gates.get(name, self.phase_list)

    def new_phase(self, P):
        self.gates = {}
        self.phase_list = P.full_barrier()

    def set_gate(self, name, ev):
        self.gates[name] = ev


def ceil_div(a, b):
    return (a + b - 1) // b


def _copy(E, out, in_):
    if hasattr(E, "tensor_copy"):
        return E.tensor_copy(out, in_)
    return E.copy(out, in_)



def emit_load_x(P, ctx, seq, waits=(), src=None):
    xin = src if src is not None else ctx.xin
    ap = xin[seq].rearrange("(tc p) d -> p tc d", p=128)
    dst = ctx.x_tm[:].rearrange("p (tc d) -> p tc d", tc=NTC)
    return P.dma("scalar", dst, ap, waits=list(waits))


def emit_store_x(P, ctx, seq, waits, dst=None):
    xout = dst if dst is not None else ctx.xout
    ap = xout[seq].rearrange("(tc p) d -> p tc d", p=128)
    src = ctx.x_tm[:].rearrange("p (tc d) -> p tc d", tc=NTC)
    return P.dma("scalar", ap, src, waits=list(waits))


def emit_tm_to_fm(P, ctx, x_ready, dst_fm, Tpad, pad=0, evac_eng="scalar"):
    x_tm = ctx.x_tm
    ident = ctx.ident_f32
    ev = None
    for fc in range(NFC):
        for tg in range(NTC // 4):
            bank, bfree = ctx.bank()
            mev = None
            for j, tc in enumerate(range(tg * 4, tg * 4 + 4)):
                def mk(fc=fc, tc=tc, j=j, bank=bank, x_tm=x_tm, ident=ident):
                    def f(E):
                        return E.transpose(
                            bank[:, j * 128:(j + 1) * 128],
                            x_tm[:, tc * D + fc * 128: tc * D + fc * 128 + 128],
                            ident[:])
                    return f
                w = [x_ready, bfree, ctx.ident_ready] if j == 0 else []
                mev = P.op("tensor", mk(), waits=w)

            def mkev(fc=fc, tg=tg, bank=bank, dst_fm=dst_fm):
                def f(E):
                    return _copy(E,
                        dst_fm[:, fc * Tpad + pad + tg * 512: fc * Tpad + pad + tg * 512 + 512],
                        bank[:, 0:512])
                return f
            ev = P.op(evac_eng, mkev(), waits=[mev])
            ctx.pb_ev[id(bank)] = ev
    return ev


def emit_ln_inplace(P, ctx, tc, dep_ev, eps=1e-5):
    x = ctx.x_tm
    sl = x[:, tc * D:(tc + 1) * D]
    st = ctx.stat[:, tc * 8:(tc + 1) * 8]
    s1, s2, m, w, u, r, ri, s2b = (st[:, i:i + 1] for i in range(8))
    dump = ctx.dump
    e1 = P.op("vector", lambda E, o=s1, i=sl: E.reduce_sum(o, i, AX.X), waits=[dep_ev])
    e2a = P.op("scalar", lambda E, o=dump, i=sl, a=s2: E.activation(o[:, 0:512], i[:, 0:512], AF.Square, accum_out=a),
               waits=[dep_ev, ctx.gate("dump")])
    e2b = P.op("scalar", lambda E, o=dump, i=sl, a=s2b: E.activation(o[:, 0:512], i[:, 512:1024], AF.Square, accum_out=a),
               waits=[dep_ev, e2a])
    ctx.set_gate("dump", e2b)
    e3 = P.op("gpsimd", lambda E, o=m, i=s1: E.tensor_scalar(o, i, 1.0 / D, None, ALU.mult), waits=[e1])
    e4a = P.op("gpsimd", lambda E, o=w, a=s2, b=s2b: E.tensor_add(o, a, b), waits=[e2a, e2b])
    e4 = P.op("gpsimd", lambda E, o=w: E.tensor_scalar(o, o, 1.0 / D, None, ALU.mult), waits=[e4a])
    e5 = P.op("gpsimd", lambda E, o=u, i=m: E.tensor_mul(o, i, i), waits=[e3])
    e6 = P.op("gpsimd", lambda E, o=w, a=u: E.tensor_sub(o, o, a), waits=[e4, e5])
    e7 = P.op("scalar", lambda E, o=r, i=w: E.activation(o, i, AF.Sqrt, bias=ctx.eps_col[:, 0:1]),
              waits=[e6, ctx.eps_ready])
    e8 = P.op("vector", lambda E, o=ri, i=r: E.reciprocal(o, i), waits=[e7])
    e9 = P.op("vector", lambda E, o=sl, a=m, b=ri: E.tensor_scalar(o, o, a, b, ALU.subtract, ALU.mult),
              waits=[e8, e3, dep_ev])
    return e9


def mm(P, ctx, bank, lhsT, rhs, n, start, stop, waits=()):
    def f(E, bank=bank, lhsT=lhsT, rhs=rhs, n=n, start=start, stop=stop):
        return E.matmul(bank[:, 0:n], lhsT, rhs, start=start, stop=stop)
    return P.op("tensor", f, waits=waits)



def emit_l1f(P, ctx, seq, x_ready):
    x_tm = ctx.x_tm
    x_fm = ctx.alloc("x_fm", [128, NFC * T], BF16)
    h_fm = ctx.alloc("h_fm", [128, 8 * T], BF16)
    dump = ctx.dump = ctx.alloc("dump", [128, 512], F32)
    w1_sb = ctx.alloc("w1_sb", [128, 8 * 1024], BF16)
    w2_sb = ctx.alloc("w2_sb", [128, 8 * 1024], BF16)
    b1_row = ctx.alloc("b1_row", [1, 4096], BF16)
    b2_row = ctx.alloc("b2_row", [1, 1024], BF16)

    fm_done = emit_tm_to_fm(P, ctx, x_ready, x_fm, T)
    eb1 = P.dma("gpsimd", b1_row[0:1, 0:4096], ctx.d_b1r[:, :], waits=[ctx.gate("rows")])
    eb2 = P.dma("gpsimd", b2_row[0:1, 0:D], ctx.d_b2r[:, :])

    add_evs = []
    for qu in range(4):
        ew1 = P.dma("gpsimd",
                    w1_sb[:].rearrange("p (ci n) -> p ci n", ci=8),
                    ctx.d_w1[:, qu * 1024:(qu + 1) * 1024].rearrange("(ci p) n -> p ci n", p=128),
                    waits=[ctx.gate("w1_free")])
        ew2 = P.dma("gpsimd",
                    w2_sb[:].rearrange("p (ci n) -> p ci n", ci=8),
                    ctx.d_w2[qu * 1024:(qu + 1) * 1024, :].rearrange("(ci p) n -> p ci n", p=128),
                    waits=[ctx.gate("w2_free")])

        h_last = None
        for fc in range(8):
            for tg in range(T // 512):
                bank, bfree = ctx.bank()
                for fi in range(8):
                    w = [ew1, fm_done, bfree, ctx.gate("h_fm")] if fi == 0 else []
                    mm(P, ctx, bank,
                       w1_sb[:, fi * 1024 + fc * 128: fi * 1024 + fc * 128 + 128],
                       x_fm[:, fi * T + tg * 512: fi * T + tg * 512 + 512],
                       512, fi == 0, False, waits=w)
                mev = mm(P, ctx, bank,
                         b1_row[0:1, qu * 1024 + fc * 128: qu * 1024 + fc * 128 + 128],
                         ctx.ones_rowb[0:1, 0:512], 512, False, True,
                         waits=[eb1, ctx.onesb_ready])

                def mkev(fc=fc, tg=tg, bank=bank, h_fm=h_fm):
                    def f(E):
                        return E.activation(
                            h_fm[:, fc * T + tg * 512: fc * T + tg * 512 + 512],
                            bank[:, 0:512], AF.Relu)
                    return f
                h_last = P.op("scalar", mkev(), waits=[mev])
                ctx.pb_ev[id(bank)] = h_last
        ctx.set_gate("w1_free", P.barrier_ev("tensor"))

        for tc in range(NTC):
            for no in range(2):
                bank, bfree = ctx.bank()
                for fi in range(8):
                    w = [ew2, h_last, bfree] if fi == 0 else []
                    mm(P, ctx, bank,
                       h_fm[:, fi * T + tc * 128: fi * T + tc * 128 + 128],
                       w2_sb[:, fi * 1024 + no * 512: fi * 1024 + no * 512 + 512],
                       512, fi == 0, (fi == 7) and qu > 0, waits=w)
                if qu == 0:
                    mev = mm(P, ctx, bank,
                             ctx.ones_rowb[0:1, 0:128],
                             b2_row[0:1, no * 512: no * 512 + 512],
                             512, False, True, waits=[eb2])
                else:
                    mev = P.barrier_ev("tensor")

                def mkadd(tc=tc, no=no, bank=bank, x_tm=x_tm):
                    def f(E):
                        sl = x_tm[:, tc * D + no * 512: tc * D + no * 512 + 512]
                        return E.tensor_tensor(sl, sl, bank[:, 0:512], ALU.add)
                    return f
                prev_add = add_evs[qu * 32 + tc * 2 + no - 32] if qu > 0 else None
                aev = P.op("vector", mkadd(), waits=[mev, prev_add])
                ctx.pb_ev[id(bank)] = aev
                add_evs.append(aev)
        ctx.set_gate("w2_free", P.barrier_ev("tensor"))
        ctx.set_gate("h_fm", P.barrier_ev("tensor"))

    out_evs = []
    for tc in range(NTC):
        dep = add_evs[3 * 32 + tc * 2 + 1]
        out_evs.append(emit_ln_inplace(P, ctx, tc, dep))
    ctx.free(x_fm, h_fm, dump, w1_sb, w2_sb, b1_row, b2_row)
    ctx.new_phase(P)
    return out_evs



def emit_l1a(P, ctx, seq, x_ready):
    x_tm = ctx.x_tm
    q_fm = ctx.alloc("q_fm", [128, NFC * T], BF16)
    k_fm = ctx.alloc("k_fm", [128, NFC * T], BF16)
    v_tm = ctx.alloc("v_tm", [128, NTC * D], BF16)
    x_fm = ctx.alloc("x_fm", [128, NFC * T], BF16)
    w_sb = ctx.alloc("w_sb", [128, 8 * 512], BF16)
    b_row = ctx.alloc("b_row", [1, 1024], BF16)

    fm_done = emit_tm_to_fm(P, ctx, x_ready, x_fm, T)

    projs = [("q", ctx.d_wq, q_fm, "fm", 0), ("k", ctx.d_wk, k_fm, "fm", 1),
             ("v", ctx.d_wv, v_tm, "tm", 2)]
    for name, dw, dst, kind, bi in projs:
        ebr = P.dma("gpsimd", b_row[0:1, 0:D], ctx.d_bqkv[:, bi * D:(bi + 1) * D],
                    waits=[ctx.gate("w1_free"), ctx.gate("rows")])
        for foh in range(2):
            ew = P.dma("gpsimd", w_sb[:].rearrange("p (ci n) -> p ci n", ci=8),
                       dw[:, foh * 512:(foh + 1) * 512].rearrange("(ci p) n -> p ci n", p=128),
                       waits=[ctx.gate("w1_free")])
            if kind == "fm":
                for fo in range(foh * 4, foh * 4 + 4):
                    for tg in range(T // 512):
                        bank, bfree = ctx.bank()
                        for fi in range(8):
                            w = [ew, fm_done, bfree] if fi == 0 else []
                            mm(P, ctx, bank,
                               w_sb[:, fi * 512 + (fo - foh * 4) * 128: fi * 512 + (fo - foh * 4) * 128 + 128],
                               x_fm[:, fi * T + tg * 512: fi * T + tg * 512 + 512],
                               512, fi == 0, False, waits=w)
                        mev = mm(P, ctx, bank,
                                 b_row[0:1, fo * 128: fo * 128 + 128],
                                 ctx.ones_rowb[0:1, 0:512], 512, False, True,
                                 waits=[ebr, ctx.onesb_ready])

                        def mkev(fo=fo, tg=tg, bank=bank, dst=dst, name=name):
                            def f(E):
                                sl = dst[:, fo * T + tg * 512: fo * T + tg * 512 + 512]
                                if name == "q":
                                    return E.tensor_scalar(sl, bank[:, 0:512], 1.0 / 32, None, ALU.mult)
                                return E.tensor_copy(sl, bank[:, 0:512])
                            return f
                        ev = P.op("vector", mkev(), waits=[mev])
                        ctx.pb_ev[id(bank)] = ev
            else:
                no = foh
                for tc in range(NTC):
                    bank, bfree = ctx.bank()
                    for fi in range(8):
                        w = [ew, fm_done, bfree] if fi == 0 else []
                        mm(P, ctx, bank,
                           x_fm[:, fi * T + tc * 128: fi * T + tc * 128 + 128],
                           w_sb[:, fi * 512: fi * 512 + 512],
                           512, fi == 0, False, waits=w)
                    mev = mm(P, ctx, bank,
                             ctx.ones_rowb[0:1, 0:128],
                             b_row[0:1, no * 512: no * 512 + 512],
                             512, False, True, waits=[ebr, ctx.onesb_ready])

                    def mkev(tc=tc, no=no, bank=bank, dst=dst):
                        def f(E):
                            return _copy(E, dst[:, tc * D + no * 512: tc * D + no * 512 + 512],
                                         bank[:, 0:512])
                        return f
                    ev = P.op("scalar", mkev(), waits=[mev])
                    ctx.pb_ev[id(bank)] = ev
            ctx.set_gate("w1_free", P.barrier_ev("tensor"))
    proj_done = P.barrier_ev("tensor")
    ctx.free(x_fm, w_sb, b_row)
    o_sb = ctx.alloc("o_sb", [128, 2 * D], F32)
    p_sb = ctx.alloc("p_sb", [128, 2 * 512], F32)
    pt_sb = ctx.alloc("pt_sb", [128, 2 * 512], BF16)
    dump = ctx.dump = ctx.alloc("dump", [128, 512], F32)

    fl = ctx.stat
    out_evs = []
    state = {}

    def flash_step(qt, kb):
        st8 = fl[:, qt % 8 * 8: qt % 8 * 8 + 8]
        mneg = [st8[:, 0:1], st8[:, 1:2]]
        mpos_old = st8[:, 2:3]
        alpha = st8[:, 3:4]
        lsum = st8[:, 4:5]
        rs = st8[:, 5:6]
        qp = qt % 2
        ps = p_sb[:, qp * 512:(qp + 1) * 512]
        pts = pt_sb[:, qp * 512:(qp + 1) * 512]
        os_ = o_sb[:, qp * 1024:(qp + 1) * 1024]
        stv = state[qt]
        sbank, sfree = ctx.bank()
        for fi in range(8):
            w = [proj_done, sfree] if fi == 0 else []
            mm(P, ctx, sbank,
               q_fm[:, fi * T + qt * 128: fi * T + qt * 128 + 128],
               k_fm[:, fi * T + kb * 512: fi * T + kb * 512 + 512],
               512, fi == 0, fi == 7, waits=w)
        smev = P.barrier_ev("tensor")
        e_t = P.op("vector", lambda E, s=sbank, o=rs: E.reduce_max(o, s[:, 0:512], AX.X, negate=True),
                   waits=[smev, stv.get("small")])
        if kb == 0:
            e_m = P.op("vector", lambda E, o=mneg[0], i=rs: E.tensor_copy(o, i), waits=[e_t])
            cur = mneg[0]
            e_al = None
        else:
            old = mneg[(kb - 1) % 2]
            cur = mneg[kb % 2]
            e_mp = P.op("gpsimd", lambda E, o=mpos_old, i=old: E.tensor_scalar(o, i, -1.0, None, ALU.mult),
                        waits=[stv.get("small"), stv.get("em"), stv.get("eal")])
            e_m = P.op("vector", lambda E, o=cur, a=old, b=rs: E.tensor_tensor(o, a, b, ALU.min),
                       waits=[e_t])
            e_al = P.op("scalar", lambda E, o=alpha, i=cur, b=mpos_old: E.activation(o, i, AF.Exp, bias=b[:, 0:1]),
                        waits=[e_m, e_mp])
        e_p = P.op("scalar", lambda E, s=sbank, b=cur, p=ps, a=rs: E.activation(
            p[:, 0:512], s[:, 0:512], AF.Exp, bias=b[:, 0:1], accum_out=a),
            waits=[e_m, e_t, stv.get("tev")])
        ctx.pb_ev[id(sbank)] = e_p
        if kb == 0:
            e_l = P.op("gpsimd", lambda E, o=lsum, i=rs: E.tensor_copy(o, i), waits=[e_p])
        else:
            e_l = P.op("vector", lambda E, o=lsum, a=alpha, b=rs: E.scalar_tensor_tensor(
                o, o, a[:, 0:1], b, ALU.mult, ALU.add), waits=[e_p, e_al, stv.get("el")])
        tbank, tfree = ctx.bank()
        tev = None
        for j in range(4):
            w = [e_p, tfree, ctx.ident_ready] if j == 0 else []
            tev = P.op("tensor", lambda E, j=j, b=tbank, p=ps: E.transpose(
                b[:, j * 128:(j + 1) * 128], p[:, j * 128:(j + 1) * 128],
                ctx.ident_f32[:]), waits=w)
        e_pt = P.op("scalar", lambda E, b=tbank, o=pts: E.copy(o[:, 0:512], b[:, 0:512]),
                    waits=[tev, stv.get("pvm")])
        ctx.pb_ev[id(tbank)] = e_pt
        o_new = [None, None]
        pvml = None
        for no in range(2):
            pvb, pvfree = ctx.bank()
            for j in range(4):
                w = [e_pt, pvfree] if j == 0 else []
                mm(P, ctx, pvb,
                   pts[:, j * 128: j * 128 + 128],
                   v_tm[:, (kb * 4 + j) * D + no * 512: (kb * 4 + j) * D + no * 512 + 512],
                   512, j == 0, j == 3, waits=w)
            pvml = P.barrier_ev("tensor")
            if kb == 0:
                e_o = P.op("vector", lambda E, b=pvb, no=no, o=os_: E.tensor_copy(
                    o[:, no * 512:(no + 1) * 512], b[:, 0:512]), waits=[pvml, stv.get("eo%d" % no)])
            else:
                e_o = P.op("vector", lambda E, b=pvb, no=no, o=os_, a=alpha: E.scalar_tensor_tensor(
                    o[:, no * 512:(no + 1) * 512], o[:, no * 512:(no + 1) * 512],
                    a[:, 0:1], b[:, 0:512], ALU.mult, ALU.add),
                    waits=[pvml, e_al, stv.get("eo%d" % no)])
            ctx.pb_ev[id(pvb)] = e_o
            o_new[no] = e_o
        state[qt] = {"small": P.barrier_ev("gpsimd"), "em": e_m, "eal": e_al,
                     "el": e_l, "tev": tev, "pvm": pvml,
                     "eo0": o_new[0], "eo1": o_new[1],
                     "lsum": lsum, "rs": rs, "os": os_}

    def flash_fin(qt):
        stv = state.pop(qt)
        st8 = fl[:, qt % 8 * 8: qt % 8 * 8 + 8]
        rcp = st8[:, 6:7]
        e_r = P.op("vector", lambda E, o=rcp, i=stv["lsum"]: E.reciprocal(o, i),
                   waits=[stv["el"], stv["eo0"], stv["eo1"]])
        e_x = P.op("vector", lambda E, qt=qt, o=stv["os"], r=rcp: E.scalar_tensor_tensor(
            ctx.x_tm[:, qt * D:(qt + 1) * D], o[:, 0:D], r[:, 0:1],
            ctx.x_tm[:, qt * D:(qt + 1) * D], ALU.mult, ALU.add), waits=[e_r])
        out_evs.append(emit_ln_inplace(P, ctx, qt, e_x))

    for qpair in range(NTC // 2):
        qa, qb = 2 * qpair, 2 * qpair + 1
        state[qa] = {}
        state[qb] = {}
        for kb in range(T // 512):
            flash_step(qa, kb)
            flash_step(qb, kb)
        flash_fin(qa)
        flash_fin(qb)
    ctx.free(q_fm, k_fm, v_tm, o_sb, p_sb, pt_sb, dump)
    ctx.new_phase(P)
    return out_evs


def setup_common(nc, st, ctx, P, need_xtm=True):
    ctx.x_tm = ctx.alloc("x_tm", [128, NTC * D], F32) if need_xtm else None
    ctx.ident_f32 = ctx.alloc("ident_sb", [128, 128], F32)
    ctx.stat = ctx.alloc("stat", [128, 128], F32)
    ctx.dump = None
    ctx.eps_col = ctx.alloc("eps_col", [128, 1], F32)
    for i in range(6):
        ctx.pb.append(st.enter_context(nc.psum_tensor("pb%d" % i, [128, 512], F32)))
    for i in range(2):
        ctx.pbt.append(st.enter_context(nc.psum_tensor("pbt%d" % i, [128, 1024], BF16)))
    ctx.ident_bf16 = ctx.alloc("ident_bf16", [128, 128], BF16)
    ctx.ones_col = ctx.alloc("ones_col", [128, 1], BF16)
    ctx.ident_ready = P.dma("sync", ctx.ident_f32[:], ctx.d_ident[:, :])
    if hasattr(ctx, "d_maskT"):
        ctx.maskT = ctx.alloc("maskTc", [128, 128], BF16)
        ctx.maskT_ready = P.dma("sync", ctx.maskT[:], ctx.d_maskT[:, :])
    ctx.identb_ready = P.op("vector", lambda E: E.tensor_copy(ctx.ident_bf16[:], ctx.ident_f32[:]),
                            waits=[ctx.ident_ready])
    ctx.ones_rowb = ctx.alloc("ones_rowb", [1, 512], BF16)
    ctx.onesb_ready = P.op("gpsimd", lambda E: E.memset(ctx.ones_rowb[:], 1.0))
    ctx.onesc_ready = P.op("gpsimd", lambda E: E.memset(ctx.ones_col[:], 1.0))
    ctx.eps_ready = P.op("gpsimd", lambda E: E.memset(ctx.eps_col[:], 1e-5))



def emit_l0a(P, ctx, seq, x_ready):
    x_tm = ctx.x_tm
    x_fm = ctx.alloc("x_fm", [128, NFC * 512], BF16)
    w_sb = ctx.alloc("w_sb", [128, 4 * 8 * 1024], BF16)
    b_row = ctx.alloc("b_row", [1, 4 * 1024], BF16)
    maskT = ctx.maskT
    qf = ctx.alloc("qf", [128, 8 * 512], BF16)
    kf = ctx.alloc("kf", [128, 8 * 512], BF16)
    vg = ctx.alloc("vg", [128, 4 * 1040], BF16)
    ktg = ctx.alloc("ktg", [128, 4 * 1024], BF16)
    szb = ctx.alloc("szb", [128, 8 * 65], BF16)
    szf = ctx.alloc("szf", [128, 8 * 65], F32)
    otm = ctx.alloc("otm", [128, 1024], BF16)
    aofc = ctx.alloc("aofc", [128, 8 * 128], BF16)
    smt = ctx.alloc("smt", [128, 8 * 128], BF16)
    es = ctx.alloc("es", [128, 512], F32)
    la = ctx.alloc("la", [128, 16], F32)
    dump = ctx.dump = ctx.alloc("dump", [128, 512], F32)

    ebr = P.dma("gpsimd", b_row[0:1, :], ctx.d_l0a_b[:, :], waits=[ctx.gate("rows")])
    emk = ctx.maskT_ready
    ew = [P.dma("gpsimd", w_sb[:, i * 8192:(i + 1) * 8192].rearrange("p (ci n) -> p ci n", ci=8),
                ctx.d_l0a_w[i].rearrange("(ci p) n -> p ci n", p=128),
                waits=[ctx.gate("l0aw%d" % i)]) for i in range(4)]
    ez = P.op("gpsimd", lambda E: E.memset(szb[:], 0.0), waits=[ctx.gate("szb")])
    ezf = P.op("gpsimd", lambda E: E.memset(szf[:], 0.0), waits=[ctx.gate("szb")])
    evo = P.op("gpsimd", lambda E: E.memset(
        vg[:].rearrange("p (s c) -> p s c", c=65)[:, :, 64:65], 1.0), waits=[ctx.gate("vg")])

    def szsb(h):
        return szb[(h % 2) * 64:(h % 2) * 64 + 64,
                   (h // 8) * 260 + ((h // 2) % 4) * 65: (h // 8) * 260 + ((h // 2) % 4) * 65 + 65]

    out_evs = []
    sz_evac = None
    sz_mm_done = None
    for tg in range(NTC // 4):
        fm_done = None
        for fc in range(NFC):
            bank, bfree = ctx.bank()
            mev = None
            for j, tc in enumerate(range(tg * 4, tg * 4 + 4)):
                def mkt(fc=fc, tc=tc, j=j, bank=bank):
                    def f(E):
                        return E.transpose(
                            bank[:, j * 128:(j + 1) * 128],
                            x_tm[:, tc * D + fc * 128: tc * D + fc * 128 + 128],
                            ctx.ident_f32[:])
                    return f
                w = [x_ready, bfree, ctx.ident_ready, ctx.gate("xfmg")] if j == 0 else []
                mev = P.op("tensor", mkt(), waits=w)

            def mkevb(fc=fc, bank=bank):
                def f(E):
                    return _copy(E, x_fm[:, fc * 512:(fc + 1) * 512], bank[:, 0:512])
                return f
            fm_done = P.op("scalar", mkevb(), waits=[mev])
            ctx.pb_ev[id(bank)] = fm_done
        pevs = {}
        for pi, dst in ((0, qf), (1, kf)):
            for fc in range(8):
                bank, bfree = ctx.bank()
                for fi in range(8):
                    w = [ew[pi], fm_done, bfree, ctx.gate("qkf")] if fi == 0 else []
                    mm(P, ctx, bank,
                       w_sb[:, pi * 8192 + fi * 1024 + fc * 128: pi * 8192 + fi * 1024 + fc * 128 + 128],
                       x_fm[:, fi * 512:(fi + 1) * 512],
                       512, fi == 0, False, waits=w)
                mev = mm(P, ctx, bank, b_row[0:1, pi * 1024 + fc * 128: pi * 1024 + fc * 128 + 128],
                         ctx.ones_rowb[0:1, 0:512], 512, False, True, waits=[ebr, ctx.onesb_ready])
                e1 = P.op("scalar", lambda E, b=bank, o=es: E.activation(o[:, 0:512], b[:, 0:512], AF.Exp),
                          waits=[mev, ctx.gate("es")])
                e2 = P.op("vector", lambda E, b=bank, o=dst, fc=fc: E.tensor_scalar(
                    o[:, fc * 512:(fc + 1) * 512], b[:, 0:512], 0.0, None, ALU.max),
                    waits=[mev, ctx.gate("qkf"), e1])
                e3 = P.op("vector", lambda E, o=dst, fc=fc, a=es: E.scalar_tensor_tensor(
                    o[:, fc * 512:(fc + 1) * 512], a[:, 0:512], 1.0,
                    o[:, fc * 512:(fc + 1) * 512], ALU.min, ALU.add),
                    waits=[e1, e2])
                ctx.set_gate("es", e3)
                ctx.pb_ev[id(bank)] = e3
                pevs[(pi, fc)] = e3
        for tci in range(4):
            for no in range(2):
                bank, bfree = ctx.bank()
                for fi in range(8):
                    w = [ew[2], fm_done, bfree, ctx.gate("vg")] if fi == 0 else []
                    mm(P, ctx, bank,
                       x_fm[:, fi * 512 + tci * 128: fi * 512 + tci * 128 + 128],
                       w_sb[:, 2 * 8192 + fi * 1024 + no * 512: 2 * 8192 + fi * 1024 + no * 512 + 512],
                       512, fi == 0, False, waits=w)
                mev = mm(P, ctx, bank, ctx.ones_rowb[0:1, 0:128],
                         b_row[0:1, 2 * 1024 + no * 512: 2 * 1024 + no * 512 + 512],
                         512, False, True, waits=[ebr, ctx.onesb_ready])
                def mkvev(b=bank, tci=tci, no=no):
                    def f(E):
                        dst = vg[:, tci * 1040 + no * 520: tci * 1040 + no * 520 + 520]
                        dst3 = dst.rearrange("p (h c) -> p h c", c=65)[:, :, 0:64]
                        return E.tensor_copy(dst3, b[:, 0:512].rearrange("p (h c) -> p h c", c=64))
                    return f
                ev = P.op("vector", mkvev(), waits=[mev, evo])
                ctx.pb_ev[id(bank)] = ev
                pevs[("v", tci, no)] = ev
        kt_evs = {}
        for tci in range(4):
            bank, bfree = ctx.bankt()
            tev = None
            for fc in range(8):
                w = [pevs[(1, fc)], bfree, ctx.gate("ktg")] if fc == 0 else [pevs[(1, fc)]]
                tev = P.op("tensor", lambda E, b=bank, fc=fc, tci=tci: E.transpose(
                    b[:, fc * 128:(fc + 1) * 128],
                    kf[:, fc * 512 + tci * 128: fc * 512 + tci * 128 + 128],
                    ctx.ident_bf16[:]), waits=w + ([ctx.identb_ready] if fc == 0 else []))
            ev = P.op("vector", lambda E, b=bank, tci=tci: E.tensor_copy(
                ktg[:, tci * 1024:(tci + 1) * 1024], b[:, 0:1024]), waits=[tev])
            ctx.pb_ev[id(bank)] = ev
            kt_evs[tci] = ev

        for tci in range(4):
            c = tg * 4 + tci
            smt_evs = {}
            num_done = []
            for h in range(16):
                fc, r0 = h // 2, (h % 2) * 64
                bank, bfree = ctx.bank()
                m1 = mm(P, ctx, bank,
                        kf[r0:r0 + 64, fc * 512 + tci * 128: fc * 512 + tci * 128 + 128],
                        qf[r0:r0 + 64, fc * 512 + tci * 128: fc * 512 + tci * 128 + 128],
                        128, True, True, waits=[pevs[(0, fc)], pevs[(1, fc)], bfree])
                smev = P.op("vector", lambda E, b=bank, h=h: E.tensor_tensor(
                    smt[:, (h % 8) * 128:(h % 8 + 1) * 128], b[:, 0:128], maskT[:], ALU.mult),
                    waits=[m1, emk, ctx.gate("smt%d" % (h % 8))])
                ctx.pb_ev[id(bank)] = smev
                smt_evs[h] = smev
                nbank, nbfree = ctx.bank()
                mmev0 = mm(P, ctx, nbank, smt[:, (h % 8) * 128:(h % 8 + 1) * 128],
                           vg[:, tci * 1040 + h * 65: tci * 1040 + h * 65 + 65],
                           65, True, False, waits=[smev, pevs[("v", tci, h // 8)], nbfree])
                def mksz(nb=nbank, h=h, fc=fc, r0=r0, tci=tci):
                    def f(E):
                        return E.matmul(nb[:, 0:65], qf[r0:r0 + 64, fc * 512 + tci * 128: fc * 512 + tci * 128 + 128],
                                        szsb(h), start=False, stop=True)
                    return f
                nev = P.op("tensor", mksz(), waits=[sz_evac, ez] if h == 0 else [])
                ctx.set_gate("smt%d" % (h % 8), nev)
                rev = P.op("vector", lambda E, nb=nbank, h=h: E.reciprocal(la[:, h:h + 1], nb[:, 64:65]),
                           waits=[nev])
                oev = P.op("vector", lambda E, nb=nbank, h=h: E.tensor_scalar(
                    otm[:, h * 64:(h + 1) * 64], nb[:, 0:64], la[:, h:h + 1], None, ALU.mult),
                    waits=[rev, ctx.gate("otm%d" % h)])
                ctx.pb_ev[id(nbank)] = oev
                ctx.set_gate("otm%d" % h, oev)
                num_done.append(oev)
            dbanks = []
            dmms = []
            for half in range(2):
                bk, bkfree = ctx.bank()
                dbanks.append(bk)
                for hh in range(8):
                    h = half * 8 + hh
                    r0b = (h % 2) * 64
                    off = ((h // 2) % 4) * 65
                    w = []
                    if hh == 0:
                        w = [kt_evs[tci], bkfree]
                    def mks(bk=bk, r0b=r0b, off=off, h=h, tci=tci):
                        def f(E):
                            return E.matmul(bk[r0b:r0b + 64, off:off + 65],
                                            ktg[:, tci * 1024 + h * 64: tci * 1024 + h * 64 + 64],
                                            vg[:, tci * 1040 + h * 65: tci * 1040 + h * 65 + 65],
                                            start=True, stop=True, tile_position=(0, r0b))
                        return f
                    dmms.append(P.op("tensor", mks(), waits=w))
            ad1 = P.op("vector", lambda E, b=dbanks[0]: E.tensor_add(szf[:, 0:260], szf[:, 0:260], b[:, 0:260]),
                       waits=[dmms[7], ezf, ctx.gate("szfa")])
            ad2 = P.op("vector", lambda E, b=dbanks[1]: E.tensor_add(szf[:, 260:520], szf[:, 260:520], b[:, 0:260]),
                       waits=[dmms[15], ctx.gate("szfb")])
            ctx.set_gate("szfa", ad1)
            ctx.set_gate("szfb", ad2)
            ctx.pb_ev[id(dbanks[0])] = ad1
            ctx.pb_ev[id(dbanks[1])] = ad2
            cp = P.op("vector", lambda E: E.tensor_copy(szb[:], szf[:]), waits=[ad1, ad2, ez])
            sz_evac = cp
            ctx.set_gate("szb", cp)
            bank, bfree = ctx.bankt()
            tev = None
            for fc2 in range(8):
                w = [num_done[fc2 * 2], num_done[fc2 * 2 + 1]]
                if fc2 == 0:
                    w = w + [bfree, ctx.gate("aofc"), ctx.identb_ready]
                tev = P.op("tensor", lambda E, b=bank, fc2=fc2: E.transpose(
                    b[:, fc2 * 128:(fc2 + 1) * 128],
                    otm[:, fc2 * 128:(fc2 + 1) * 128], ctx.ident_bf16[:]), waits=w)
            aev = P.op("vector", lambda E, b=bank: E.tensor_copy(aofc[:], b[:, 0:1024]),
                       waits=[tev])
            ctx.pb_ev[id(bank)] = aev
            add_evs = []
            for no in range(2):
                bank, bfree = ctx.bank()
                for fi in range(8):
                    w = [aev, ew[3], bfree] if fi == 0 else []
                    mm(P, ctx, bank, aofc[:, fi * 128:(fi + 1) * 128],
                       w_sb[:, 3 * 8192 + fi * 1024 + no * 512: 3 * 8192 + fi * 1024 + no * 512 + 512],
                       512, fi == 0, False, waits=w)
                mev = mm(P, ctx, bank, ctx.ones_rowb[0:1, 0:128],
                         b_row[0:1, 3 * 1024 + no * 512: 3 * 1024 + no * 512 + 512],
                         512, False, True, waits=[ebr, ctx.onesb_ready])
                def mkadd(c=c, no=no, bank=bank):
                    def f(E):
                        sl = x_tm[:, c * D + no * 512: c * D + no * 512 + 512]
                        return E.tensor_tensor(sl, sl, bank[:, 0:512], ALU.add)
                    return f
                ae = P.op("vector", mkadd(), waits=[mev])
                ctx.pb_ev[id(bank)] = ae
                add_evs.append(ae)
            ctx.set_gate("aofc", P.barrier_ev("tensor"))
            out_evs.append(emit_ln_inplace(P, ctx, c, add_evs[-1]))
        gbar = P.barrier_ev("tensor")
        ctx.set_gate("qkf", gbar)
        ctx.set_gate("vg", gbar)
        ctx.set_gate("ktg", gbar)
        ctx.set_gate("xfmg", gbar)

    for i in range(4):
        ctx.set_gate("l0aw%d" % i, P.barrier_ev("tensor"))
    ctx.free(x_fm, w_sb, b_row, qf, kf, vg, ktg, szb, szf, otm, aofc, smt, es, la, dump)
    ctx.new_phase(P)
    return out_evs



NHEAD = 16
W1HALF = [list(range(0, 11)), list(range(11, 22))]


def fc_rows(fc):
    return 42 if fc == 21 else 128


def emit_ln_buf(P, ctx, sl, slot, dep_ev, eps=1e-5):
    st = ctx.stat[:, slot * 8:(slot + 1) * 8]
    s1, s2, m, w, u, r, ri, s2b = (st[:, i:i + 1] for i in range(8))
    dump = ctx.dump
    gst = ctx.gate("lnst%d" % slot)
    e1 = P.op("vector", lambda E, o=s1, i=sl: E.reduce_sum(o, i, AX.X), waits=[dep_ev, gst])
    e2a = P.op("scalar", lambda E, o=dump, i=sl, a=s2: E.activation(o[:, 0:512], i[:, 0:512], AF.Square, accum_out=a),
               waits=[dep_ev, gst, ctx.gate("dump")])
    e2b = P.op("scalar", lambda E, o=dump, i=sl, a=s2b: E.activation(o[:, 0:512], i[:, 512:1024], AF.Square, accum_out=a),
               waits=[e2a])
    ctx.set_gate("dump", e2b)
    e3 = P.op("gpsimd", lambda E, o=m, i=s1: E.tensor_scalar(o, i, 1.0 / D, None, ALU.mult), waits=[e1])
    e4a = P.op("gpsimd", lambda E, o=w, a=s2, b=s2b: E.tensor_add(o, a, b), waits=[e2a, e2b])
    e4 = P.op("gpsimd", lambda E, o=w: E.tensor_scalar(o, o, 1.0 / D, None, ALU.mult), waits=[e4a])
    e5 = P.op("gpsimd", lambda E, o=u, i=m: E.tensor_mul(o, i, i), waits=[e3])
    e6 = P.op("gpsimd", lambda E, o=w, a=u: E.tensor_sub(o, o, a), waits=[e4, e5])
    e7 = P.op("scalar", lambda E, o=r, i=w: E.activation(o, i, AF.Sqrt, bias=ctx.eps_col[:, 0:1]),
              waits=[e6, ctx.eps_ready])
    e8 = P.op("vector", lambda E, o=ri, i=r: E.reciprocal(o, i), waits=[e7])
    e9 = P.op("vector", lambda E, o=sl, a=m, b=ri: E.tensor_scalar(o, o, a, b, ALU.subtract, ALU.mult),
              waits=[e8, e3, dep_ev])
    ctx.set_gate("lnst%d" % slot, e9)
    return e9


def emit_l0f_layer(P, ctx, seq, li, final_ln=False, prev_final=None):
    NT = NTC
    Tp = T + 1
    g = ctx.gate
    sg = ctx.set_gate

    xs_fm = ctx.alloc("xs_fm", [128, NFC * Tp], BF16)
    wqkv = ctx.alloc("wqkv", [128, 8 * 3072], BF16)
    wo = ctx.alloc("wo", [128, 8 * 1024], BF16)
    qkvg = ctx.alloc("qkvg", [128, 4 * 3072], BF16)
    qf = ctx.alloc("qf", [128, 2 * 1024], BF16)
    kf = ctx.alloc("kf", [128, 4 * 1024], BF16)
    vr = ctx.alloc("vr", [128, 4 * 1024], BF16)
    aofm = ctx.alloc("aofm", [128, 2 * 1024], BF16)
    tabs = ctx.alloc("tabs", [128, 4 * NT * 64], F32)
    reps = ctx.alloc("reps", [128, 4 * 64], F32)
    t1 = ctx.alloc("t1", [128, 4 * 1024], F32)
    t2 = ctx.alloc("t2", [128, 2 * 1024], F32)
    lmask = ctx.alloc("lmask", [128, 384], BF16)
    p_sb = ctx.alloc("p_sb", [128, 3 * 384], F32)
    p2_sb = ctx.alloc("p2_sb", [128, 4 * 384], BF16)
    diag = ctx.alloc("diag", [128, 4 * 128], BF16)
    pt_sb = ctx.alloc("pt_sb", [128, 2 * 384], BF16)
    xstage = ctx.alloc("xstage", [128, 2 * 1024], F32)
    rstd = ctx.alloc("rstd", [128, 4 * 16], F32)
    dump = ctx.dump = ctx.alloc("dumpA", [128, 128], F32)

    ewq = P.dma("gpsimd", wqkv[:].rearrange("p (ci n) -> p ci n", ci=8),
                ctx.d_wqkv[li].rearrange("(ci p) n -> p ci n", p=128), waits=[g("wqkv")])
    ewo = P.dma("gpsimd", wo[:].rearrange("p (ci n) -> p ci n", ci=8),
                ctx.d_wo0[li].rearrange("(ci p) n -> p ci n", p=128), waits=[g("wo")])
    etab = P.dma("sync", tabs[:].rearrange("p (j n) -> p j n", j=4),
                 ctx.d_rotab[:, :, :, :].rearrange("j p tc d -> p j (tc d)"), waits=[g("tabs")])
    erep = P.dma("sync", reps[:].rearrange("p (j n) -> p j n", j=4),
                 ctx.d_qs4[li].rearrange("j p d -> p j d"), waits=[g("tabs")])
    emsk = P.dma("gpsimd", lmask[:], ctx.d_lmask[:, :], waits=[g("lmask")])
    tab_evs = []
    for j in range(4):
        def mkc(j=j):
            def f(E):
                r = reps[:, j * 64:(j + 1) * 64]
                rb = bass.AP(r.tensor, r.offset, [[r.ap[0][0], 128], [0, NT], [1, 64]])
                t3 = tabs[:, j * NT * 64:(j + 1) * NT * 64].rearrange("p (tc d) -> p tc d", d=64)
                return E.tensor_tensor(t3, t3, rb, ALU.mult)
            return f
        tab_evs.append(P.op("gpsimd", mkc(), waits=[etab, erep]))
    tabs_done = tab_evs[-1]

    epad = P.op("gpsimd", lambda E: E.memset(
        xs_fm[:].rearrange("p (c t) -> p c t", c=NFC)[:, :, 0:1], 0.0), waits=[g("xsfm")])
    xev = {}
    fm_evs = []
    for tc in range(NT):
        sl = tc % 2
        w0 = [g("xst%d" % sl)]
        if prev_final is not None:
            w0.append(prev_final[tc])
        exl = P.dma("scalar", xstage[:, sl * 1024:(sl + 1) * 1024],
                    ctx.d_xcur[seq, tc * 128:(tc + 1) * 128, :], waits=w0)
        xev[tc] = exl
        for half in range(2):
            bank, bfree = ctx.bank()
            mev = None
            for j in range(4):
                fc = half * 4 + j
                def mkt(fc=fc, j=j, bank=bank, sl=sl):
                    def f(E):
                        return E.transpose(
                            bank[:, j * 128:(j + 1) * 128],
                            xstage[:, sl * 1024 + fc * 128: sl * 1024 + fc * 128 + 128],
                            ctx.ident_f32[:])
                    return f
                w = [exl, bfree, ctx.ident_ready] if j == 0 else []
                mev = P.op("tensor", mkt(), waits=w)

            def mkev(tc=tc, half=half, bank=bank):
                def f(E):
                    dst = xs_fm[:]
                    base = (half * 4) * Tp + 1 + tc * 128
                    d3 = bass.AP(dst.tensor, dst.offset + base, [[dst.ap[0][0], 128], [Tp, 4], [1, 128]])
                    return E.tensor_copy(d3, bank[:, 0:512].rearrange("p (f t) -> p f t", f=4))
                return f
            ev = P.op("vector", mkev(), waits=[mev, epad])
            ctx.pb_ev[id(bank)] = ev
            fm_evs.append(ev)
        sg("xst%d" % sl, fm_evs[-1])
    fm_done = fm_evs[-1]
    a0bar = P.barrier_ev("tensor")
    sg("xs20", a0bar)
    sg("xs21", a0bar)

    pj_evs = {}

    def emit_qkv(c):
        slot = c % 4
        for fot in range(6):
            bank, bfree = ctx.bank()
            for fi in range(8):
                off = 1 if fi < 4 else 0
                w = [ewq, fm_done, bfree, g("qkvg_s%d" % slot)] if fi == 0 else []
                mm(P, ctx, bank,
                   xs_fm[:, fi * Tp + off + c * 128: fi * Tp + off + c * 128 + 128],
                   wqkv[:, fi * 3072 + fot * 512: fi * 3072 + fot * 512 + 512],
                   512, fi == 0, fi == 7, waits=w)
            mev = P.barrier_ev("tensor")

            def mkev(slot=slot, fot=fot, bank=bank):
                def f(E):
                    return E.copy(qkvg[:, slot * 3072 + fot * 512: slot * 3072 + fot * 512 + 512],
                                  bank[:, 0:512])
                return f
            ev = P.op("scalar", mkev(), waits=[mev])
            ctx.pb_ev[id(bank)] = ev
            pj_evs[(c, fot)] = ev

    def emit_chunk(qc):
        tc4 = qc % 4
        rco = (qc % 2) * 32
        qkv_ready = [pj_evs[(qc, f)] for f in range(6)]
        rr_evs = {}
        rot_evs = {}
        for pi in (0, 1):
            base = pi * 1024
            reg = qkvg[:, tc4 * 3072 + base: tc4 * 3072 + base + 1024]
            qw = qkv_ready[2 * pi: 2 * pi + 2]
            sqe = None
            for h4 in range(NHEAD):
                wq4 = qw + [g("rst%d_%d" % (pi, qc % 2)), g("sqd%d" % (h4 % 2))] if h4 == 0 else [g("sqd%d" % (h4 % 2))]
                def mksq(h4=h4, pi=pi, reg=reg, dump=dump, rco=rco):
                    def f(E):
                        return E.activation(
                            dump[:, (h4 % 2) * 64:(h4 % 2) * 64 + 64],
                            reg[:, h4 * 64:(h4 + 1) * 64], AF.Square,
                            accum_out=rstd[:, rco + pi * 16 + h4: rco + pi * 16 + h4 + 1])
                    return f
                sqe = P.op("scalar", mksq(), waits=wq4)
                sg("sqd%d" % (h4 % 2), sqe)
            sqr = P.op("scalar", lambda E, o=rstd, pi=pi, rco=rco: E.activation(
                o[:, rco + pi * 16:rco + (pi + 1) * 16], o[:, rco + pi * 16:rco + (pi + 1) * 16], AF.Sqrt), waits=[sqe])
            cle = P.op("vector", lambda E, o=rstd, pi=pi, rco=rco: E.tensor_scalar(
                o[:, rco + pi * 16:rco + (pi + 1) * 16], o[:, rco + pi * 16:rco + (pi + 1) * 16], 1e-12, None, ALU.max),
                waits=[sqr])
            rre = P.op("vector", lambda E, o=rstd, pi=pi, rco=rco: E.reciprocal(
                o[:, rco + pi * 16:rco + (pi + 1) * 16], o[:, rco + pi * 16:rco + (pi + 1) * 16]), waits=[cle])
            rr_evs[pi] = rre
            ct, st_ = (0, 1) if pi == 0 else (2, 3)
            tb_c = tabs[:, (ct * NT + qc) * 64: (ct * NT + qc) * 64 + 64]
            tb_s0 = tabs[:, (st_ * NT + qc) * 64: (st_ * NT + qc) * 64 + 32]
            tb_s1 = tabs[:, (st_ * NT + qc) * 64 + 32: (st_ * NT + qc) * 64 + 64]
            apc = bass.AP(tb_c.tensor, tb_c.offset, [[tb_c.ap[0][0], 128], [0, 16], [1, 64]])
            aps0 = bass.AP(tb_s0.tensor, tb_s0.offset, [[tb_s0.ap[0][0], 128], [0, 16], [1, 32]])
            aps1 = bass.AP(tb_s1.tensor, tb_s1.offset, [[tb_s1.ap[0][0], 128], [0, 16], [1, 64 - 32]])
            par = (qc % 2) * 2 + pi
            t1s = t1[:, par * 1024:(par + 1) * 1024]
            t2s = t2[:, pi * 1024:(pi + 1) * 1024]
            e_t1 = P.op("vector", lambda E, r=reg, apc=apc, o=t1s: E.tensor_tensor(
                o.rearrange("p (h d) -> p h d", d=64),
                r.rearrange("p (h d) -> p h d", d=64), apc, ALU.mult),
                waits=qw + [tabs_done, g("t1_%d" % par)])
            e_t2a = P.op("gpsimd", lambda E, r=reg, aps0=aps0, o=t2s: E.tensor_tensor(
                o.rearrange("p (h d) -> p h d", d=64)[:, :, 0:32],
                r.rearrange("p (h d) -> p h d", d=64)[:, :, 32:64], aps0, ALU.mult),
                waits=qw + [tabs_done, g("t2_%d" % pi)])
            e_t2b = P.op("gpsimd", lambda E, r=reg, aps1=aps1, o=t2s: E.tensor_tensor(
                o.rearrange("p (h d) -> p h d", d=64)[:, :, 32:64],
                r.rearrange("p (h d) -> p h d", d=64)[:, :, 0:32], aps1, ALU.mult),
                waits=[])
            e_rot = P.op("gpsimd", lambda E, r=reg, a=t1s, b=t2s: E.tensor_add(r, a, b),
                         waits=[e_t1, e_t2a, e_t2b, sqe])
            sg("t1_%d" % par, e_rot)
            sg("t2_%d" % pi, e_rot)
            if pi == 1:
                last = e_rot
                for h in range(NHEAD):
                    def mkr(h=h, r=reg, rco=rco):
                        def f(E):
                            return E.activation(
                                r[:, h * 64:(h + 1) * 64], r[:, h * 64:(h + 1) * 64],
                                AF.Identity, scale=rstd[:, rco + 16 + h:rco + 16 + h + 1])
                        return f
                    last = P.op("scalar", mkr(), waits=[e_rot, rre] if h == 0 else [])
                rot_evs[pi] = last
            else:
                rot_evs[pi] = e_rot
        qslot = qc % 2
        kslot = qc % 4
        tevs = {}
        for pi, dstbuf, slot, gname in ((0, qf, qslot, "qfr"), (1, kf, kslot, "kfr")):
            bank, bfree = ctx.bankt()
            tev = None
            for fc in range(8):
                def mkt(pi=pi, fc=fc, bank=bank, tc4=tc4):
                    def f(E):
                        srcc = qkvg[:, tc4 * 3072 + pi * 1024 + fc * 128: tc4 * 3072 + pi * 1024 + fc * 128 + 128]
                        return E.transpose(bank[:, fc * 128:(fc + 1) * 128], srcc, ctx.ident_bf16[:])
                    return f
                w = [rot_evs[pi], bfree, ctx.identb_ready] if fc == 0 else []
                tev = P.op("tensor", mkt(), waits=w)
            ev = P.op("vector", lambda E, b=bank, dstbuf=dstbuf, slot=slot: E.tensor_copy(
                dstbuf[:, slot * 1024:(slot + 1) * 1024], b[:, 0:1024]),
                waits=[tev, g(gname + str(slot))])
            ctx.pb_ev[id(bank)] = ev
            tevs[pi] = ev
        vslot = qc % 4
        vev = P.op("gpsimd", lambda E, tc4=tc4, vslot=vslot: E.tensor_copy(
            vr[:, vslot * 1024:(vslot + 1) * 1024],
            qkvg[:, tc4 * 3072 + 2048: tc4 * 3072 + 3072]),
            waits=qkv_ready[4:6] + [g("vr%d" % vslot)])
        sg("qkvg_s%d" % tc4, P.barrier_ev("gpsimd"))
        kcs = [k for k in (qc - 2, qc - 1, qc) if k >= 0]
        nk = len(kcs)
        o_evs = []
        st1 = {}
        st2 = {}

        def stage1(h):
            fc, r0 = h // 2, (h % 2) * 64
            sbank, sfree = ctx.bank()
            smev = None
            for j, kc in enumerate(kcs):
                def mks(j=j, kc=kc, fc=fc, r0=r0, sbank=sbank, qslot=qslot):
                    def f(E):
                        return E.matmul(
                            sbank[:, j * 128:(j + 1) * 128],
                            qf[r0:r0 + 64, qslot * 1024 + fc * 128: qslot * 1024 + fc * 128 + 128],
                            kf[r0:r0 + 64, (kc % 4) * 1024 + fc * 128: (kc % 4) * 1024 + fc * 128 + 128],
                            start=True, stop=True)
                    return f
                w = []
                if j == 0:
                    w = [tevs[0], sfree]
                    if h == 0:
                        w.append(tevs[1])
                smev = P.op("tensor", mks(), waits=w)
            st1[h] = (sbank, smev)

        def stage2a(h):
            hs = h % 3
            po = hs * 384
            so = 96 + (h % 4) * 8
            sbank, smev = st1.pop(h)
            sme = P.op("vector", lambda E, b=sbank, h=h, nk=nk, po=po, rco=rco: E.scalar_tensor_tensor(
                p_sb[:, po:po + nk * 128], b[:, 0:nk * 128], rstd[:, rco + h:rco + h + 1],
                lmask[:, (3 - nk) * 128:384], ALU.mult, ALU.add),
                waits=[smev, rr_evs[0], emsk, g("p_sb%d" % hs)])
            ctx.pb_ev[id(sbank)] = sme
            mne = P.op("vector", lambda E, nk=nk, po=po, so=so: E.reduce_max(
                ctx.stat[:, so:so + 1], p_sb[:, po:po + nk * 128], AX.X, negate=True),
                waits=[sme, g("mn%d" % (h % 4))])
            po2 = (h % 4) * 384
            pe = P.op("scalar", lambda E, nk=nk, po=po, po2=po2, so=so: E.activation(
                p2_sb[:, po2:po2 + nk * 128], p_sb[:, po:po + nk * 128], AF.Exp,
                bias=ctx.stat[:, so:so + 1], accum_out=ctx.stat[:, so + 1:so + 2]),
                waits=[mne, g("p2x%d" % (h % 4)), g("soA%d" % (h % 4))])
            sg("mn%d" % (h % 4), pe)
            sg("p_sb%d" % hs, pe)
            st2[h] = pe

        def stage2b(h):
            fc, r0 = h // 2, (h % 2) * 64
            hs = h % 3
            po = hs * 384
            so = 96 + (h % 4) * 8
            po2 = (h % 4) * 384
            pe = st2.pop(h)
            rcpe = P.op("vector", lambda E, so=so: E.reciprocal(
                ctx.stat[:, so + 2:so + 3], ctx.stat[:, so + 1:so + 2]), waits=[pe])
            sg("soA%d" % (h % 4), rcpe)
            dge = P.op("vector", lambda E, hs2=h % 4, so=so: E.tensor_scalar(
                diag[:, hs2 * 128:(hs2 + 1) * 128], ctx.ident_bf16[:],
                ctx.stat[:, so + 2:so + 3], None, ALU.mult),
                waits=[rcpe, ctx.identb_ready, g("diag%d" % (h % 4))])
            tbank, tfree = ctx.bank()
            ptev = None
            for j in range(nk):
                def mkt2(j=j, tbank=tbank, po2=po2, hs2=h % 4):
                    def f(E):
                        return E.matmul(tbank[:, j * 128:(j + 1) * 128],
                                        p2_sb[:, po2 + j * 128: po2 + (j + 1) * 128],
                                        diag[:, hs2 * 128:(hs2 + 1) * 128],
                                        start=True, stop=True)
                    return f
                w = [pe, dge, tfree] if j == 0 else []
                ptev = P.op("tensor", mkt2(), waits=w)
            sg("p2x%d" % (h % 4), ptev)
            sg("diag%d" % (h % 4), ptev)
            pox = (h % 2) * 384
            pte = P.op("scalar", lambda E, b=tbank, nk=nk, pox=pox: E.copy(
                pt_sb[:, pox:pox + nk * 128], b[:, 0:nk * 128]),
                waits=[ptev, g("pt%d" % (h % 2))])
            ctx.pb_ev[id(tbank)] = pte
            obank, ofree = ctx.bank()
            omev = None
            for j, kc in enumerate(kcs):
                def mko(j=j, kc=kc, h=h, obank=obank, nk=nk, pox=pox):
                    def f(E):
                        return E.matmul(obank[0:64, 0:128],
                                        vr[:, (kc % 4) * 1024 + h * 64: (kc % 4) * 1024 + h * 64 + 64],
                                        pt_sb[:, pox + j * 128: pox + j * 128 + 128],
                                        start=(j == 0), stop=(j == nk - 1))
                    return f
                w = [pte] if j == 0 else []
                if j == 0:
                    w += [ofree, vev]
                omev = P.op("tensor", mko(), waits=w)
            sg("pt%d" % (h % 2), omev)
            oev = P.op("scalar", lambda E, b=obank, fc=fc, r0=r0, qp=qc % 2: E.copy(
                aofm[r0:r0 + 64, qp * 1024 + fc * 128: qp * 1024 + fc * 128 + 128],
                b[0:64, 0:128]), waits=[omev, g("aofm%d" % h)])
            ctx.pb_ev[id(obank)] = oev
            o_evs.append(oev)
            sg("aofm%d" % h, oev)

        for h in range(NHEAD + 2):
            if h < NHEAD:
                stage1(h)
            if 1 <= h <= NHEAD:
                stage2a(h - 1)
            if h >= 2:
                stage2b(h - 2)
        sg("qfr%d" % qslot, P.barrier_ev("tensor"))
        sg("rst0_%d" % (qc % 2), P.barrier_ev("vector"))
        sg("rst1_%d" % (qc % 2), P.barrier_ev("vector"))
        tb_ = P.barrier_ev("tensor")
        gb_ = P.barrier_ev("gpsimd")
        ctx.gates["qkvg_s%d" % tc4] = [tb_, gb_]
        sl2 = qc % 2
        exl2 = P.dma("scalar", xstage[:, sl2 * 1024:(sl2 + 1) * 1024],
                     ctx.d_xcur[seq, qc * 128:(qc + 1) * 128, :], waits=[g("xs2%d" % sl2)])
        add_evs = []
        for no in range(2):
            bank, bfree = ctx.bank()
            for fi in range(8):
                w = [o_evs[-1], ewo, bfree] if fi == 0 else []
                mm(P, ctx, bank, aofm[:, (qc % 2) * 1024 + fi * 128: (qc % 2) * 1024 + fi * 128 + 128],
                   wo[:, fi * 1024 + no * 512: fi * 1024 + no * 512 + 512],
                   512, fi == 0, fi == 7, waits=w)
            mev = P.barrier_ev("tensor")

            def mka(no=no, bank=bank, sl2=sl2):
                def f(E):
                    slx = xstage[:, sl2 * 1024 + no * 512: sl2 * 1024 + no * 512 + 512]
                    return E.tensor_tensor(slx, slx, bank[:, 0:512], ALU.add)
                return f
            ae = P.op("vector", mka(), waits=[mev, exl2])
            ctx.pb_ev[id(bank)] = ae
            add_evs.append(ae)
        est = P.dma("gpsimd", ctx.d_xnxt[seq, qc * 128:(qc + 1) * 128, :],
                    xstage[:, sl2 * 1024:(sl2 + 1) * 1024], waits=add_evs)
        sg("xs2%d" % sl2, est)
        xev[("a", qc)] = est

    for c in range(NT + 2):
        if c < NT:
            emit_qkv(c)
        if c >= 2:
            emit_chunk(c - 2)
    a_done_t = P.barrier_ev("tensor")
    ctx.free(xs_fm, wqkv, wo, qkvg, qf, kf, vr, aofm, tabs, reps, t1, t2,
             lmask, p_sb, p2_sb, diag, pt_sb, xstage, rstd, dump)
    ctx.new_phase(P)

    xn_fm = ctx.alloc("xn_fm", [128, NFC * T], BF16)
    prod = ctx.alloc("prod", [128, 11 * T], BF16)
    w2h = ctx.alloc("w2h", [128, 11 * 1024], BF16)
    w1c = ctx.alloc("w1c", [128, 2 * 8 * 128], BF16)
    b1r = ctx.alloc("b1r", [1, 5460], BF16)
    b2r = ctx.alloc("b2r", [1, 1024], BF16)
    grep = ctx.alloc("grep", [128, 1024], F32)
    brep = ctx.alloc("brep", [128, 1024], F32)
    xst = ctx.alloc("xstB", [128, 4 * 1024], F32)
    xs_t = ctx.alloc("xs_t", [128, 4 * 1024], F32)
    ggb = ctx.alloc("gg", [128, 2 * 512], F32)
    x0st = ctx.alloc("x0st", [128, 1024], F32) if final_ln else None
    dump = ctx.dump = ctx.alloc("dumpB", [128, 512], F32)

    eb1 = P.dma("gpsimd", b1r[0:1, :], ctx.d_b1[li:li + 1, :], waits=[g("b1r")])
    eb2 = P.dma("gpsimd", b2r[0:1, :], ctx.d_b2[li:li + 1, :], waits=[g("b2r")])
    egr = P.dma("sync", grep[:], ctx.d_lng[li], waits=[g("grep")])
    ebr_ = P.dma("sync", brep[:], ctx.d_lnb[li], waits=[g("brep")])

    xnevs = []
    prev_stage = None
    z1_prev = None
    for tc in range(NT):
        sl = tc % 4
        exl = P.dma("scalar", xst[:, sl * 1024:(sl + 1) * 1024],
                    ctx.d_xnxt[seq, tc * 128:(tc + 1) * 128, :],
                    waits=[xev[("a", tc)], g("xb%d" % sl), z1_prev])
        c1 = P.op("vector", lambda E, sl=sl: E.tensor_copy(
            xs_t[:, sl * 1024: sl * 1024 + 512], xst[:, sl * 1024: sl * 1024 + 512]),
            waits=[exl, g("xt%d" % sl)])
        if tc == 0:
            z1 = P.op("gpsimd", lambda E, sl=sl: E.memset(xs_t[0:1, sl * 1024 + 512: sl * 1024 + 1024], 0.0),
                      waits=[g("xt%d" % sl)])
        else:
            psl = (tc - 1) % 4
            z1 = P.dma("sync", xs_t[0:1, sl * 1024 + 512: sl * 1024 + 1024],
                       xst[127:128, psl * 1024 + 512: psl * 1024 + 1024],
                       waits=[prev_stage, g("xt%d" % sl)])
        sh = P.dma("sync", xs_t[1:128, sl * 1024 + 512: sl * 1024 + 1024],
                   xst[0:127, sl * 1024 + 512: sl * 1024 + 1024],
                   waits=[exl, g("xt%d" % sl)])
        prev_stage = exl
        z1_prev = z1
        sg("xb%d" % sl, exl)
        sle = xs_t[:, sl * 1024:(sl + 1) * 1024]
        dep = Ev(sh.sem, sh.count)
        e0 = P.op("vector", lambda E: E.nop(), waits=[c1, z1, sh])
        lne = emit_ln_buf(P, ctx, sle, tc % 8, e0)
        e10 = P.op("gpsimd", lambda E, o=sle: E.tensor_mul(o, o, grep[:]), waits=[lne, egr])
        e11 = P.op("gpsimd", lambda E, o=sle: E.tensor_add(o, o, brep[:]), waits=[e10, ebr_])
        for half in range(2):
            bank, bfree = ctx.bank()
            mev = None
            for j in range(4):
                fc = half * 4 + j
                def mkt(fc=fc, j=j, bank=bank, sl=sl):
                    def f(E):
                        return E.transpose(bank[:, j * 128:(j + 1) * 128],
                                           xs_t[:, sl * 1024 + fc * 128: sl * 1024 + fc * 128 + 128],
                                           ctx.ident_f32[:])
                    return f
                w = [e11, bfree] if j == 0 else []
                mev = P.op("tensor", mkt(), waits=w)

            def mkev(tc=tc, half=half, bank=bank):
                def f(E):
                    dst = xn_fm[:]
                    base = (half * 4) * T + tc * 128
                    d3 = bass.AP(dst.tensor, dst.offset + base, [[dst.ap[0][0], 128], [T, 4], [1, 128]])
                    return E.tensor_copy(d3, bank[:, 0:512].rearrange("p (f t) -> p f t", f=4))
                return f
            ev = P.op("vector", mkev(), waits=[mev, g("xnfm")])
            ctx.pb_ev[id(bank)] = ev
            xnevs.append(ev)
        sg("xt%d" % sl, P.barrier_ev("tensor"))
    xn_done = xnevs[-1]
    fb0 = P.full_barrier()
    for _i in range(4):
        ctx.gates["xb%d" % _i] = fb0

    final_evs = []
    bevs = {}
    for hi, half in enumerate(W1HALF):
        prod_evs = []
        for idx, fcl in enumerate(half):
            rows = fc_rows(fcl)
            ea = P.dma("gpsimd", w1c[0:rows, 0:8 * 128].rearrange("p (ci n) -> p ci n", ci=8)
                       if rows == 128 else w1c[:, 0:8 * rows].rearrange("p (ci n) -> p ci n", ci=8),
                       ctx.d_w1[li][:, fcl * 128: fcl * 128 + rows].rearrange("(ci p) n -> p ci n", p=128),
                       waits=[g("w1a")])
            eg_ = P.dma("gpsimd", w1c[:, 8 * 128:8 * 128 + 8 * rows].rearrange("p (ci n) -> p ci n", ci=8),
                        ctx.d_w1[li][:, 2730 + fcl * 128: 2730 + fcl * 128 + rows].rearrange("(ci p) n -> p ci n", p=128),
                        waits=[g("w1g")])
            for tg in range(T // 512):
                abank, afree = ctx.bank()
                gbank, gfree = ctx.bank()
                amms = []
                for fi in range(8):
                    w = [ea, xn_done, afree, g("prodh")] if fi == 0 else []
                    def mk1(fi=fi, rows=rows, abank=abank, tg=tg):
                        def f(E):
                            return E.matmul(abank[0:rows, 0:512],
                                            w1c[:, fi * rows: fi * rows + rows] if rows != 128 else w1c[:, fi * 128: fi * 128 + 128],
                                            xn_fm[:, fi * T + tg * 512: fi * T + tg * 512 + 512],
                                            start=(fi == 0), stop=False)
                        return f
                    amms.append(P.op("tensor", mk1(), waits=w))
                def mkba(rows=rows, abank=abank, fcl=fcl):
                    def f(E):
                        return E.matmul(abank[0:rows, 0:512], b1r[0:1, fcl * 128: fcl * 128 + rows],
                                        ctx.ones_rowb[0:1, 0:512], start=False, stop=True)
                    return f
                mma = P.op("tensor", mkba(), waits=[eb1, ctx.onesb_ready])
                for fi in range(8):
                    w = [eg_, xn_done, gfree] if fi == 0 else []
                    def mk1g(fi=fi, rows=rows, gbank=gbank, tg=tg):
                        def f(E):
                            return E.matmul(gbank[0:rows, 0:512],
                                            w1c[:, 8 * 128 + fi * rows: 8 * 128 + fi * rows + rows],
                                            xn_fm[:, fi * T + tg * 512: fi * T + tg * 512 + 512],
                                            start=(fi == 0), stop=False)
                        return f
                    P.op("tensor", mk1g(), waits=w)
                def mkbg(rows=rows, gbank=gbank, fcl=fcl):
                    def f(E):
                        return E.matmul(gbank[0:rows, 0:512], b1r[0:1, 2730 + fcl * 128: 2730 + fcl * 128 + rows],
                                        ctx.ones_rowb[0:1, 0:512], start=False, stop=True)
                    return f
                mmg = P.op("tensor", mkbg(), waits=[eb1])
                gsl = tg % 2
                gge = P.op("scalar", lambda E, b=gbank, rows=rows, gsl=gsl: E.activation(
                    ggb[0:rows, gsl * 512:(gsl + 1) * 512], b[0:rows, 0:512], AF.Gelu),
                    waits=[mmg, g("gg%d" % gsl)])

                def mkp(idx=idx, tg=tg, rows=rows, abank=abank, gsl=gsl):
                    def f(E):
                        return E.tensor_tensor(
                            prod[0:rows, idx * T + tg * 512: idx * T + tg * 512 + 512],
                            abank[0:rows, 0:512], ggb[0:rows, gsl * 512:(gsl + 1) * 512], ALU.mult)
                    return f
                pev = P.op("vector", mkp(), waits=[mma, gge])
                sg("gg%d" % gsl, pev)
                ctx.pb_ev[id(abank)] = pev
                ctx.pb_ev[id(gbank)] = pev
                prod_evs.append(pev)
            sg("w1a", P.barrier_ev("tensor"))
            sg("w1g", P.barrier_ev("tensor"))
        prod_done = prod_evs[-1]
        nfull = 11 if hi == 0 else 10
        row0 = half[0] * 128
        ew2 = P.dma("gpsimd", w2h[:, 0:nfull * 1024].rearrange("p (ci n) -> p ci n", ci=nfull),
                    ctx.d_w2[li][row0: row0 + nfull * 128, :].rearrange("(ci p) n -> p ci n", p=128),
                    waits=[g("w2h")])
        if hi == 1:
            ew2b = P.dma("gpsimd", w2h[0:42, 10 * 1024:11 * 1024],
                         ctx.d_w2[li][2688:2730, :], waits=[g("w2h")])
        else:
            ew2b = ew2
        for tc in range(NT):
            sl = tc % 2
            exl = P.dma("scalar", xst[:, sl * 1024:(sl + 1) * 1024],
                        ctx.d_xnxt[seq, tc * 128:(tc + 1) * 128, :],
                        waits=[g("xb%d" % sl)] + ([bevs[tc]] if hi == 1 else []))
            adds = []
            for no in range(2):
                bank, bfree = ctx.bank()
                for j, fcl in enumerate(half):
                    rows = fc_rows(fcl)
                    w = [ew2, ew2b, prod_done, bfree] if j == 0 else []
                    def mk2(j=j, rows=rows, tc=tc, no=no, bank=bank, hi=hi, lh=len(half)):
                        def f(E):
                            return E.matmul(bank[:, 0:512],
                                            prod[0:rows, j * T + tc * 128: j * T + tc * 128 + 128],
                                            w2h[0:rows, j * 1024 + no * 512: j * 1024 + no * 512 + 512],
                                            start=(j == 0), stop=(j == lh - 1 and hi == 1))
                        return f
                    P.op("tensor", mk2(), waits=w)
                if hi == 0:
                    mev = mm(P, ctx, bank, ctx.ones_rowb[0:1, 0:128],
                             b2r[0:1, no * 512: no * 512 + 512], 512, False, True, waits=[eb2])
                else:
                    mev = P.barrier_ev("tensor")

                def mka(no=no, bank=bank, sl=sl):
                    def f(E):
                        slx = xst[:, sl * 1024 + no * 512: sl * 1024 + no * 512 + 512]
                        return E.tensor_tensor(slx, slx, bank[:, 0:512], ALU.add)
                    return f
                ae = P.op("vector", mka(), waits=[mev, exl])
                ctx.pb_ev[id(bank)] = ae
                adds.append(ae)
            if hi == 0 or not final_ln:
                est = P.dma("gpsimd", ctx.d_xnxt[seq, tc * 128:(tc + 1) * 128, :],
                            xst[:, sl * 1024:(sl + 1) * 1024], waits=adds)
                sg("xb%d" % sl, est)
                bevs[tc] = est
                if hi == 1:
                    final_evs.append(est)
            else:
                ex0 = P.dma("sync", x0st[:], ctx.d_x0[seq, tc * 128:(tc + 1) * 128, :],
                            waits=[g("x0st")])
                a0 = P.op("vector", lambda E, sl=sl: E.tensor_tensor(
                    xst[:, sl * 1024:(sl + 1) * 1024], xst[:, sl * 1024:(sl + 1) * 1024],
                    x0st[:], ALU.add), waits=adds + [ex0])
                sg("x0st", a0)
                lne = emit_ln_buf(P, ctx, xst[:, sl * 1024:(sl + 1) * 1024], tc % 8, a0)
                est = P.dma("gpsimd", ctx.xout[seq, tc * 128:(tc + 1) * 128, :],
                            xst[:, sl * 1024:(sl + 1) * 1024], waits=[lne])
                sg("xb%d" % sl, est)
                final_evs.append(est)
        sg("prodh", P.barrier_ev("tensor"))
        sg("w2h", P.barrier_ev("tensor"))
    frees = [xn_fm, prod, w2h, w1c, b1r, b2r, grep, brep, xst, xs_t, ggb]
    if final_ln:
        frees.append(x0st)
    frees.append(dump)
    ctx.free(*frees)
    ctx.new_phase(P)
    return final_evs



def host_rotab(n=None):
    n = n or T
    DIM_HEAD, WINDOW = 64, 256
    half = np.arange(0, DIM_HEAD, 2, dtype=np.float32)
    inv_freq = (1.0 / 10000.0 ** (half / DIM_HEAD)).astype(np.float32)
    t = np.arange(n, dtype=np.float32)
    freqs = t[:, None] * inv_freq[None, :]
    freqs = np.concatenate([freqs, freqs], -1)
    base_scale = (half + 0.4 * DIM_HEAD) / (1.4 * DIM_HEAD)
    power = (t - n // 2) / (WINDOW // 2)
    scale = (base_scale[None, :].astype(np.float32) ** power[:, None].astype(np.float32)).astype(np.float32)
    scale = np.concatenate([scale, scale], -1)
    cos, sin = np.cos(freqs), np.sin(freqs)
    sign = np.where(np.arange(64) < 32, -1.0, 1.0).astype(np.float32)
    CS8 = (8.0 * cos * scale).astype(np.float32)
    SS8s = (8.0 * sin * scale * sign[None, :]).astype(np.float32)
    CI = (cos / scale).astype(np.float32)
    SIs = (sin / scale * sign[None, :]).astype(np.float32)
    out = np.stack([CS8, SS8s, CI, SIs])
    ntc = n // 128
    return np.ascontiguousarray(out.reshape(4, ntc, 128, 64).transpose(0, 2, 1, 3))


def host_lmask():
    q = np.arange(128)[:, None]
    k = np.arange(128)[None, :]
    m0 = np.where(q <= k, 0.0, -1e38).astype(np.float32)
    m1 = np.zeros((128, 128), np.float32)
    m2 = np.where(k <= q, 0.0, -1e38).astype(np.float32)
    return np.ascontiguousarray(np.concatenate([m0, m1, m2], axis=1))


def host_qs4(q_scale, k_scale):
    out = np.empty((6, 4, 128, 64), np.float32)
    for li in range(6):
        qs = q_scale[li]
        ks = k_scale[li]
        out[li, 0] = np.tile(qs, (128, 1))
        out[li, 1] = np.tile(np.concatenate([qs[32:], qs[:32]]), (128, 1))
        out[li, 2] = np.tile(ks, (128, 1))
        out[li, 3] = np.tile(np.concatenate([ks[32:], ks[:32]]), (128, 1))
    return out



N_CORES = 8


def _build_l0a():
    nc = bass.Bass("TRN2", target_bir_lowering=False)
    st = ExitStack()
    ctx = Ctx(nc)
    ctx.xin = nc.dram_tensor("xin", [2, T, D], F32, kind="ExternalInput")
    ctx.d_l0a_w = nc.dram_tensor("l0aw", [4, 1024, 1024], F32, kind="ExternalInput")
    ctx.d_l0a_b = nc.dram_tensor("l0ab", [1, 4 * 1024], F32, kind="ExternalInput")
    ctx.d_maskT = nc.dram_tensor("maskT", [128, 128], BF16, kind="ExternalInput")
    ctx.d_ident = nc.dram_tensor("ident", [128, 128], F32, kind="ExternalInput")
    ctx.xout = nc.dram_tensor("xout", [2, T, D], F32, kind="ExternalOutput")
    P = Prog(nc, st)
    setup_common(nc, st, ctx, P)
    prev_store = None
    for seq in range(2):
        exl = emit_load_x(P, ctx, seq, waits=[prev_store])
        out_evs = emit_l0a(P, ctx, seq, exl)
        prev_store = emit_store_x(P, ctx, seq, waits=out_evs)
    P.op("sync", lambda E: E.nop(), waits=[prev_store])
    P.replay()
    st.close()
    return nc


def _build_l0f():
    nc = bass.Bass("TRN2", target_bir_lowering=False)
    st = ExitStack()
    ctx = Ctx(nc)
    ctx.d_x0 = nc.dram_tensor("x0", [2, T, D], F32, kind="ExternalInput")
    ctx.d_wqkv = nc.dram_tensor("wqkv", [6, 1024, 3072], F32, kind="ExternalInput")
    ctx.d_wo0 = nc.dram_tensor("wo0", [6, 1024, 1024], F32, kind="ExternalInput")
    ctx.d_w1 = nc.dram_tensor("w1", [6, 1024, 5460], F32, kind="ExternalInput")
    ctx.d_b1 = nc.dram_tensor("b1", [6, 5460], F32, kind="ExternalInput")
    ctx.d_w2 = nc.dram_tensor("w2", [6, 2730, 1024], F32, kind="ExternalInput")
    ctx.d_b2 = nc.dram_tensor("b2", [6, 1024], F32, kind="ExternalInput")
    ctx.d_lng = nc.dram_tensor("lng", [6, 128, 1024], F32, kind="ExternalInput")
    ctx.d_lnb = nc.dram_tensor("lnb", [6, 128, 1024], F32, kind="ExternalInput")
    ctx.d_qs4 = nc.dram_tensor("qs4", [6, 4, 128, 64], F32, kind="ExternalInput")
    ctx.d_rotab = nc.dram_tensor("rotab", [4, 128, T // 128, 64], F32, kind="ExternalInput")
    ctx.d_lmask = nc.dram_tensor("lmask", [128, 384], F32, kind="ExternalInput")
    ctx.d_ident = nc.dram_tensor("ident", [128, 128], F32, kind="ExternalInput")
    ctx.xout = nc.dram_tensor("xout", [2, T, D], F32, kind="ExternalOutput")
    xs1 = nc.dram_tensor("xs1", [2, T, D], F32)
    xs2 = nc.dram_tensor("xs2", [2, T, D], F32)
    P = Prog(nc, st)
    setup_common(nc, st, ctx, P, need_xtm=False)
    prev = {0: None, 1: None}
    for seq in range(2):
        for li in range(6):
            ctx.d_xcur = ctx.d_x0 if li == 0 else (xs1 if li % 2 == 1 else xs2)
            ctx.d_xnxt = xs1 if li % 2 == 0 else xs2
            evs = emit_l0f_layer(P, ctx, seq, li, final_ln=(li == 5), prev_final=prev[seq])
            prev[seq] = evs
    P.op("sync", lambda E: E.nop(), waits=[prev[0][-1], prev[1][-1]])
    P.replay()
    st.close()
    return nc


def _build_l1a():
    nc = bass.Bass("TRN2", target_bir_lowering=False)
    st = ExitStack()
    ctx = Ctx(nc)
    ctx.xin = nc.dram_tensor("xin", [2, T, D], F32, kind="ExternalInput")
    ctx.d_wq = nc.dram_tensor("wq", [1024, 1024], F32, kind="ExternalInput")
    ctx.d_wk = nc.dram_tensor("wk", [1024, 1024], F32, kind="ExternalInput")
    ctx.d_wv = nc.dram_tensor("wv", [1024, 1024], F32, kind="ExternalInput")
    ctx.d_bqkv = nc.dram_tensor("bqkv", [1, 3 * 1024], F32, kind="ExternalInput")
    ctx.d_ident = nc.dram_tensor("ident", [128, 128], F32, kind="ExternalInput")
    ctx.xout = nc.dram_tensor("xout", [2, T, D], F32, kind="ExternalOutput")
    P = Prog(nc, st)
    setup_common(nc, st, ctx, P)
    prev_store = None
    for seq in range(2):
        exl = emit_load_x(P, ctx, seq, waits=[prev_store])
        out_evs = emit_l1a(P, ctx, seq, exl)
        prev_store = emit_store_x(P, ctx, seq, waits=out_evs)
    P.op("sync", lambda E: E.nop(), waits=[prev_store])
    P.replay()
    st.close()
    return nc


def _build_l1f():
    nc = bass.Bass("TRN2", target_bir_lowering=False)
    st = ExitStack()
    ctx = Ctx(nc)
    ctx.xin = nc.dram_tensor("xin", [2, T, D], F32, kind="ExternalInput")
    ctx.d_w1 = nc.dram_tensor("w1", [1024, 4096], F32, kind="ExternalInput")
    ctx.d_b1r = nc.dram_tensor("b1r", [1, 4096], F32, kind="ExternalInput")
    ctx.d_w2 = nc.dram_tensor("w2", [4096, 1024], F32, kind="ExternalInput")
    ctx.d_b2r = nc.dram_tensor("b2r", [1, 1024], F32, kind="ExternalInput")
    ctx.d_ident = nc.dram_tensor("ident", [128, 128], F32, kind="ExternalInput")
    ctx.xout = nc.dram_tensor("xout", [2, T, D], F32, kind="ExternalOutput")
    P = Prog(nc, st)
    setup_common(nc, st, ctx, P)
    prev_store = None
    for seq in range(2):
        exl = emit_load_x(P, ctx, seq, waits=[prev_store])
        out_evs = emit_l1f(P, ctx, seq, exl)
        prev_store = emit_store_x(P, ctx, seq, waits=out_evs)
    P.op("sync", lambda E: E.nop(), waits=[prev_store])
    P.replay()
    st.close()
    return nc


def _run(nc, shared, xs, xkey="xin"):
    in_maps = [dict(shared, **{xkey: x}) for x in xs]
    res = run_bass_kernel_spmd(nc, in_maps, core_ids=list(range(N_CORES)))
    return [r["xout"] for r in res.results]


def kernel_unfused(**inputs):
    i = inputs
    ident = np.eye(128, dtype=np.float32)
    x = np.ascontiguousarray(i["L0_attn_x"])
    xs = [np.ascontiguousarray(x[2 * c:2 * c + 2]) for c in range(N_CORES)]

    wstack = np.ascontiguousarray(np.stack([
        i["L0_attn_Wq"].T, i["L0_attn_Wk"].T, i["L0_attn_Wv"].T, i["L0_attn_Wo"].T]))
    l0ab = np.ascontiguousarray(np.concatenate(
        [i["L0_attn_bq"], i["L0_attn_bk"], i["L0_attn_bv"], i["L0_attn_bo"]])[None, :])
    maskT = np.triu(np.ones((128, 128))).astype(ml_dtypes.bfloat16)
    shared = {"l0aw": wstack, "l0ab": l0ab, "maskT": maskT, "ident": ident}
    xs = _run(_build_l0a(), shared, xs)

    shared = {
        "wqkv": np.ascontiguousarray(i["L0_ffn_Wqkv"]),
        "wo0": np.ascontiguousarray(i["L0_ffn_Wo"]),
        "w1": np.ascontiguousarray(i["L0_ffn_W1"]),
        "b1": np.ascontiguousarray(i["L0_ffn_b1"]),
        "w2": np.ascontiguousarray(i["L0_ffn_W2"]),
        "b2": np.ascontiguousarray(i["L0_ffn_b2"]),
        "lng": np.ascontiguousarray(np.tile(np.asarray(i["L0_ffn_ln_g"])[:, None, :], (1, 128, 1))),
        "lnb": np.ascontiguousarray(np.tile(np.asarray(i["L0_ffn_ln_b"])[:, None, :], (1, 128, 1))),
        "qs4": host_qs4(np.asarray(i["L0_ffn_q_scale"]), np.asarray(i["L0_ffn_k_scale"])),
        "rotab": host_rotab(T),
        "lmask": host_lmask(),
        "ident": ident,
    }
    xs = _run(_build_l0f(), shared, xs, xkey="x0")

    shared = {
        "wq": np.ascontiguousarray(i["L1_attn_Wq"]),
        "wk": np.ascontiguousarray(i["L1_attn_Wk"]),
        "wv": np.ascontiguousarray(i["L1_attn_Wv"]),
        "bqkv": np.ascontiguousarray(np.concatenate(
            [i["L1_attn_bq"], i["L1_attn_bk"], i["L1_attn_bv"]])[None, :]),
        "ident": ident,
    }
    xs = _run(_build_l1a(), shared, xs)

    shared = {
        "w1": np.ascontiguousarray(i["L1_ffn_w1"]),
        "b1r": np.ascontiguousarray(np.asarray(i["L1_ffn_b1"])[None, :]),
        "w2": np.ascontiguousarray(i["L1_ffn_w2"]),
        "b2r": np.ascontiguousarray(np.asarray(i["L1_ffn_b2"])[None, :]),
        "ident": ident,
    }
    xs = _run(_build_l1f(), shared, xs)

    return np.concatenate(xs, axis=0).astype(np.float32)


def _build_fused():
    nc = bass.Bass("TRN2", target_bir_lowering=False)
    st = ExitStack()
    ctx = Ctx(nc)
    ctx.d_x0in = nc.dram_tensor("x0in", [2, T, D], F32, kind="ExternalInput")
    ctx.d_l0a_w = nc.dram_tensor("l0aw", [4, 1024, 1024], F32, kind="ExternalInput")
    ctx.d_l0a_b = nc.dram_tensor("l0ab", [1, 4 * 1024], F32, kind="ExternalInput")
    ctx.d_maskT = nc.dram_tensor("maskT", [128, 128], BF16, kind="ExternalInput")
    ctx.d_wqkv = nc.dram_tensor("wqkv", [6, 1024, 3072], F32, kind="ExternalInput")
    ctx.d_wo0 = nc.dram_tensor("wo0", [6, 1024, 1024], F32, kind="ExternalInput")
    ctx.d_w1 = nc.dram_tensor("w1", [6, 1024, 5460], F32, kind="ExternalInput")
    ctx.d_b1 = nc.dram_tensor("b1", [6, 5460], F32, kind="ExternalInput")
    ctx.d_w2 = nc.dram_tensor("w2", [6, 2730, 1024], F32, kind="ExternalInput")
    ctx.d_b2 = nc.dram_tensor("b2", [6, 1024], F32, kind="ExternalInput")
    ctx.d_lng = nc.dram_tensor("lng", [6, 128, 1024], F32, kind="ExternalInput")
    ctx.d_lnb = nc.dram_tensor("lnb", [6, 128, 1024], F32, kind="ExternalInput")
    ctx.d_qs4 = nc.dram_tensor("qs4", [6, 4, 128, 64], F32, kind="ExternalInput")
    ctx.d_rotab = nc.dram_tensor("rotab", [4, 128, T // 128, 64], F32, kind="ExternalInput")
    ctx.d_lmask = nc.dram_tensor("lmask", [128, 384], F32, kind="ExternalInput")
    ctx.d_wq = nc.dram_tensor("wq", [1024, 1024], F32, kind="ExternalInput")
    ctx.d_wk = nc.dram_tensor("wk", [1024, 1024], F32, kind="ExternalInput")
    ctx.d_wv = nc.dram_tensor("wv", [1024, 1024], F32, kind="ExternalInput")
    ctx.d_bqkv = nc.dram_tensor("bqkv", [1, 3 * 1024], F32, kind="ExternalInput")
    ctx.d_w1b = nc.dram_tensor("w1b", [1024, 4096], F32, kind="ExternalInput")
    ctx.d_b1rb = nc.dram_tensor("b1rb", [1, 4096], F32, kind="ExternalInput")
    ctx.d_w2b = nc.dram_tensor("w2b", [4096, 1024], F32, kind="ExternalInput")
    ctx.d_b2rb = nc.dram_tensor("b2rb", [1, 1024], F32, kind="ExternalInput")
    ctx.d_ident = nc.dram_tensor("ident", [128, 128], F32, kind="ExternalInput")
    real_out = nc.dram_tensor("xout", [2, T, D], F32, kind="ExternalOutput")
    xa = nc.dram_tensor("xa", [2, T, D], F32)
    xs1 = nc.dram_tensor("xs1", [2, T, D], F32)
    xs2 = nc.dram_tensor("xs2", [2, T, D], F32)
    xb = nc.dram_tensor("xb", [2, T, D], F32)

    P = Prog(nc, st)
    setup_common(nc, st, ctx, P, need_xtm=False)

    ctx.x_tm = ctx.alloc("x_tm", [128, NTC * D], F32)
    ctx.xin = ctx.d_x0in
    ctx.xout = xa
    prev_store = None
    for seq in range(2):
        exl = emit_load_x(P, ctx, seq, waits=[prev_store])
        out_evs = emit_l0a(P, ctx, seq, exl)
        prev_store = emit_store_x(P, ctx, seq, waits=out_evs)
    ctx.free(ctx.x_tm)
    ctx.x_tm = None
    ctx.new_phase(P)

    ctx.d_x0 = xa
    ctx.xout = xb
    prev = {0: None, 1: None}
    for seq in range(2):
        for li in range(6):
            ctx.d_xcur = xa if li == 0 else (xs1 if li % 2 == 1 else xs2)
            ctx.d_xnxt = xs1 if li % 2 == 0 else xs2
            evs = emit_l0f_layer(P, ctx, seq, li, final_ln=(li == 5), prev_final=prev[seq])
            prev[seq] = evs

    ctx.x_tm = ctx.alloc("x_tm2", [128, NTC * D], F32)
    ctx.xin = xb
    ctx.xout = real_out
    ctx.d_w1, ctx.d_b1r, ctx.d_w2, ctx.d_b2r = ctx.d_w1b, ctx.d_b1rb, ctx.d_w2b, ctx.d_b2rb
    prev_store = None
    for seq in range(2):
        exl = emit_load_x(P, ctx, seq, waits=[prev_store, prev[seq]], src=xb)
        evs_a = emit_l1a(P, ctx, seq, exl)
        evs_f = emit_l1f(P, ctx, seq, evs_a)
        prev_store = emit_store_x(P, ctx, seq, waits=evs_f, dst=real_out)
    P.op("sync", lambda E: E.nop(), waits=[prev_store])
    P.replay()
    st.close()
    return nc


def kernel(**inputs):
    i = inputs
    ident = np.eye(128, dtype=np.float32)
    x = np.ascontiguousarray(i["L0_attn_x"])
    xs = [np.ascontiguousarray(x[2 * c:2 * c + 2]) for c in range(N_CORES)]
    wstack = np.ascontiguousarray(np.stack([
        i["L0_attn_Wq"].T, i["L0_attn_Wk"].T, i["L0_attn_Wv"].T, i["L0_attn_Wo"].T]))
    shared = {
        "l0aw": wstack,
        "l0ab": np.ascontiguousarray(np.concatenate(
            [i["L0_attn_bq"], i["L0_attn_bk"], i["L0_attn_bv"], i["L0_attn_bo"]])[None, :]),
        "maskT": np.triu(np.ones((128, 128))).astype(ml_dtypes.bfloat16),
        "wqkv": np.ascontiguousarray(i["L0_ffn_Wqkv"]),
        "wo0": np.ascontiguousarray(i["L0_ffn_Wo"]),
        "w1": np.ascontiguousarray(i["L0_ffn_W1"]),
        "b1": np.ascontiguousarray(i["L0_ffn_b1"]),
        "w2": np.ascontiguousarray(i["L0_ffn_W2"]),
        "b2": np.ascontiguousarray(i["L0_ffn_b2"]),
        "lng": np.ascontiguousarray(np.tile(np.asarray(i["L0_ffn_ln_g"])[:, None, :], (1, 128, 1))),
        "lnb": np.ascontiguousarray(np.tile(np.asarray(i["L0_ffn_ln_b"])[:, None, :], (1, 128, 1))),
        "qs4": host_qs4(np.asarray(i["L0_ffn_q_scale"]), np.asarray(i["L0_ffn_k_scale"])),
        "rotab": host_rotab(T),
        "lmask": host_lmask(),
        "wq": np.ascontiguousarray(i["L1_attn_Wq"]),
        "wk": np.ascontiguousarray(i["L1_attn_Wk"]),
        "wv": np.ascontiguousarray(i["L1_attn_Wv"]),
        "bqkv": np.ascontiguousarray(np.concatenate(
            [i["L1_attn_bq"], i["L1_attn_bk"], i["L1_attn_bv"]])[None, :]),
        "w1b": np.ascontiguousarray(i["L1_ffn_w1"]),
        "b1rb": np.ascontiguousarray(np.asarray(i["L1_ffn_b1"])[None, :]),
        "w2b": np.ascontiguousarray(i["L1_ffn_w2"]),
        "b2rb": np.ascontiguousarray(np.asarray(i["L1_ffn_b2"])[None, :]),
        "ident": ident,
    }
    nc = _build_fused()
    in_maps = [dict(shared, x0in=s) for s in xs]
    res = run_bass_kernel_spmd(nc, in_maps, core_ids=list(range(N_CORES)))
    return np.concatenate([r["xout"] for r in res.results], axis=0).astype(np.float32)
